# version 06
# baseline (speedup 1.0000x reference)
import contextlib
import math
import numpy as np
import ml_dtypes
import concourse.bass as bass
import concourse.mybir as mybir
from concourse.bass_utils import run_bass_kernel_spmd

F32 = mybir.dt.float32
BF16 = mybir.dt.bfloat16
I32 = mybir.dt.int32
I8 = mybir.dt.int8
AF = mybir.ActivationFunctionType
ALU = mybir.AluOpType

E = 4096
S = 4096
NB = 4
H = 32
DH = 128
NL = 4
NCH = 32
EPS = 1e-5
SCALE = DH ** -0.5
WIN_T = 50.0
BETA = 10.0
KILL = -1.0e5
NKT = 36
KPAD = NKT * 128


def _slopes():
    base = 2.0 ** (-(2.0 ** (-(math.log2(H) - 3))))
    return [base ** (i + 1) for i in range(H)]


SLOPES = _slopes()


def _qg_of(h):
    for qg in (512, 256, 128):
        if SLOPES[h] * (qg - 1) <= 70.0:
            return qg
    return 64


QG = [_qg_of(h) for h in range(H)]


def _nkt(h, t):
    w = WIN_T / SLOPES[h]
    qend = (2 * t + 2) * 512
    return min(int(math.ceil((1024.0 + w) / 128.0)), qend // 128, NKT)


def _needs_cm(qg, sub, kt):
    for par in (0, 1):
        qmb = 1024 - 512 * par - sub * qg
        lo = kt * 128 - KPAD + qmb
        hi = lo + 127
        if hi >= 1 and lo <= qg - 1:
            return True
    return False


def _att_schedule():
    entries = []
    col = 0
    cm_map = {}
    cm_cols = 0
    for t in range(4):
        for h in range(H):
            qg = QG[h]
            nk = _nkt(h, t)
            for sub in range(512 // qg):
                for kt in range(NKT - nk, NKT):
                    r = NKT - 1 - kt
                    cm = None
                    if _needs_cm(qg, sub, kt):
                        key = (qg, sub, r)
                        if key not in cm_map:
                            cm_map[key] = cm_cols
                            cm_cols += qg
                        cm = cm_map[key]
                    entries.append(dict(t=t, h=h, sub=sub, kt=kt, qg=qg,
                                        col=col, cm=cm))
                    col += 1
    return entries, col, cm_map, cm_cols


ATT, NBIAS, CM_MAP, CM_COLS = _att_schedule()
SLOT_COL = []
for _t in range(4):
    _cs = [e["col"] for e in ATT if e["t"] == _t]
    SLOT_COL.append((min(_cs), max(_cs) + 1))
MAXBIAS = max(b_ - a_ for (a_, b_) in SLOT_COL)

ENGINES = ("pe", "act", "dve", "pool", "sp")


class Bld:
    def __init__(self, nc):
        self.nc = nc
        self.ops = {e: [] for e in ENGINES}
        self.sems = {}
        self._waited = {e: {} for e in ENGINES}

    def sem(self, name):
        if name not in self.sems:
            self.sems[name] = [self.nc.alloc_semaphore(name), 0]
        return self.sems[name]

    def op(self, eng, fn, waits=(), sem=None, dma=False):
        wl = []
        for w in waits:
            if w is None:
                continue
            h, v = w
            if v <= 0:
                continue
            k = id(h)
            if self._waited[eng].get(k, -1) >= v:
                continue
            self._waited[eng][k] = v
            wl.append((h, v))
        tok = None
        inc = None
        if sem is not None:
            rec = self.sem(sem)
            rec[1] += 16 if dma else 1
            tok = (rec[0], rec[1])
            inc = (rec[0], 16 if dma else 1)
        self.ops[eng].append((wl, fn, inc))
        return tok

    def fence(self, engs=ENGINES):
        return [self.op(e, None, sem="f_" + e) for e in engs]

    def phase_barrier(self):
        self.fence()
        waits = [(rec[0], rec[1]) for rec in self.sems.values()]
        for e in ENGINES:
            self.op(e, None, waits=waits)

    def emit(self):
        nc = self.nc
        with nc.Block() as blk:
            def mk(ename):
                def body(eng):
                    for (wl, fn, inc) in self.ops[ename]:
                        for (h, v) in wl:
                            eng.wait_ge(h, v)
                        if fn is None:
                            if inc is not None:
                                eng.nop().then_inc(inc[0], inc[1])
                        else:
                            ins = fn()
                            if inc is not None:
                                ins.then_inc(inc[0], inc[1])
                return body
            blk.tensor(mk("pe"))
            blk.scalar(mk("act"))
            blk.vector(mk("dve"))
            blk.gpsimd(mk("pool"))
            blk.sync(mk("sp"))


class Stream:

    def __init__(self, b, name, nslots):
        self.b = b
        self.name = name
        self.n = nslots
        self.war = [[] for _ in range(nslots)]
        self.i = 0

    def slot(self):
        return self.i % self.n

    def issue(self, eng, fn, waits=()):
        s = self.i % self.n
        self.i += 1
        tok = self.b.op(eng, fn, waits=list(waits) + self.war[s],
                        sem=f"{self.name}{s}", dma=True)
        self.war[s] = []
        return tok

    def consumed(self, s, tok):
        if tok is not None:
            self.war[s].append(tok)


def build_program(debug=False):
    nc = bass.Bass("TRN2", target_bir_lowering=False)
    b = Bld(nc)
    dbg = "ExternalOutput" if debug else "Internal"

    x_in = nc.dram_tensor("x_fm", [E, S], F32, kind="ExternalInput").ap()
    w0 = nc.dram_tensor("w0pp", [NCH, 128, E], F32, kind="ExternalInput").ap()
    wqp = nc.dram_tensor("wqpp", [34, 128, E], F32, kind="ExternalInput").ap()
    wcp = nc.dram_tensor("wcpp", [NCH, 128, E], F32, kind="ExternalInput").ap()
    aTp = nc.dram_tensor("aTp", [128, E], F32, kind="ExternalInput").ap()
    bTp = nc.dram_tensor("bTp", [128, E], F32, kind="ExternalInput").ap()
    b0v = nc.dram_tensor("b0v", [128, NCH], F32, kind="ExternalInput").ap()
    mk1 = nc.dram_tensor("mk1", [NCH, 128, NL, S], I8, kind="ExternalInput").ap()
    mk2 = nc.dram_tensor("mk2", [NCH, 128, NL, 2048], I8, kind="ExternalInput").ap()
    btbl = nc.dram_tensor("btbl", [128, NBIAS], F32, kind="ExternalInput").ap()
    cmt = nc.dram_tensor("cmt", [128, max(CM_COLS, 1)], BF16, kind="ExternalInput").ap()
    ixq = nc.dram_tensor("ixq", [128, 4 * NCH], I32, kind="ExternalInput").ap()
    ixm = nc.dram_tensor("ixm", [128, NCH * 4], I32, kind="ExternalInput").ap()
    ident_in = nc.dram_tensor("ident", [128, 192], BF16, kind="ExternalInput").ap()

    xmid = nc.dram_tensor("xmid", [E, S], F32, kind=dbg).ap()
    xln2d = nc.dram_tensor("xln2d", [E, S], BF16, kind=dbg).ap()
    pk = nc.dram_tensor("pk", [128, KPAD + S], BF16, kind=dbg).ap()
    pvx = nc.dram_tensor("pvx", [128, (NKT + 32) * 129], BF16, kind=dbg).ap()
    qd = nc.dram_tensor("qd", [H, 128, 2048], BF16, kind=dbg).ap()
    x2 = nc.dram_tensor("x2", [E, 2048], F32, kind=dbg).ap()
    out = nc.dram_tensor("out_fm", [E, 2048], F32, kind="ExternalOutput").ap()

    st_x = Stream(b, "sx", 4)
    st_w = Stream(b, "sw", 2)
    st_m = Stream(b, "sm", 4)
    st_xe = Stream(b, "sxe", 4)
    st_wr = Stream(b, "swr", 3)
    st_g = Stream(b, "sg", 4)
    st_c = Stream(b, "sc", 2)

    with contextlib.ExitStack() as top:
        ones_b = top.enter_context(nc.sbuf_tensor("ones_b", [128, 1], BF16))
        ones_r = top.enter_context(nc.sbuf_tensor("ones_r", [1, 128], F32))
        ident = top.enter_context(nc.sbuf_tensor("ident_sb", [128, 192], BF16))
        b0h = top.enter_context(nc.sbuf_tensor("b0h", [128, NCH], F32))

        b.op("dve", lambda: nc.vector.memset(ones_b[:], 1.0), sem="dve")
        b.op("dve", lambda: nc.vector.memset(ones_r[:], 1.0), sem="dve")
        tid = st_c.issue("sp", lambda: nc.sync.dma_start(out=ident[:], in_=ident_in))
        tb0 = st_c.issue("sp", lambda: nc.sync.dma_start(out=b0h[:], in_=b0v))
        tb0h = b.op("dve", lambda: nc.vector.tensor_scalar_mul(b0h[:], b0h[:], 0.5),
                    waits=[tb0], sem="dve")

        def ln_pass(nm, src_ap, src_col0, ntg, xln_sb, dram_dst=None):
            assert ntg == 4
            Tw = ntg * 512
            with nc.psum_tensor(nm + "_s1", [128, 512], F32) as ps_s1, \
                 nc.psum_tensor(nm + "_s2", [128, 512], F32) as ps_s2, \
                 nc.psum_tensor(nm + "_b1", [128, 4 * 512], F32) as ps_b1, \
                 nc.psum_tensor(nm + "_b2", [128, 512], F32) as ps_b2, \
                 nc.sbuf_tensor(nm + "_x", [128, 3 * 2048], F32) as lnx, \
                 nc.sbuf_tensor(nm + "_bf", [128, 3 * 2048], BF16) as lnb, \
                 nc.sbuf_tensor(nm + "_sq", [128, 3 * 2048], BF16) as lnq, \
                 nc.sbuf_tensor(nm + "_t", [128, 4 * 512], F32) as lnt, \
                 nc.sbuf_tensor(nm + "_r", [128, 4 * 512], F32) as lnr, \
                 nc.sbuf_tensor(nm + "_o", [128, 4 * 512], BF16) as lno, \
                 nc.sbuf_tensor(nm + "_rw", [1, 5 * 512], F32) as rows:
                st_ln = Stream(b, nm + "sx", 3)
                m_last = [None, None]
                for c in range(NCH):
                    s = st_ln.slot()
                    xt = st_ln.issue("sp", lambda c=c, s=s: nc.sync.dma_start(
                        out=lnx[:, s * 2048:(s + 1) * 2048],
                        in_=src_ap[c * 128:(c + 1) * 128, src_col0:src_col0 + 2048]))
                    ct = b.op("dve", lambda s=s: nc.vector.tensor_copy(
                        lnb[:, s * 2048:(s + 1) * 2048],
                        lnx[:, s * 2048:(s + 1) * 2048]), waits=[xt], sem="dve")
                    qt = b.op("act", lambda s=s: nc.scalar.activation(
                        lnq[:, s * 2048:(s + 1) * 2048],
                        lnx[:, s * 2048:(s + 1) * 2048], AF.Square),
                        waits=[xt], sem="act")
                    for tg in range(4):
                        m1 = b.op("pe", lambda s=s, c=c, tg=tg: nc.tensor.matmul(
                            ps_s1[32 * tg:32 * tg + 1, :], ones_b[:],
                            lnb[:, s * 2048 + tg * 512:s * 2048 + (tg + 1) * 512],
                            start=(c == 0), stop=(c == NCH - 1),
                            tile_position=(0, 32 * tg)),
                            waits=[ct], sem="pe")
                        m2 = b.op("pe", lambda s=s, c=c, tg=tg: nc.tensor.matmul(
                            ps_s2[32 * tg:32 * tg + 1, :], ones_b[:],
                            lnq[:, s * 2048 + tg * 512:s * 2048 + (tg + 1) * 512],
                            start=(c == 0), stop=(c == NCH - 1),
                            tile_position=(0, 32 * tg)),
                            waits=[qt], sem="pe")
                    st_ln.consumed(s, m2)
                    m_last = [m1, m2]
                bc1s = []
                for tg in range(4):
                    rm = rows[0:1, 0:512]
                    rr = rows[0:1, 512:1024]
                    ra = rows[0:1, 1024:1536]
                    rb = rows[0:1, 1536:2048]
                    rc = rows[0:1, 2048:2560]
                    b.op("dve", lambda rm=rm, tg=tg: nc.vector.tensor_scalar_mul(
                        rm, ps_s1[32 * tg:32 * tg + 1, :], 1.0 / E),
                        waits=m_last, sem="dve")
                    b.op("dve", lambda rb=rb, rm=rm: nc.vector.tensor_mul(rb, rm, rm),
                         sem="dve")
                    b.op("dve", lambda ra=ra, tg=tg: nc.vector.tensor_scalar_mul(
                        ra, ps_s2[32 * tg:32 * tg + 1, :], 1.0 / E), sem="dve")
                    b.op("dve", lambda ra=ra, rb=rb: nc.vector.tensor_sub(ra, ra, rb),
                         sem="dve")
                    tv = b.op("dve", lambda ra=ra: nc.vector.tensor_scalar_add(
                        ra, ra, EPS), sem="dve")
                    ts = b.op("act", lambda rb=rb, ra=ra: nc.scalar.activation(
                        rb, ra, AF.Sqrt), waits=[tv], sem="act")
                    b.op("dve", lambda rc=rc, rb=rb: nc.vector.reciprocal(rc, rb),
                         waits=[ts], sem="dve")
                    b.op("dve", lambda rb=rb, rc=rc: nc.vector.tensor_mul(rb, rc, rc),
                         sem="dve")
                    b.op("dve", lambda rb=rb, ra=ra: nc.vector.tensor_mul(rb, rb, ra),
                         sem="dve")
                    b.op("dve", lambda rb=rb: nc.vector.tensor_scalar(
                        rb, rb, -0.5, 1.5, ALU.mult, ALU.add), sem="dve")
                    trs = b.op("dve", lambda rr=rr, rc=rc, rb=rb: nc.vector.tensor_mul(
                        rr, rc, rb), sem="dve")
                    bc1 = b.op("pe", lambda rm=rm, tg=tg: nc.tensor.matmul(
                        ps_b1[:, tg * 512:(tg + 1) * 512], ones_r[:], rm,
                        start=True, stop=True), waits=[trs], sem="pe")
                    bc1s.append(bc1)
                    bc2 = b.op("pe", lambda rr=rr: nc.tensor.matmul(
                        ps_b2[:], ones_r[:], rr, start=True, stop=True), sem="pe")
                    trv = b.op("dve", lambda tg=tg: nc.vector.tensor_copy(
                        lnr[:, tg * 512:(tg + 1) * 512], ps_b2[:]),
                        waits=[bc2], sem="dve")
                lnt_war = [[] for _ in range(4)]
                lno_war = [[] for _ in range(4)]
                for c in range(NCH):
                    s = st_ln.slot()
                    xt = st_ln.issue("sp", lambda c=c, s=s: nc.sync.dma_start(
                        out=lnx[:, s * 2048:(s + 1) * 2048],
                        in_=src_ap[c * 128:(c + 1) * 128, src_col0:src_col0 + 2048]))
                    last = None
                    for tg in range(4):
                        tsb = b.op("dve", lambda s=s, tg=tg: nc.vector.tensor_sub(
                            lnt[:, tg * 512:(tg + 1) * 512],
                            lnx[:, s * 2048 + tg * 512:s * 2048 + (tg + 1) * 512],
                            ps_b1[:, tg * 512:(tg + 1) * 512]),
                            waits=[xt, trv, bc1s[tg]] + lnt_war[tg], sem="dve")
                        lnt_war[tg] = []
                        last = tsb
                        if dram_dst is None:
                            dcol = c * Tw + tg * 512
                            tp = b.op("pool", lambda tg=tg, dcol=dcol: nc.gpsimd.tensor_mul(
                                xln_sb[:, dcol:dcol + 512],
                                lnt[:, tg * 512:(tg + 1) * 512],
                                lnr[:, tg * 512:(tg + 1) * 512]),
                                waits=[tsb], sem="pool")
                            lnt_war[tg].append(tp)
                        else:
                            tp = b.op("pool", lambda tg=tg: nc.gpsimd.tensor_mul(
                                lno[:, tg * 512:(tg + 1) * 512],
                                lnt[:, tg * 512:(tg + 1) * 512],
                                lnr[:, tg * 512:(tg + 1) * 512]),
                                waits=[tsb] + lno_war[tg], sem="pool")
                            lnt_war[tg].append(tp)
                            lno_war[tg] = []
                            wt = st_wr.issue("act", lambda tg=tg, c=c:
                                             nc.scalar.dma_start(
                                out=dram_dst[c * 128:(c + 1) * 128,
                                             src_col0 + tg * 512:src_col0 + tg * 512 + 512],
                                in_=lno[:, tg * 512:(tg + 1) * 512]), waits=[tp])
                            lno_war[tg].append(wt)
                    st_ln.consumed(s, last)
                b.phase_barrier()

        def ffn_pass(nm, xln_sb, ntg, mask_ap, mask_col0,
                     xsrc_ap, xsrc_col0, xdst_ap, xdst_col0):
            Tw = ntg * 512
            with nc.psum_tensor(nm + "_ba", [128, 4 * 512], F32) as ps_ba, \
                 nc.psum_tensor(nm + "_y", [128, 2 * 512], F32) as ps_y, \
                 nc.psum_tensor(nm + "_h", [128, 512], F32) as ps_h, \
                 nc.sbuf_tensor(nm + "_aT", [128, E], BF16) as aTs, \
                 nc.sbuf_tensor(nm + "_bT", [128, E], BF16) as bTs, \
                 nc.sbuf_tensor(nm + "_hs", [128, Tw], BF16) as h_sb, \
                 nc.sbuf_tensor(nm + "_w", [128, 2 * E], BF16) as wbuf, \
                 nc.sbuf_tensor(nm + "_m", [128, 4 * NL * 512], I8) as mbuf, \
                 nc.sbuf_tensor(nm + "_xe", [128, 4 * 512], F32) as xeb, \
                 nc.sbuf_tensor(nm + "_t0", [128, 2 * 512], F32) as t0b, \
                 nc.sbuf_tensor(nm + "_ac", [128, 2 * 512], F32) as acb, \
                 nc.sbuf_tensor(nm + "_xn", [128, 2 * 512], F32) as xnb:

                ta = st_c.issue("pool", lambda: nc.gpsimd.dma_start(
                    out=aTs[:], in_=aTp))
                tb = st_c.issue("pool", lambda: nc.gpsimd.dma_start(
                    out=bTs[:], in_=bTp))
                htoks = [None] * ntg
                hprev = None
                for tg in range(ntg):
                    mm = None
                    for dc in range(NCH):
                        mm = b.op("pe", lambda dc=dc, tg=tg: nc.tensor.matmul(
                            ps_h[:], aTs[:, dc * 128:(dc + 1) * 128],
                            xln_sb[:, dc * Tw + tg * 512:dc * Tw + tg * 512 + 512],
                            start=(dc == 0), stop=(dc == NCH - 1)),
                            waits=([ta, hprev] if dc == 0 else ()), sem="pe")
                    htoks[tg] = b.op("dve", lambda tg=tg: nc.vector.tensor_copy(
                        h_sb[:, tg * 512:(tg + 1) * 512], ps_h[:]),
                        waits=[mm], sem="dve")
                    hprev = htoks[tg]

                ba_war = [[], [], [], []]
                y_war = [[], []]
                t0_war = [[], []]
                ac_war = [[], []]
                xn_war = [[], []]
                for oc in range(NCH):
                    ws = st_w.slot()
                    wt = st_w.issue("pool", lambda oc=oc, ws=ws: nc.gpsimd.dma_start(
                        out=wbuf[:, ws * E:(ws + 1) * E], in_=w0[oc]))
                    lastmm = None
                    xss, mss, xets, mts = [], [], [], []
                    for tg in range(ntg):
                        xs = st_xe.slot()
                        xets.append(st_xe.issue("sp", lambda oc=oc, tg=tg, xs=xs:
                                                nc.sync.dma_start(
                            out=xeb[:, xs * 512:(xs + 1) * 512],
                            in_=xsrc_ap[oc * 128:(oc + 1) * 128,
                                        xsrc_col0 + tg * 512:xsrc_col0 + tg * 512 + 512])))
                        xss.append(xs)
                        ms = st_m.slot()
                        mts.append(st_m.issue("act", lambda oc=oc, tg=tg, ms=ms:
                                              nc.scalar.dma_start(
                            out=mbuf[:, ms * NL * 512:(ms + 1) * NL * 512]
                                .rearrange("p (l t) -> p l t", l=NL),
                            in_=mask_ap[oc, :, :,
                                        mask_col0 + tg * 512:mask_col0 + tg * 512 + 512])))
                        mss.append(ms)
                    mmtg = [None] * ntg
                    for dc in range(NCH):
                        for tg in range(ntg):
                            mmtg[tg] = b.op("pe", lambda dc=dc, tg=tg, ws=ws:
                                            nc.tensor.matmul(
                                ps_ba[:, tg * 512:(tg + 1) * 512],
                                wbuf[:, ws * E + dc * 128:ws * E + (dc + 1) * 128],
                                xln_sb[:, dc * Tw + tg * 512:dc * Tw + tg * 512 + 512],
                                start=(dc == 0), stop=(dc == NCH - 1)),
                                waits=([wt] + ba_war[tg]) if dc == 0 else (), sem="pe")
                            if dc == 0:
                                ba_war[tg] = []
                    for tg in range(ntg):
                        u = oc * ntg + tg
                        ab = u % 2
                        bb = tg
                        mm = mmtg[tg]
                        xs = xss[tg]
                        ms = mss[tg]
                        xet = xets[tg]
                        mt = mts[tg]
                        lastmm = mm

                        def ymm(l, yb):
                            return b.op("pe", lambda l=l, oc=oc, tg=tg, yb=yb:
                                        nc.tensor.matmul(
                                ps_y[:, yb * 512:(yb + 1) * 512],
                                bTs[32 * l:32 * l + 16, oc * 128:(oc + 1) * 128],
                                h_sb[32 * l:32 * l + 16, tg * 512:(tg + 1) * 512],
                                start=True, stop=True, tile_position=(32 * l, 0)),
                                waits=[tb, htoks[tg]] + y_war[yb], sem="pe")
                        y0 = ymm(0, 0)
                        y_war[0] = []
                        y1 = ymm(1, 1)
                        y_war[1] = []
                        d0 = b.op("dve", lambda ab=ab: nc.vector.tensor_copy(
                            acb[:, ab * 512:(ab + 1) * 512], ps_y[:, 0:512]),
                            waits=[y0] + ac_war[ab], sem="dve")
                        ac_war[ab] = []
                        y_war[0].append(d0)
                        y2 = ymm(2, 0)
                        y_war[0] = []
                        d1 = b.op("dve", lambda ab=ab, ms=ms: nc.vector.copy_predicated(
                            acb[:, ab * 512:(ab + 1) * 512],
                            mbuf[:, ms * NL * 512 + 512:ms * NL * 512 + 1024],
                            ps_y[:, 512:1024]), waits=[y1, mt], sem="dve")
                        y_war[1].append(d1)
                        y3 = ymm(3, 1)
                        y_war[1] = []
                        d2 = b.op("dve", lambda ab=ab, ms=ms: nc.vector.copy_predicated(
                            acb[:, ab * 512:(ab + 1) * 512],
                            mbuf[:, ms * NL * 512 + 1024:ms * NL * 512 + 1536],
                            ps_y[:, 0:512]), waits=[y2], sem="dve")
                        y_war[0].append(d2)
                        d3 = b.op("dve", lambda ab=ab, ms=ms: nc.vector.copy_predicated(
                            acb[:, ab * 512:(ab + 1) * 512],
                            mbuf[:, ms * NL * 512 + 1536:ms * NL * 512 + 2048],
                            ps_y[:, 512:1024]), waits=[y3], sem="dve")
                        y_war[1].append(d3)
                        st_m.consumed(ms, d3)
                        tact = b.op("act", lambda oc=oc, bb=bb, ab=ab:
                                    nc.scalar.activation(
                            t0b[:, ab * 512:(ab + 1) * 512],
                            ps_ba[:, bb * 512:(bb + 1) * 512], AF.Identity,
                            bias=b0h[:, oc:oc + 1], scale=0.5),
                            waits=[mm, tb0h] + t0_war[ab], sem="act")
                        t0_war[ab] = []
                        ba_war[bb].append(tact)
                        tst = b.op("dve", lambda ab=ab: nc.vector.scalar_tensor_tensor(
                            acb[:, ab * 512:(ab + 1) * 512],
                            acb[:, ab * 512:(ab + 1) * 512], 4.0,
                            t0b[:, ab * 512:(ab + 1) * 512], ALU.mult, ALU.add),
                            waits=[tact], sem="dve")
                        t0_war[ab].append(tst)
                        tpn = b.op("pool", lambda ab=ab, xs=xs: nc.gpsimd.tensor_add(
                            xnb[:, ab * 512:(ab + 1) * 512],
                            acb[:, ab * 512:(ab + 1) * 512],
                            xeb[:, xs * 512:(xs + 1) * 512]),
                            waits=[tst, xet] + xn_war[ab], sem="pool")
                        xn_war[ab] = []
                        ac_war[ab].append(tpn)
                        st_xe.consumed(xs, tpn)
                        wtok = st_wr.issue("act", lambda oc=oc, tg=tg, ab=ab:
                                           nc.scalar.dma_start(
                            out=xdst_ap[oc * 128:(oc + 1) * 128,
                                        xdst_col0 + tg * 512:xdst_col0 + tg * 512 + 512],
                            in_=xnb[:, ab * 512:(ab + 1) * 512]), waits=[tpn])
                        xn_war[ab].append(wtok)
                    st_w.consumed(ws, lastmm)
                b.phase_barrier()

        with contextlib.ExitStack() as sa:
            xln = sa.enter_context(nc.sbuf_tensor("xln", [128, NCH * 2048], BF16))
            for hf in range(2):
                ln_pass(f"l1{hf}", x_in, hf * 2048, 4, xln)
                ffn_pass(f"f1{hf}", xln, 4, mk1, hf * 2048,
                         x_in, hf * 2048, xmid, hf * 2048)
            ln_pass("l2a", xmid, 0, 4, None, dram_dst=xln2d)
            ln_pass("l2b", xmid, 2048, 4, None, dram_dst=xln2d)

        with contextlib.ExitStack() as sq:
            kvs = sq.enter_context(nc.sbuf_tensor("kvs", [128, 2 * 512], BF16))
            v_sb = sq.enter_context(nc.sbuf_tensor("v_sb", [128, S], BF16))
            xst = sq.enter_context(nc.sbuf_tensor("xst", [128, 3 * 2048], BF16))
            wqb = sq.enter_context(nc.sbuf_tensor("wqb", [128, 2 * E], BF16))
            zbuf = sq.enter_context(nc.sbuf_tensor("zbuf", [128, 516], BF16))
            tz = b.op("dve", lambda: nc.vector.memset(zbuf[:], 0.0), sem="dve")
            for i in range(9):
                st_wr.issue("act", lambda i=i: nc.scalar.dma_start(
                    out=pk[:, i * 512:(i + 1) * 512], in_=zbuf[:, 0:512]), waits=[tz])
            for i in range(9):
                st_wr.issue("act", lambda i=i: nc.scalar.dma_start(
                    out=pvx[:, i * 516:(i + 1) * 516], in_=zbuf[:, 0:516]), waits=[tz])
            with nc.psum_tensor("ps_kv", [128, 8 * 512], F32) as ps_kv:
                kv_war = [[] for _ in range(8)]
                kvslot_war = [[], []]
                vdone = []
                st_kv = Stream(b, "skv", 3)
                wts = []
                for kvoc in range(2):
                    ws = st_w.slot()
                    wts.append(st_w.issue("pool", lambda kvoc=kvoc, ws=ws:
                                          nc.gpsimd.dma_start(
                        out=wqb[:, ws * E:(ws + 1) * E], in_=wqp[32 + kvoc])))
                mmk = [None] * 8
                for hf in range(2):
                    for dc in range(NCH):
                        s = st_kv.slot()
                        xt = st_kv.issue("sp", lambda dc=dc, hf=hf, s=s:
                                         nc.sync.dma_start(
                            out=xst[:, s * 2048:(s + 1) * 2048],
                            in_=xln2d[dc * 128:(dc + 1) * 128,
                                      hf * 2048:(hf + 1) * 2048]))
                        last = None
                        for kvoc in range(2):
                            for tg in range(4):
                                bk = kvoc * 4 + tg
                                last = b.op("pe", lambda dc=dc, kvoc=kvoc, tg=tg,
                                            bk=bk, s=s: nc.tensor.matmul(
                                    ps_kv[:, bk * 512:(bk + 1) * 512],
                                    wqb[:, (kvoc % 2) * E + dc * 128:
                                        (kvoc % 2) * E + (dc + 1) * 128],
                                    xst[:, s * 2048 + tg * 512:s * 2048 + (tg + 1) * 512],
                                    start=(dc == 0), stop=(dc == NCH - 1)),
                                    waits=([wts[kvoc], xt] + kv_war[bk])
                                          if dc == 0 else [xt], sem="pe")
                                if dc == 0:
                                    kv_war[bk] = []
                                mmk[bk] = last
                        st_kv.consumed(s, last)
                    for kvoc in range(2):
                        for tg in range(4):
                            bk = kvoc * 4 + tg
                            gtg = hf * 4 + tg
                            kb = bk % 2
                            ev = b.op("dve", lambda bk=bk, kb=kb: nc.vector.tensor_copy(
                                kvs[:, kb * 512:(kb + 1) * 512],
                                ps_kv[:, bk * 512:(bk + 1) * 512]),
                                waits=[mmk[bk]] + kv_war[bk] + kvslot_war[kb],
                                sem="dve")
                            kvslot_war[kb] = []
                            kv_war[bk].append(ev)
                            if kvoc == 0:
                                wk = st_wr.issue("act", lambda gtg=gtg, kb=kb:
                                                 nc.scalar.dma_start(
                                    out=pk[:, KPAD + gtg * 512:KPAD + (gtg + 1) * 512],
                                    in_=kvs[:, kb * 512:(kb + 1) * 512]), waits=[ev])
                                kv_war[bk].append(wk)
                                kvslot_war[kb].append(wk)
                            else:
                                vt = b.op("dve", lambda gtg=gtg, kb=kb:
                                          nc.vector.tensor_copy(
                                    v_sb[:, gtg * 512:(gtg + 1) * 512],
                                    kvs[:, kb * 512:(kb + 1) * 512]),
                                    waits=[ev], sem="dve")
                                vdone.append(vt)
                                kv_war[bk].append(vt)
                                kvslot_war[kb].append(vt)
                st_w.consumed(st_w.i % st_w.n, mmk[0])
                st_w.consumed((st_w.i + 1) % st_w.n, mmk[7])
            with nc.psum_tensor("ps_vt", [128, 128], BF16) as ps_vt, \
                 nc.sbuf_tensor("vstg", [128, 2 * 129], BF16) as vstg:
                to1 = b.op("dve", lambda: nc.vector.memset(vstg[:], 1.0),
                           waits=vdone, sem="dve")
                vw = [[], []]
                tprev = None
                for kt in range(32):
                    vs = kt % 2
                    tt = b.op("pe", lambda kt=kt: nc.tensor.transpose(
                        ps_vt[:], v_sb[:, kt * 128:(kt + 1) * 128], ident[:, 0:128]),
                        waits=[to1, tid] + ([tprev] if tprev else []), sem="pe")
                    ev = b.op("dve", lambda vs=vs: nc.vector.tensor_copy(
                        vstg[:, vs * 129:vs * 129 + 128], ps_vt[:]),
                        waits=[tt] + vw[vs], sem="dve")
                    vw[vs] = []
                    tprev = ev
                    wt = st_wr.issue("act", lambda kt=kt, vs=vs: nc.scalar.dma_start(
                        out=pvx[:, (NKT + kt) * 129:(NKT + kt + 1) * 129],
                        in_=vstg[:, vs * 129:(vs + 1) * 129]), waits=[ev])
                    vw[vs].append(wt)
            b.phase_barrier()
            with nc.sbuf_tensor("xlq", [128, 4 * NCH * 512], BF16) as xlq, \
                 nc.sbuf_tensor("ixqs", [128, 4 * NCH], I32) as ixqs, \
                 nc.sbuf_tensor("qstg", [128, 2 * 512], BF16) as qstg, \
                 nc.psum_tensor("ps_q", [128, 4 * 512], F32) as ps_q:
                tix = st_c.issue("sp", lambda: nc.sync.dma_start(out=ixqs[:], in_=ixq))
                gtoks = []
                for g in range(4):
                    for dc in range(NCH):
                        gt = st_g.issue("pool", lambda g=g, dc=dc:
                                        nc.gpsimd.indirect_dma_start(
                            xlq[:, (g * NCH + dc) * 512:(g * NCH + dc + 1) * 512],
                            None, xln2d,
                            bass.IndirectOffsetOnAxis(
                                ixqs[:, g * NCH + dc:g * NCH + dc + 1], 1)),
                            waits=[tix])
                        gtoks.append(gt)
                q_war = [[], [], [], []]
                qs_war = [[], []]
                for oc in range(NCH):
                    ws = st_w.slot()
                    wt = st_w.issue("pool", lambda oc=oc, ws=ws: nc.gpsimd.dma_start(
                        out=wqb[:, ws * E:(ws + 1) * E], in_=wqp[oc]))
                    mmg = [None] * 4
                    for dc in range(NCH):
                        for g in range(4):
                            mmg[g] = b.op("pe", lambda dc=dc, g=g, ws=ws:
                                          nc.tensor.matmul(
                                ps_q[:, g * 512:(g + 1) * 512],
                                wqb[:, ws * E + dc * 128:ws * E + (dc + 1) * 128],
                                xlq[:, (g * NCH + dc) * 512:(g * NCH + dc + 1) * 512],
                                start=(dc == 0), stop=(dc == NCH - 1)),
                                waits=([wt] + q_war[g] + [gtoks[g * NCH + dc]])
                                      if dc == 0 else [gtoks[g * NCH + dc]],
                                sem="pe")
                            if dc == 0:
                                q_war[g] = []
                    for g in range(4):
                        qb = g % 2
                        ev = b.op("dve", lambda qb=qb, g=g: nc.vector.tensor_copy(
                            qstg[:, qb * 512:(qb + 1) * 512],
                            ps_q[:, g * 512:(g + 1) * 512]),
                            waits=[mmg[g]] + qs_war[qb], sem="dve")
                        qs_war[qb] = []
                        q_war[g].append(ev)
                        wqt = st_wr.issue("act", lambda oc=oc, g=g, qb=qb:
                                          nc.scalar.dma_start(
                            out=qd[oc, :, g * 512:(g + 1) * 512],
                            in_=qstg[:, qb * 512:(qb + 1) * 512]), waits=[ev])
                        qs_war[qb].append(wqt)
                    st_w.consumed(ws, mmg[3])
            b.phase_barrier()

        with contextlib.ExitStack() as sbA:
            attn = sbA.enter_context(nc.sbuf_tensor("attn", [128, H * 2048], BF16))
            st_kw = Stream(b, "skw", 1)
            st_vw = Stream(b, "svw", 1)
            st_qw = Stream(b, "sqw", 2)
            st_bi = Stream(b, "sbi", 1)
            with nc.sbuf_tensor("kwin", [128, NKT * 128], BF16) as kwin, \
                 nc.sbuf_tensor("vwin", [128, NKT * 129], BF16) as vwin, \
                 nc.sbuf_tensor("qwin", [128, 2 * 8 * 512], BF16) as qwin, \
                 nc.sbuf_tensor("pbuf", [128, 16 * 512], BF16) as pbuf, \
                 nc.sbuf_tensor("bias", [128, MAXBIAS], F32) as bias, \
                 nc.sbuf_tensor("cms", [128, max(CM_COLS, 1)], BF16) as cms, \
                 nc.sbuf_tensor("atok", [128, 2 * 128], BF16) as atok, \
                 nc.sbuf_tensor("arec", [128, 2], F32) as arec, \
                 nc.psum_tensor("ps_s", [128, 4 * 512], F32) as ps_s, \
                 nc.psum_tensor("ps_pv", [128, 4 * 512], F32) as ps_pv:
                tcm = st_c.issue("sp", lambda: nc.sync.dma_start(out=cms[:], in_=cmt))
                pv_war = [[] for _ in range(4)]
                atreg_war = [[] for _ in range(4)]
                s_war = [[], [], [], []]
                at_war = [[], []]
                rec_war = [[], []]
                pb_war = [[] for _ in range(16)]
                uctr = 0
                for t in range(4):
                    QEND = (2 * t + 2) * 512
                    QT = QEND // 128
                    tkw = st_kw.issue("sp", lambda QEND=QEND: nc.sync.dma_start(
                        out=kwin[:], in_=pk[:, QEND:QEND + KPAD]))
                    tvw = st_vw.issue("sp", lambda QT=QT: nc.sync.dma_start(
                        out=vwin[:], in_=pvx[:, QT * 129:(QT + NKT) * 129]))
                    lo, hi = SLOT_COL[t]
                    tbi = st_bi.issue("sp", lambda lo=lo, hi=hi: nc.sync.dma_start(
                        out=bias[:, 0:hi - lo], in_=btbl[:, lo:hi]))
                    ents = [e for e in ATT if e["t"] == t]
                    tqw = None
                    for h in range(H):
                        if h % 8 == 0:
                            hh = h // 8
                            qs_ = hh % 2
                            tqw = st_qw.issue("sp", lambda t=t, hh=hh, qs_=qs_:
                                              nc.sync.dma_start(
                                out=qwin[:, qs_ * 8 * 512:(qs_ + 1) * 8 * 512]
                                    .rearrange("p (h t) -> p h t", h=8),
                                in_=qd[hh * 8:(hh + 1) * 8, :,
                                       t * 512:(t + 1) * 512]
                                    .rearrange("h d t -> d h t")))
                        qg = QG[h]
                        qoff = ((h // 8) % 2) * 4096 + (h % 8) * 512
                        hents = [e for e in ents if e["h"] == h]
                        units = []
                        for sub in sorted(set(e["sub"] for e in hents)):
                            sents = sorted([e for e in hents if e["sub"] == sub],
                                           key=lambda d: d["kt"])
                            for i, e in enumerate(sents):
                                units.append((e, i == 0, i == len(sents) - 1))
                        pv_list = []
                        touched = [False] * 4

                        def emit_pv(e, first, last, pb, ptok, touched=touched,
                                    pv_list=pv_list, qg=qg):
                            sub, kt = e["sub"], e["kt"]
                            njt = max(1, qg // 128)
                            for j in range(njt):
                                if qg >= 128:
                                    qi = sub * njt + j
                                    prow = 0
                                    rows_ = 128
                                else:
                                    qi = sub // 2
                                    prow = 64 * (sub % 2)
                                    rows_ = 64
                                w_ = [ptok, tvw]
                                if first and not touched[qi]:
                                    w_ = w_ + pv_war[qi] + atreg_war[qi]
                                    touched[qi] = True
                                    pv_war[qi] = []
                                    atreg_war[qi] = []
                                tp = (0, prow) if prow else None
                                pm = b.op("pe", lambda pb=pb, j=j, kt=kt, qi=qi,
                                          prow=prow, rows_=rows_, first=first,
                                          last=last, tp=tp, qg=qg:
                                          nc.tensor.matmul(
                                    ps_pv[prow:prow + rows_,
                                          qi * 512:qi * 512 + 129],
                                    pbuf[:, pb * 512 + j * 128:
                                         pb * 512 + j * 128 + min(128, qg)],
                                    vwin[:, kt * 129:(kt + 1) * 129],
                                    start=first, stop=last, tile_position=tp),
                                    waits=w_, sem="pe")
                                pb_war[pb].append(pm)
                                if last:
                                    pv_list.append((qi, prow, rows_, sub, j, pm))

                        BDEPTH = 12
                        pend = []
                        for b0_ in range(0, len(units), BDEPTH):
                            batch = units[b0_:b0_ + BDEPTH]
                            newpend = []
                            for i, (e, first, last) in enumerate(batch):
                                kt = e["kt"]
                                ss = uctr % 4
                                pb = uctr % 16
                                uctr += 1
                                sm = b.op("pe", lambda kt=kt, qg=qg, ss=ss,
                                          qo=qoff + e["sub"] * qg:
                                          nc.tensor.matmul(
                                    ps_s[:, ss * 512:ss * 512 + qg],
                                    kwin[:, kt * 128:(kt + 1) * 128],
                                    qwin[:, qo:qo + qg],
                                    start=True, stop=True),
                                    waits=[tkw, tqw] + s_war[ss], sem="pe")
                                s_war[ss] = []
                                ex = b.op("act", lambda ss=ss, pb=pb, qg=qg,
                                          cix=e["col"] - lo: nc.scalar.activation(
                                    pbuf[:, pb * 512:pb * 512 + qg],
                                    ps_s[:, ss * 512:ss * 512 + qg], AF.Exp,
                                    bias=bias[:, cix:cix + 1], scale=SCALE),
                                    waits=[sm, tbi] + pb_war[pb], sem="act")
                                pb_war[pb] = []
                                s_war[ss].append(ex)
                                ptok = ex
                                if e["cm"] is not None:
                                    ptok = b.op("dve", lambda pb=pb, qg=qg,
                                                cmo=e["cm"]: nc.vector.tensor_mul(
                                        pbuf[:, pb * 512:pb * 512 + qg],
                                        pbuf[:, pb * 512:pb * 512 + qg],
                                        cms[:, cmo:cmo + qg]),
                                        waits=[ptok, tcm], sem="dve")
                                newpend.append((e, first, last, pb, ptok))
                                if i < len(pend):
                                    emit_pv(*pend[i])
                            for j in range(len(batch), len(pend)):
                                emit_pv(*pend[j])
                            pend = newpend
                        for p_ in pend:
                            emit_pv(*p_)
                        for (qi, prow, rows_, sub, j, pm) in pv_list:
                            lcol = t * 512 + sub * qg + j * 128
                            ab_ = (sub + j) % 2
                            rt = b.op("dve", lambda qi=qi, prow=prow, rows_=rows_,
                                      ab_=ab_: nc.vector.reciprocal(
                                arec[prow:prow + rows_, ab_:ab_ + 1],
                                ps_pv[prow:prow + rows_, qi * 512 + 128:qi * 512 + 129]),
                                waits=[pm] + rec_war[ab_], sem="dve")
                            dt = b.op("dve", lambda qi=qi, prow=prow, rows_=rows_,
                                      ab_=ab_: nc.vector.tensor_scalar(
                                atok[prow:prow + rows_, ab_ * 128:ab_ * 128 + 128],
                                ps_pv[prow:prow + rows_, qi * 512:qi * 512 + 128],
                                arec[prow:prow + rows_, ab_:ab_ + 1], None, ALU.mult),
                                waits=[rt] + at_war[ab_], sem="dve")
                            at_war[ab_] = []
                            rec_war[ab_] = [dt]
                            pv_war[qi].append(dt)
                            if rows_ == 128:
                                idap = ident[0:128, 0:128]
                            elif prow == 0:
                                idap = ident[0:64, 0:64]
                            else:
                                idap = ident[64:128, 128:192]
                            at_ap = ps_pv[:, qi * 512 + 256:qi * 512 + 320].bitcast(BF16)
                            tt = b.op("pe", lambda prow=prow, rows_=rows_, ab_=ab_,
                                      idap=idap, at_ap=at_ap: nc.tensor.transpose(
                                at_ap[0:128, 0:rows_],
                                atok[prow:prow + rows_, ab_ * 128:ab_ * 128 + 128],
                                idap),
                                waits=[dt] + atreg_war[qi], sem="pe")
                            atreg_war[qi] = []
                            ev = b.op("dve", lambda h=h, lcol=lcol, rows_=rows_,
                                      at_ap=at_ap: nc.vector.tensor_copy(
                                attn[:, h * 2048 + lcol:h * 2048 + lcol + rows_],
                                at_ap[0:128, 0:rows_]),
                                waits=[tt], sem="dve")
                            atreg_war[qi].append(ev)
                            pv_war[qi].append(ev)
                            at_war[ab_].append(ev)
                        if h % 8 == 7:
                            st_qw.consumed((h // 8) % 2, b.fence(("pe",))[0])
                    fpe = b.fence(("pe",))[0]
                    st_kw.consumed(0, fpe)
                    st_vw.consumed(0, fpe)
                    st_bi.consumed(0, b.fence(("act",))[0])
                    st_qw.consumed(0, fpe)
                    st_qw.consumed(1, fpe)
                b.phase_barrier()

            with nc.sbuf_tensor("wcb", [128, 2 * E], BF16) as wcb, \
                 nc.sbuf_tensor("xmg", [128, 4 * 512], F32) as xmg, \
                 nc.sbuf_tensor("ixms", [128, NCH * 4], I32) as ixms, \
                 nc.sbuf_tensor("x2s", [128, 2 * 512], F32) as x2s, \
                 nc.psum_tensor("ps_wc", [128, 4 * 512], F32) as ps_wc:
                tix = st_c.issue("sp", lambda: nc.sync.dma_start(out=ixms[:], in_=ixm))
                wc_war = [[], [], [], []]
                x2_war = [[], []]
                for oc in range(NCH):
                    ws = st_w.slot()
                    wt = st_w.issue("pool", lambda oc=oc, ws=ws: nc.gpsimd.dma_start(
                        out=wcb[:, ws * E:(ws + 1) * E], in_=wcp[oc]))
                    mms = [None] * 4
                    for dc in range(NCH):
                        for tg in range(4):
                            mms[tg] = b.op("pe", lambda dc=dc, tg=tg, ws=ws:
                                           nc.tensor.matmul(
                                ps_wc[:, tg * 512:(tg + 1) * 512],
                                wcb[:, ws * E + dc * 128:ws * E + (dc + 1) * 128],
                                attn[:, dc * 2048 + tg * 512:dc * 2048 + (tg + 1) * 512],
                                start=(dc == 0), stop=(dc == NCH - 1)),
                                waits=([wt] + wc_war[tg]) if dc == 0 else (), sem="pe")
                            if dc == 0:
                                wc_war[tg] = []
                    for tg in range(4):
                        gs = st_g.slot()
                        gt = st_g.issue("pool", lambda oc=oc, tg=tg, gs=gs:
                                        nc.gpsimd.indirect_dma_start(
                            xmg[:, gs * 512:(gs + 1) * 512], None, xmid,
                            bass.IndirectOffsetOnAxis(
                                ixms[:, oc * 4 + tg:oc * 4 + tg + 1], 1)),
                            waits=[tix])
                        xb_ = tg % 2
                        ad = b.op("dve", lambda tg=tg, gs=gs, xb_=xb_:
                                  nc.vector.tensor_add(
                            x2s[:, xb_ * 512:(xb_ + 1) * 512],
                            ps_wc[:, tg * 512:(tg + 1) * 512],
                            xmg[:, gs * 512:(gs + 1) * 512]),
                            waits=[mms[tg], gt] + x2_war[xb_], sem="dve")
                        x2_war[xb_] = []
                        wc_war[tg].append(ad)
                        st_g.consumed(gs, ad)
                        wt2 = st_wr.issue("act", lambda oc=oc, tg=tg, xb_=xb_:
                                          nc.scalar.dma_start(
                            out=x2[oc * 128:(oc + 1) * 128, tg * 512:(tg + 1) * 512],
                            in_=x2s[:, xb_ * 512:(xb_ + 1) * 512]), waits=[ad])
                        x2_war[xb_].append(wt2)
                    st_w.consumed(ws, mms[3])
                b.phase_barrier()

        with contextlib.ExitStack() as sc:
            xln3 = sc.enter_context(nc.sbuf_tensor("xln3", [128, NCH * 2048], BF16))
            ln_pass("l3", x2, 0, 4, xln3)
            ffn_pass("f2", xln3, 4, mk2, 0, x2, 0, out, 0)

        final_waits = [(rec[0], rec[1]) for rec in b.sems.values()]
        b.op("sp", None, waits=final_waits)

    b.emit()
    return nc


def _host_tables(par):
    bias = np.zeros((128, NBIAS), np.float32)
    ki = np.arange(128)
    for e in ATT:
        t, h, sub, kt, qg, col = e["t"], e["h"], e["sub"], e["kt"], e["qg"], e["col"]
        qend = (2 * t + 2) * 512
        own = (2 * t + par) * 512
        qbase = own + sub * qg
        kpos = qend - KPAD + kt * 128 + ki
        v = SLOPES[h] * (kpos - qbase) - BETA
        killed = (kpos < 0) | (kpos > qbase + qg - 1)
        bias[:, col] = np.where(killed, KILL, v).astype(np.float32)
        if e["cm"] is None:
            assert np.all(killed | (kpos <= qbase)), ("missing cmask", t, h, sub, kt)
    cmask = np.zeros((128, max(CM_COLS, 1)), np.float32)
    for (qg, sub, r), coff in CM_MAP.items():
        kt = NKT - 1 - r
        qend_m_qbase = 1024 - 512 * par - sub * qg
        kpos_rel = kt * 128 + ki - KPAD + qend_m_qbase
        qj = np.arange(qg)
        cmask[:, coff:coff + qg] = (qj[None, :] >= kpos_rel[:, None])
    return bias, cmask.astype(ml_dtypes.bfloat16)


def _threefry2x32(kx, ky, c0, c1):
    rot = [(13, 15, 26, 6), (17, 29, 16, 24)]
    ks0 = np.uint32(kx); ks1 = np.uint32(ky)
    ks2 = np.uint32(np.uint32(0x1BD11BDA) ^ ks0 ^ ks1)
    x0 = (c0 + ks0).astype(np.uint32); x1 = (c1 + ks1).astype(np.uint32)
    ks = [ks0, ks1, ks2]
    for i in range(5):
        for d in rot[i % 2]:
            x0 = (x0 + x1).astype(np.uint32)
            x1 = ((x1 << np.uint32(d)) | (x1 >> np.uint32(32 - d))).astype(np.uint32)
            x1 = (x0 ^ x1).astype(np.uint32)
        x0 = (x0 + ks[(i + 1) % 3]).astype(np.uint32)
        x1 = (x1 + ks[(i + 2) % 3] + np.uint32(i + 1)).astype(np.uint32)
    return x0, x1


def _dropout_mask_indices():
    n = NB * S * E
    out = np.empty(n, np.int8)
    chunk = 1 << 24
    for c0 in range(0, n, chunk):
        c = np.arange(c0, min(n, c0 + chunk), dtype=np.uint32)
        x0, x1 = _threefry2x32(0, 42, np.zeros_like(c), c)
        bits = x0 ^ x1
        fl = ((bits >> np.uint32(9)) | np.uint32(0x3F800000)).view(np.float32) - np.float32(1.0)
        out[c0:c0 + len(c)] = np.floor(fl * 4.0).astype(np.int8)
    return out.reshape(NB, S, E)


def _ffn_mask_tables(idx_ot):
    O, T = idx_ot.shape
    m = np.empty((NCH, 128, NL, T), dtype=np.int8)
    for l in range(NL):
        m[:, :, l, :] = (idx_ot == l).reshape(NCH, 128, T)
    return m


_PROG_CACHE = {}


def kernel(L0_attn_hidden_states, L0_attn_w_qkv, L0_attn_w_c,
           L0_ffn_W0, L0_ffn_b0, L0_ffn_A, L0_ffn_B):
    x = np.ascontiguousarray(np.asarray(L0_attn_hidden_states, dtype=np.float32))
    wqkv = np.ascontiguousarray(np.asarray(L0_attn_w_qkv, dtype=np.float32))
    wc = np.ascontiguousarray(np.asarray(L0_attn_w_c, dtype=np.float32))
    W0 = np.ascontiguousarray(np.asarray(L0_ffn_W0, dtype=np.float32))
    b0 = np.ascontiguousarray(np.asarray(L0_ffn_b0, dtype=np.float32))
    A = np.ascontiguousarray(np.asarray(L0_ffn_A, dtype=np.float32))
    Bm = np.ascontiguousarray(np.asarray(L0_ffn_B, dtype=np.float32))

    w0pp = np.ascontiguousarray(
        W0.reshape(NCH, 128, NCH, 128).transpose(0, 3, 2, 1)).reshape(NCH, 128, E)
    wqpp = np.ascontiguousarray(
        wqkv.reshape(NCH, 128, 34, 128).transpose(2, 1, 0, 3)).reshape(34, 128, E)
    wcpp = np.ascontiguousarray(
        wc.reshape(NCH, 128, NCH, 128).transpose(2, 1, 0, 3)).reshape(NCH, 128, E)
    aTp = np.zeros((128, E), np.float32)
    for dc in range(NCH):
        for l in range(NL):
            aTp[:, dc * 128 + 32 * l:dc * 128 + 32 * l + 16] = \
                A[l, :, dc * 128:(dc + 1) * 128].T
    bTp = np.zeros((128, E), np.float32)
    for l in range(NL):
        bTp[32 * l:32 * l + 16, :] = Bm[l].T
    b0v = np.ascontiguousarray(b0.reshape(NCH, 128).T)

    idx_all = _dropout_mask_indices()
    ident = np.zeros((128, 192), dtype=ml_dtypes.bfloat16)
    ident[:, 0:128] = np.eye(128, dtype=ml_dtypes.bfloat16)
    for i in range(64):
        ident[64 + i, 128 + i] = 1

    in_maps = []
    masks_cache = {}
    for c in range(8):
        bi = c // 2
        par = c % 2
        stripes = [2 * t + par for t in range(4)]
        xc = np.ascontiguousarray(x[bi].T)
        if bi not in masks_cache:
            masks_cache[bi] = _ffn_mask_tables(np.ascontiguousarray(idx_all[bi].T))
        mk1c = masks_cache[bi]
        own_cols = np.concatenate(
            [np.arange(s * 512, (s + 1) * 512) for s in stripes])
        mk2c = np.ascontiguousarray(mk1c[:, :, :, own_cols])
        biasT, cmaskT = _host_tables(par)
        p = np.arange(128)
        ixq_ = np.empty((128, 4 * NCH), np.int32)
        ixm_ = np.empty((128, NCH * 4), np.int32)
        for g in range(4):
            for dc in range(NCH):
                ixq_[:, g * NCH + dc] = (dc * 128 + p) * S + stripes[g] * 512
        for oc in range(NCH):
            for tg in range(4):
                ixm_[:, oc * 4 + tg] = (oc * 128 + p) * S + stripes[tg] * 512
        in_maps.append(dict(
            x_fm=xc, w0pp=w0pp, wqpp=wqpp, wcpp=wcpp, aTp=aTp, bTp=bTp,
            b0v=b0v, mk1=mk1c, mk2=mk2c, btbl=biasT, cmt=cmaskT,
            ixq=ixq_, ixm=ixm_, ident=ident))

    if "prog" not in _PROG_CACHE:
        _PROG_CACHE["prog"] = build_program()
    nc = _PROG_CACHE["prog"]
    res = run_bass_kernel_spmd(nc, in_maps, core_ids=list(range(8)))

    outp = np.empty((NB, S, E), np.float32)
    for c in range(8):
        bi = c // 2
        par = c % 2
        o = res.results[c]["out_fm"]
        for g in range(4):
            s = 2 * g + par
            outp[bi, s * 512:(s + 1) * 512, :] = o[:, g * 512:(g + 1) * 512].T
    return outp


# version 07
# speedup vs baseline: 1.1540x; 1.1540x over previous
import contextlib
import math
import numpy as np
import ml_dtypes
import concourse.bass as bass
import concourse.mybir as mybir
from concourse.bass_utils import run_bass_kernel_spmd

F32 = mybir.dt.float32
BF16 = mybir.dt.bfloat16
I32 = mybir.dt.int32
I8 = mybir.dt.int8
AF = mybir.ActivationFunctionType
ALU = mybir.AluOpType

E = 4096
S = 4096
NB = 4
H = 32
DH = 128
NL = 4
NCH = 32
EPS = 1e-5
SCALE = DH ** -0.5
WIN_T = 50.0
BETA = 10.0
KILL = -1.0e5
NKT = 36
KPAD = NKT * 128


def _slopes():
    base = 2.0 ** (-(2.0 ** (-(math.log2(H) - 3))))
    return [base ** (i + 1) for i in range(H)]


SLOPES = _slopes()


def _qg_of(h):
    for qg in (512, 256, 128):
        if SLOPES[h] * (qg - 1) <= 70.0:
            return qg
    return 64


QG = [_qg_of(h) for h in range(H)]


def _nkt(h, t):
    w = WIN_T / SLOPES[h]
    qend = (2 * t + 2) * 512
    return min(int(math.ceil((1024.0 + w) / 128.0)), qend // 128, NKT)


def _needs_cm(qg, sub, kt):
    for par in (0, 1):
        qmb = 1024 - 512 * par - sub * qg
        lo = kt * 128 - KPAD + qmb
        hi = lo + 127
        if hi >= 1 and lo <= qg - 1:
            return True
    return False


def _att_schedule():
    entries = []
    col = 0
    cm_map = {}
    cm_cols = 0
    for t in range(4):
        for h in range(H):
            qg = QG[h]
            nk = _nkt(h, t)
            for sub in range(512 // qg):
                for kt in range(NKT - nk, NKT):
                    r = NKT - 1 - kt
                    cm = None
                    if _needs_cm(qg, sub, kt):
                        key = (qg, sub, r)
                        if key not in cm_map:
                            cm_map[key] = cm_cols
                            cm_cols += qg
                        cm = cm_map[key]
                    entries.append(dict(t=t, h=h, sub=sub, kt=kt, qg=qg,
                                        col=col, cm=cm))
                    col += 1
    return entries, col, cm_map, cm_cols


ATT, NBIAS, CM_MAP, CM_COLS = _att_schedule()
SLOT_COL = []
for _t in range(4):
    _cs = [e["col"] for e in ATT if e["t"] == _t]
    SLOT_COL.append((min(_cs), max(_cs) + 1))
MAXBIAS = max(b_ - a_ for (a_, b_) in SLOT_COL)

ENGINES = ("pe", "act", "dve", "pool", "sp")


class Bld:
    def __init__(self, nc):
        self.nc = nc
        self.ops = {e: [] for e in ENGINES}
        self.sems = {}
        self._waited = {e: {} for e in ENGINES}

    def sem(self, name):
        if name not in self.sems:
            self.sems[name] = [self.nc.alloc_semaphore(name), 0]
        return self.sems[name]

    def op(self, eng, fn, waits=(), sem=None, dma=False):
        wl = []
        for w in waits:
            if w is None:
                continue
            h, v = w
            if v <= 0:
                continue
            k = id(h)
            if self._waited[eng].get(k, -1) >= v:
                continue
            self._waited[eng][k] = v
            wl.append((h, v))
        tok = None
        inc = None
        if sem is not None:
            rec = self.sem(sem)
            rec[1] += 16 if dma else 1
            tok = (rec[0], rec[1])
            inc = (rec[0], 16 if dma else 1)
        self.ops[eng].append((wl, fn, inc))
        return tok

    def fence(self, engs=ENGINES):
        return [self.op(e, None, sem="f_" + e) for e in engs]

    def phase_barrier(self):
        self.fence()
        waits = [(rec[0], rec[1]) for rec in self.sems.values()]
        for e in ENGINES:
            self.op(e, None, waits=waits)

    def emit(self):
        nc = self.nc
        with nc.Block() as blk:
            def mk(ename):
                def body(eng):
                    for (wl, fn, inc) in self.ops[ename]:
                        for (h, v) in wl:
                            eng.wait_ge(h, v)
                        if fn is None:
                            if inc is not None:
                                eng.nop().then_inc(inc[0], inc[1])
                        else:
                            ins = fn()
                            if inc is not None:
                                ins.then_inc(inc[0], inc[1])
                return body
            blk.tensor(mk("pe"))
            blk.scalar(mk("act"))
            blk.vector(mk("dve"))
            blk.gpsimd(mk("pool"))
            blk.sync(mk("sp"))


class Stream:

    def __init__(self, b, name, nslots):
        self.b = b
        self.name = name
        self.n = nslots
        self.war = [[] for _ in range(nslots)]
        self.i = 0

    def slot(self):
        return self.i % self.n

    def issue(self, eng, fn, waits=()):
        s = self.i % self.n
        self.i += 1
        tok = self.b.op(eng, fn, waits=list(waits) + self.war[s],
                        sem=f"{self.name}{s}", dma=True)
        self.war[s] = []
        return tok

    def consumed(self, s, tok):
        if tok is not None:
            self.war[s].append(tok)


def build_program(debug=False):
    nc = bass.Bass("TRN2", target_bir_lowering=False)
    b = Bld(nc)
    dbg = "ExternalOutput" if debug else "Internal"

    x_in = nc.dram_tensor("x_fm", [E, S], F32, kind="ExternalInput").ap()
    w0 = nc.dram_tensor("w0pp", [NCH, 128, E], F32, kind="ExternalInput").ap()
    wqp = nc.dram_tensor("wqpp", [34, 128, E], F32, kind="ExternalInput").ap()
    wcp = nc.dram_tensor("wcpp", [NCH, 128, E], F32, kind="ExternalInput").ap()
    aTp = nc.dram_tensor("aTp", [128, E], F32, kind="ExternalInput").ap()
    bTp = nc.dram_tensor("bTp", [128, E], F32, kind="ExternalInput").ap()
    b0v = nc.dram_tensor("b0v", [128, NCH], F32, kind="ExternalInput").ap()
    mk1 = nc.dram_tensor("mk1", [NCH, 128, NL, S], I8, kind="ExternalInput").ap()
    mk2 = nc.dram_tensor("mk2", [NCH, 128, NL, 2048], I8, kind="ExternalInput").ap()
    btbl = nc.dram_tensor("btbl", [128, NBIAS], F32, kind="ExternalInput").ap()
    cmt = nc.dram_tensor("cmt", [128, max(CM_COLS, 1)], BF16, kind="ExternalInput").ap()
    ixq = nc.dram_tensor("ixq", [128, 4 * NCH], I32, kind="ExternalInput").ap()
    ixm = nc.dram_tensor("ixm", [128, NCH * 4], I32, kind="ExternalInput").ap()
    ident_in = nc.dram_tensor("ident", [128, 192], BF16, kind="ExternalInput").ap()

    xmid = nc.dram_tensor("xmid", [E, S], F32, kind=dbg).ap()
    xln2d = nc.dram_tensor("xln2d", [E, S], BF16, kind=dbg).ap()
    pk = nc.dram_tensor("pk", [128, KPAD + S], BF16, kind=dbg).ap()
    pvx = nc.dram_tensor("pvx", [128, (NKT + 32) * 129], BF16, kind=dbg).ap()
    qd = nc.dram_tensor("qd", [H, 128, 2048], BF16, kind=dbg).ap()
    x2 = nc.dram_tensor("x2", [E, 2048], F32, kind=dbg).ap()
    out = nc.dram_tensor("out_fm", [E, 2048], F32, kind="ExternalOutput").ap()

    st_x = Stream(b, "sx", 4)
    st_w = Stream(b, "sw", 2)
    st_m = Stream(b, "sm", 4)
    st_xe = Stream(b, "sxe", 4)
    st_wr = Stream(b, "swr", 3)
    st_g = Stream(b, "sg", 4)
    st_c = Stream(b, "sc", 2)

    with contextlib.ExitStack() as top:
        ones_b = top.enter_context(nc.sbuf_tensor("ones_b", [128, 1], BF16))
        ones_r = top.enter_context(nc.sbuf_tensor("ones_r", [1, 128], F32))
        ident = top.enter_context(nc.sbuf_tensor("ident_sb", [128, 192], BF16))
        b0h = top.enter_context(nc.sbuf_tensor("b0h", [128, NCH], F32))

        b.op("dve", lambda: nc.vector.memset(ones_b[:], 1.0), sem="dve")
        b.op("dve", lambda: nc.vector.memset(ones_r[:], 1.0), sem="dve")
        tid = st_c.issue("sp", lambda: nc.sync.dma_start(out=ident[:], in_=ident_in))
        tb0 = st_c.issue("sp", lambda: nc.sync.dma_start(out=b0h[:], in_=b0v))
        tb0h = b.op("dve", lambda: nc.vector.tensor_scalar_mul(b0h[:], b0h[:], 0.5),
                    waits=[tb0], sem="dve")

        def ln_pass(nm, src_ap, src_col0, ntg, xln_sb, dram_dst=None):
            assert ntg == 4
            Tw = ntg * 512
            with nc.psum_tensor(nm + "_s1", [128, 512], F32) as ps_s1, \
                 nc.psum_tensor(nm + "_s2", [128, 512], F32) as ps_s2, \
                 nc.psum_tensor(nm + "_b1", [128, 4 * 512], F32) as ps_b1, \
                 nc.psum_tensor(nm + "_b2", [128, 512], F32) as ps_b2, \
                 nc.sbuf_tensor(nm + "_x", [128, 3 * 2048], F32) as lnx, \
                 nc.sbuf_tensor(nm + "_bf", [128, 3 * 2048], BF16) as lnb, \
                 nc.sbuf_tensor(nm + "_sq", [128, 3 * 2048], BF16) as lnq, \
                 nc.sbuf_tensor(nm + "_t", [128, 4 * 512], F32) as lnt, \
                 nc.sbuf_tensor(nm + "_r", [128, 4 * 512], F32) as lnr, \
                 nc.sbuf_tensor(nm + "_o", [128, 4 * 512], BF16) as lno, \
                 nc.sbuf_tensor(nm + "_rw", [1, 5 * 512], F32) as rows:
                st_ln = Stream(b, nm + "sx", 3)
                m_last = [None, None]
                for c in range(NCH):
                    s = st_ln.slot()
                    xt = st_ln.issue("sp", lambda c=c, s=s: nc.sync.dma_start(
                        out=lnx[:, s * 2048:(s + 1) * 2048],
                        in_=src_ap[c * 128:(c + 1) * 128, src_col0:src_col0 + 2048]))
                    ct = b.op("dve", lambda s=s: nc.vector.tensor_copy(
                        lnb[:, s * 2048:(s + 1) * 2048],
                        lnx[:, s * 2048:(s + 1) * 2048]), waits=[xt], sem="dve")
                    qt = b.op("act", lambda s=s: nc.scalar.activation(
                        lnq[:, s * 2048:(s + 1) * 2048],
                        lnx[:, s * 2048:(s + 1) * 2048], AF.Square),
                        waits=[xt], sem="act")
                    for tg in range(4):
                        m1 = b.op("pe", lambda s=s, c=c, tg=tg: nc.tensor.matmul(
                            ps_s1[32 * tg:32 * tg + 1, :], ones_b[:],
                            lnb[:, s * 2048 + tg * 512:s * 2048 + (tg + 1) * 512],
                            start=(c == 0), stop=(c == NCH - 1),
                            tile_position=(0, 32 * tg)),
                            waits=[ct], sem="pe")
                        m2 = b.op("pe", lambda s=s, c=c, tg=tg: nc.tensor.matmul(
                            ps_s2[32 * tg:32 * tg + 1, :], ones_b[:],
                            lnq[:, s * 2048 + tg * 512:s * 2048 + (tg + 1) * 512],
                            start=(c == 0), stop=(c == NCH - 1),
                            tile_position=(0, 32 * tg)),
                            waits=[qt], sem="pe")
                    st_ln.consumed(s, m2)
                    m_last = [m1, m2]
                bc1s = []
                for tg in range(4):
                    rm = rows[0:1, 0:512]
                    rr = rows[0:1, 512:1024]
                    ra = rows[0:1, 1024:1536]
                    rb = rows[0:1, 1536:2048]
                    rc = rows[0:1, 2048:2560]
                    b.op("dve", lambda rm=rm, tg=tg: nc.vector.tensor_scalar_mul(
                        rm, ps_s1[32 * tg:32 * tg + 1, :], 1.0 / E),
                        waits=m_last, sem="dve")
                    b.op("dve", lambda rb=rb, rm=rm: nc.vector.tensor_mul(rb, rm, rm),
                         sem="dve")
                    b.op("dve", lambda ra=ra, tg=tg: nc.vector.tensor_scalar_mul(
                        ra, ps_s2[32 * tg:32 * tg + 1, :], 1.0 / E), sem="dve")
                    b.op("dve", lambda ra=ra, rb=rb: nc.vector.tensor_sub(ra, ra, rb),
                         sem="dve")
                    tv = b.op("dve", lambda ra=ra: nc.vector.tensor_scalar_add(
                        ra, ra, EPS), sem="dve")
                    ts = b.op("act", lambda rb=rb, ra=ra: nc.scalar.activation(
                        rb, ra, AF.Sqrt), waits=[tv], sem="act")
                    b.op("dve", lambda rc=rc, rb=rb: nc.vector.reciprocal(rc, rb),
                         waits=[ts], sem="dve")
                    b.op("dve", lambda rb=rb, rc=rc: nc.vector.tensor_mul(rb, rc, rc),
                         sem="dve")
                    b.op("dve", lambda rb=rb, ra=ra: nc.vector.tensor_mul(rb, rb, ra),
                         sem="dve")
                    b.op("dve", lambda rb=rb: nc.vector.tensor_scalar(
                        rb, rb, -0.5, 1.5, ALU.mult, ALU.add), sem="dve")
                    trs = b.op("dve", lambda rr=rr, rc=rc, rb=rb: nc.vector.tensor_mul(
                        rr, rc, rb), sem="dve")
                    bc1 = b.op("pe", lambda rm=rm, tg=tg: nc.tensor.matmul(
                        ps_b1[:, tg * 512:(tg + 1) * 512], ones_r[:], rm,
                        start=True, stop=True), waits=[trs], sem="pe")
                    bc1s.append(bc1)
                    bc2 = b.op("pe", lambda rr=rr: nc.tensor.matmul(
                        ps_b2[:], ones_r[:], rr, start=True, stop=True), sem="pe")
                    trv = b.op("dve", lambda tg=tg: nc.vector.tensor_copy(
                        lnr[:, tg * 512:(tg + 1) * 512], ps_b2[:]),
                        waits=[bc2], sem="dve")
                lnt_war = [[] for _ in range(4)]
                lno_war = [[] for _ in range(4)]
                for c in range(NCH):
                    s = st_ln.slot()
                    xt = st_ln.issue("sp", lambda c=c, s=s: nc.sync.dma_start(
                        out=lnx[:, s * 2048:(s + 1) * 2048],
                        in_=src_ap[c * 128:(c + 1) * 128, src_col0:src_col0 + 2048]))
                    last = None
                    for tg in range(4):
                        tsb = b.op("dve", lambda s=s, tg=tg: nc.vector.tensor_sub(
                            lnt[:, tg * 512:(tg + 1) * 512],
                            lnx[:, s * 2048 + tg * 512:s * 2048 + (tg + 1) * 512],
                            ps_b1[:, tg * 512:(tg + 1) * 512]),
                            waits=[xt, trv, bc1s[tg]] + lnt_war[tg], sem="dve")
                        lnt_war[tg] = []
                        last = tsb
                        if dram_dst is None:
                            dcol = c * Tw + tg * 512
                            tp = b.op("pool", lambda tg=tg, dcol=dcol: nc.gpsimd.tensor_mul(
                                xln_sb[:, dcol:dcol + 512],
                                lnt[:, tg * 512:(tg + 1) * 512],
                                lnr[:, tg * 512:(tg + 1) * 512]),
                                waits=[tsb], sem="pool")
                            lnt_war[tg].append(tp)
                        else:
                            tp = b.op("pool", lambda tg=tg: nc.gpsimd.tensor_mul(
                                lno[:, tg * 512:(tg + 1) * 512],
                                lnt[:, tg * 512:(tg + 1) * 512],
                                lnr[:, tg * 512:(tg + 1) * 512]),
                                waits=[tsb] + lno_war[tg], sem="pool")
                            lnt_war[tg].append(tp)
                            lno_war[tg] = []
                            wt = st_wr.issue("act", lambda tg=tg, c=c:
                                             nc.scalar.dma_start(
                                out=dram_dst[c * 128:(c + 1) * 128,
                                             src_col0 + tg * 512:src_col0 + tg * 512 + 512],
                                in_=lno[:, tg * 512:(tg + 1) * 512]), waits=[tp])
                            lno_war[tg].append(wt)
                    st_ln.consumed(s, last)
                b.phase_barrier()

        def ffn_pass(nm, xln_sb, ntg, mask_ap, mask_col0,
                     xsrc_ap, xsrc_col0, xdst_ap, xdst_col0):
            Tw = ntg * 512
            with nc.psum_tensor(nm + "_ba", [128, 4 * 512], F32) as ps_ba, \
                 nc.psum_tensor(nm + "_y", [128, 2 * 512], F32) as ps_y, \
                 nc.psum_tensor(nm + "_h", [128, 512], F32) as ps_h, \
                 nc.sbuf_tensor(nm + "_aT", [128, E], BF16) as aTs, \
                 nc.sbuf_tensor(nm + "_bT", [128, E], BF16) as bTs, \
                 nc.sbuf_tensor(nm + "_hs", [128, Tw], BF16) as h_sb, \
                 nc.sbuf_tensor(nm + "_w", [128, 2 * E], BF16) as wbuf, \
                 nc.sbuf_tensor(nm + "_m", [128, 4 * NL * 512], I8) as mbuf, \
                 nc.sbuf_tensor(nm + "_xe", [128, 4 * 512], F32) as xeb, \
                 nc.sbuf_tensor(nm + "_t0", [128, 4 * 512], F32) as t0b, \
                 nc.sbuf_tensor(nm + "_ac", [128, 2 * 512], F32) as acb, \
                 nc.sbuf_tensor(nm + "_xn", [128, 2 * 512], F32) as xnb:

                ta = st_c.issue("pool", lambda: nc.gpsimd.dma_start(
                    out=aTs[:], in_=aTp))
                tb = st_c.issue("pool", lambda: nc.gpsimd.dma_start(
                    out=bTs[:], in_=bTp))
                htoks = [None] * ntg
                hprev = None
                for tg in range(ntg):
                    mm = None
                    for dc in range(NCH):
                        mm = b.op("pe", lambda dc=dc, tg=tg: nc.tensor.matmul(
                            ps_h[:], aTs[:, dc * 128:(dc + 1) * 128],
                            xln_sb[:, dc * Tw + tg * 512:dc * Tw + tg * 512 + 512],
                            start=(dc == 0), stop=(dc == NCH - 1)),
                            waits=([ta, hprev] if dc == 0 else ()), sem="pe")
                    htoks[tg] = b.op("dve", lambda tg=tg: nc.vector.tensor_copy(
                        h_sb[:, tg * 512:(tg + 1) * 512], ps_h[:]),
                        waits=[mm], sem="dve")
                    hprev = htoks[tg]

                ba_war = [[], [], [], []]
                y_war = [[], []]
                t0_war = [[], [], [], []]
                ac_war = [[], []]
                xn_war = [[], []]
                for oc in range(NCH):
                    ws = st_w.slot()
                    wt = st_w.issue("pool", lambda oc=oc, ws=ws: nc.gpsimd.dma_start(
                        out=wbuf[:, ws * E:(ws + 1) * E], in_=w0[oc]))
                    lastmm = None
                    xss, mss, xets, mts = [], [], [], []
                    for tg in range(ntg):
                        xs = st_xe.slot()
                        xets.append(st_xe.issue("sp", lambda oc=oc, tg=tg, xs=xs:
                                                nc.sync.dma_start(
                            out=xeb[:, xs * 512:(xs + 1) * 512],
                            in_=xsrc_ap[oc * 128:(oc + 1) * 128,
                                        xsrc_col0 + tg * 512:xsrc_col0 + tg * 512 + 512])))
                        xss.append(xs)
                        ms = st_m.slot()
                        mts.append(st_m.issue("act", lambda oc=oc, tg=tg, ms=ms:
                                              nc.scalar.dma_start(
                            out=mbuf[:, ms * NL * 512:(ms + 1) * NL * 512]
                                .rearrange("p (l t) -> p l t", l=NL),
                            in_=mask_ap[oc, :, :,
                                        mask_col0 + tg * 512:mask_col0 + tg * 512 + 512])))
                        mss.append(ms)
                    mmtg = [None] * ntg
                    for dc in range(NCH):
                        for tg in range(ntg):
                            mmtg[tg] = b.op("pe", lambda dc=dc, tg=tg, ws=ws:
                                            nc.tensor.matmul(
                                ps_ba[:, tg * 512:(tg + 1) * 512],
                                wbuf[:, ws * E + dc * 128:ws * E + (dc + 1) * 128],
                                xln_sb[:, dc * Tw + tg * 512:dc * Tw + tg * 512 + 512],
                                start=(dc == 0), stop=(dc == NCH - 1)),
                                waits=([wt] + ba_war[tg]) if dc == 0 else (), sem="pe")
                            if dc == 0:
                                ba_war[tg] = []
                    tacts = []
                    for tg in range(ntg):
                        u = oc * ntg + tg
                        ab = u % 2
                        tacts.append(b.op("act", lambda oc=oc, tg=tg:
                                          nc.scalar.activation(
                            t0b[:, tg * 512:(tg + 1) * 512],
                            ps_ba[:, tg * 512:(tg + 1) * 512], AF.Identity,
                            bias=b0h[:, oc:oc + 1], scale=0.5),
                            waits=[mmtg[tg], tb0h] + t0_war[tg], sem="act"))
                        t0_war[tg] = []
                    for tg in range(ntg):
                        u = oc * ntg + tg
                        ab = u % 2
                        bb = tg
                        mm = mmtg[tg]
                        xs = xss[tg]
                        ms = mss[tg]
                        xet = xets[tg]
                        mt = mts[tg]
                        tact = tacts[tg]
                        lastmm = mm

                        def ymm(l, yb):
                            return b.op("pe", lambda l=l, oc=oc, tg=tg, yb=yb:
                                        nc.tensor.matmul(
                                ps_y[:, yb * 512:(yb + 1) * 512],
                                bTs[32 * l:32 * l + 16, oc * 128:(oc + 1) * 128],
                                h_sb[32 * l:32 * l + 16, tg * 512:(tg + 1) * 512],
                                start=True, stop=True, tile_position=(32 * l, 0)),
                                waits=[tb, htoks[tg]] + y_war[yb], sem="pe")
                        y0 = ymm(0, 0)
                        y_war[0] = []
                        y1 = ymm(1, 1)
                        y_war[1] = []
                        d0 = b.op("dve", lambda ab=ab: nc.vector.tensor_copy(
                            acb[:, ab * 512:(ab + 1) * 512], ps_y[:, 0:512]),
                            waits=[y0] + ac_war[ab], sem="dve")
                        ac_war[ab] = []
                        y_war[0].append(d0)
                        y2 = ymm(2, 0)
                        y_war[0] = []
                        d1 = b.op("dve", lambda ab=ab, ms=ms: nc.vector.copy_predicated(
                            acb[:, ab * 512:(ab + 1) * 512],
                            mbuf[:, ms * NL * 512 + 512:ms * NL * 512 + 1024],
                            ps_y[:, 512:1024]), waits=[y1, mt], sem="dve")
                        y_war[1].append(d1)
                        y3 = ymm(3, 1)
                        y_war[1] = []
                        d2 = b.op("dve", lambda ab=ab, ms=ms: nc.vector.copy_predicated(
                            acb[:, ab * 512:(ab + 1) * 512],
                            mbuf[:, ms * NL * 512 + 1024:ms * NL * 512 + 1536],
                            ps_y[:, 0:512]), waits=[y2], sem="dve")
                        y_war[0].append(d2)
                        d3 = b.op("dve", lambda ab=ab, ms=ms: nc.vector.copy_predicated(
                            acb[:, ab * 512:(ab + 1) * 512],
                            mbuf[:, ms * NL * 512 + 1536:ms * NL * 512 + 2048],
                            ps_y[:, 512:1024]), waits=[y3], sem="dve")
                        y_war[1].append(d3)
                        st_m.consumed(ms, d3)
                        ba_war[bb].append(tact)
                        tst = b.op("dve", lambda ab=ab, tg=tg: nc.vector.scalar_tensor_tensor(
                            acb[:, ab * 512:(ab + 1) * 512],
                            acb[:, ab * 512:(ab + 1) * 512], 4.0,
                            t0b[:, tg * 512:(tg + 1) * 512], ALU.mult, ALU.add),
                            waits=[tact], sem="dve")
                        t0_war[tg].append(tst)
                        tpn = b.op("pool", lambda ab=ab, xs=xs: nc.gpsimd.tensor_add(
                            xnb[:, ab * 512:(ab + 1) * 512],
                            acb[:, ab * 512:(ab + 1) * 512],
                            xeb[:, xs * 512:(xs + 1) * 512]),
                            waits=[tst, xet] + xn_war[ab], sem="pool")
                        xn_war[ab] = []
                        ac_war[ab].append(tpn)
                        st_xe.consumed(xs, tpn)
                        wtok = st_wr.issue("act", lambda oc=oc, tg=tg, ab=ab:
                                           nc.scalar.dma_start(
                            out=xdst_ap[oc * 128:(oc + 1) * 128,
                                        xdst_col0 + tg * 512:xdst_col0 + tg * 512 + 512],
                            in_=xnb[:, ab * 512:(ab + 1) * 512]), waits=[tpn])
                        xn_war[ab].append(wtok)
                    st_w.consumed(ws, lastmm)
                b.phase_barrier()

        with contextlib.ExitStack() as sa:
            xln = sa.enter_context(nc.sbuf_tensor("xln", [128, NCH * 2048], BF16))
            for hf in range(2):
                ln_pass(f"l1{hf}", x_in, hf * 2048, 4, xln)
                ffn_pass(f"f1{hf}", xln, 4, mk1, hf * 2048,
                         x_in, hf * 2048, xmid, hf * 2048)
            ln_pass("l2a", xmid, 0, 4, None, dram_dst=xln2d)
            ln_pass("l2b", xmid, 2048, 4, None, dram_dst=xln2d)

        with contextlib.ExitStack() as sq:
            kvs = sq.enter_context(nc.sbuf_tensor("kvs", [128, 2 * 512], BF16))
            v_sb = sq.enter_context(nc.sbuf_tensor("v_sb", [128, S], BF16))
            xst = sq.enter_context(nc.sbuf_tensor("xst", [128, 3 * 2048], BF16))
            wqb = sq.enter_context(nc.sbuf_tensor("wqb", [128, 2 * E], BF16))
            zbuf = sq.enter_context(nc.sbuf_tensor("zbuf", [128, 516], BF16))
            tz = b.op("dve", lambda: nc.vector.memset(zbuf[:], 0.0), sem="dve")
            for i in range(9):
                st_wr.issue("act", lambda i=i: nc.scalar.dma_start(
                    out=pk[:, i * 512:(i + 1) * 512], in_=zbuf[:, 0:512]), waits=[tz])
            for i in range(9):
                st_wr.issue("act", lambda i=i: nc.scalar.dma_start(
                    out=pvx[:, i * 516:(i + 1) * 516], in_=zbuf[:, 0:516]), waits=[tz])
            with nc.psum_tensor("ps_kv", [128, 8 * 512], F32) as ps_kv:
                kv_war = [[] for _ in range(8)]
                kvslot_war = [[], []]
                vdone = []
                st_kv = Stream(b, "skv", 3)
                wts = []
                for kvoc in range(2):
                    ws = st_w.slot()
                    wts.append(st_w.issue("pool", lambda kvoc=kvoc, ws=ws:
                                          nc.gpsimd.dma_start(
                        out=wqb[:, ws * E:(ws + 1) * E], in_=wqp[32 + kvoc])))
                mmk = [None] * 8
                for hf in range(2):
                    for dc in range(NCH):
                        s = st_kv.slot()
                        xt = st_kv.issue("sp", lambda dc=dc, hf=hf, s=s:
                                         nc.sync.dma_start(
                            out=xst[:, s * 2048:(s + 1) * 2048],
                            in_=xln2d[dc * 128:(dc + 1) * 128,
                                      hf * 2048:(hf + 1) * 2048]))
                        last = None
                        for kvoc in range(2):
                            for tg in range(4):
                                bk = kvoc * 4 + tg
                                last = b.op("pe", lambda dc=dc, kvoc=kvoc, tg=tg,
                                            bk=bk, s=s: nc.tensor.matmul(
                                    ps_kv[:, bk * 512:(bk + 1) * 512],
                                    wqb[:, (kvoc % 2) * E + dc * 128:
                                        (kvoc % 2) * E + (dc + 1) * 128],
                                    xst[:, s * 2048 + tg * 512:s * 2048 + (tg + 1) * 512],
                                    start=(dc == 0), stop=(dc == NCH - 1)),
                                    waits=([wts[kvoc], xt] + kv_war[bk])
                                          if dc == 0 else [xt], sem="pe")
                                if dc == 0:
                                    kv_war[bk] = []
                                mmk[bk] = last
                        st_kv.consumed(s, last)
                    for kvoc in range(2):
                        for tg in range(4):
                            bk = kvoc * 4 + tg
                            gtg = hf * 4 + tg
                            kb = bk % 2
                            ev = b.op("dve", lambda bk=bk, kb=kb: nc.vector.tensor_copy(
                                kvs[:, kb * 512:(kb + 1) * 512],
                                ps_kv[:, bk * 512:(bk + 1) * 512]),
                                waits=[mmk[bk]] + kv_war[bk] + kvslot_war[kb],
                                sem="dve")
                            kvslot_war[kb] = []
                            kv_war[bk].append(ev)
                            if kvoc == 0:
                                wk = st_wr.issue("act", lambda gtg=gtg, kb=kb:
                                                 nc.scalar.dma_start(
                                    out=pk[:, KPAD + gtg * 512:KPAD + (gtg + 1) * 512],
                                    in_=kvs[:, kb * 512:(kb + 1) * 512]), waits=[ev])
                                kv_war[bk].append(wk)
                                kvslot_war[kb].append(wk)
                            else:
                                vt = b.op("dve", lambda gtg=gtg, kb=kb:
                                          nc.vector.tensor_copy(
                                    v_sb[:, gtg * 512:(gtg + 1) * 512],
                                    kvs[:, kb * 512:(kb + 1) * 512]),
                                    waits=[ev], sem="dve")
                                vdone.append(vt)
                                kv_war[bk].append(vt)
                                kvslot_war[kb].append(vt)
                st_w.consumed(st_w.i % st_w.n, mmk[0])
                st_w.consumed((st_w.i + 1) % st_w.n, mmk[7])
            with nc.psum_tensor("ps_vt", [128, 128], BF16) as ps_vt, \
                 nc.sbuf_tensor("vstg", [128, 2 * 129], BF16) as vstg:
                to1 = b.op("dve", lambda: nc.vector.memset(vstg[:], 1.0),
                           waits=vdone, sem="dve")
                vw = [[], []]
                tprev = None
                for kt in range(32):
                    vs = kt % 2
                    tt = b.op("pe", lambda kt=kt: nc.tensor.transpose(
                        ps_vt[:], v_sb[:, kt * 128:(kt + 1) * 128], ident[:, 0:128]),
                        waits=[to1, tid] + ([tprev] if tprev else []), sem="pe")
                    ev = b.op("dve", lambda vs=vs: nc.vector.tensor_copy(
                        vstg[:, vs * 129:vs * 129 + 128], ps_vt[:]),
                        waits=[tt] + vw[vs], sem="dve")
                    vw[vs] = []
                    tprev = ev
                    wt = st_wr.issue("act", lambda kt=kt, vs=vs: nc.scalar.dma_start(
                        out=pvx[:, (NKT + kt) * 129:(NKT + kt + 1) * 129],
                        in_=vstg[:, vs * 129:(vs + 1) * 129]), waits=[ev])
                    vw[vs].append(wt)
            b.phase_barrier()
            with nc.sbuf_tensor("xlq", [128, 4 * NCH * 512], BF16) as xlq, \
                 nc.sbuf_tensor("ixqs", [128, 4 * NCH], I32) as ixqs, \
                 nc.sbuf_tensor("qstg", [128, 2 * 512], BF16) as qstg, \
                 nc.psum_tensor("ps_q", [128, 4 * 512], F32) as ps_q:
                tix = st_c.issue("sp", lambda: nc.sync.dma_start(out=ixqs[:], in_=ixq))
                gtoks = []
                for g in range(4):
                    for dc in range(NCH):
                        gt = st_g.issue("pool", lambda g=g, dc=dc:
                                        nc.gpsimd.indirect_dma_start(
                            xlq[:, (g * NCH + dc) * 512:(g * NCH + dc + 1) * 512],
                            None, xln2d,
                            bass.IndirectOffsetOnAxis(
                                ixqs[:, g * NCH + dc:g * NCH + dc + 1], 1)),
                            waits=[tix])
                        gtoks.append(gt)
                q_war = [[], [], [], []]
                qs_war = [[], []]
                for oc in range(NCH):
                    ws = st_w.slot()
                    wt = st_w.issue("pool", lambda oc=oc, ws=ws: nc.gpsimd.dma_start(
                        out=wqb[:, ws * E:(ws + 1) * E], in_=wqp[oc]))
                    mmg = [None] * 4
                    for dc in range(NCH):
                        for g in range(4):
                            mmg[g] = b.op("pe", lambda dc=dc, g=g, ws=ws:
                                          nc.tensor.matmul(
                                ps_q[:, g * 512:(g + 1) * 512],
                                wqb[:, ws * E + dc * 128:ws * E + (dc + 1) * 128],
                                xlq[:, (g * NCH + dc) * 512:(g * NCH + dc + 1) * 512],
                                start=(dc == 0), stop=(dc == NCH - 1)),
                                waits=([wt] + q_war[g] + [gtoks[g * NCH + dc]])
                                      if dc == 0 else [gtoks[g * NCH + dc]],
                                sem="pe")
                            if dc == 0:
                                q_war[g] = []
                    for g in range(4):
                        qb = g % 2
                        ev = b.op("dve", lambda qb=qb, g=g: nc.vector.tensor_copy(
                            qstg[:, qb * 512:(qb + 1) * 512],
                            ps_q[:, g * 512:(g + 1) * 512]),
                            waits=[mmg[g]] + qs_war[qb], sem="dve")
                        qs_war[qb] = []
                        q_war[g].append(ev)
                        wqt = st_wr.issue("act", lambda oc=oc, g=g, qb=qb:
                                          nc.scalar.dma_start(
                            out=qd[oc, :, g * 512:(g + 1) * 512],
                            in_=qstg[:, qb * 512:(qb + 1) * 512]), waits=[ev])
                        qs_war[qb].append(wqt)
                    st_w.consumed(ws, mmg[3])
            b.phase_barrier()

        with contextlib.ExitStack() as sbA:
            attn = sbA.enter_context(nc.sbuf_tensor("attn", [128, H * 2048], BF16))
            st_kw = Stream(b, "skw", 1)
            st_vw = Stream(b, "svw", 1)
            st_qw = Stream(b, "sqw", 2)
            st_bi = Stream(b, "sbi", 1)
            with nc.sbuf_tensor("kwin", [128, NKT * 128], BF16) as kwin, \
                 nc.sbuf_tensor("vwin", [128, NKT * 129], BF16) as vwin, \
                 nc.sbuf_tensor("qwin", [128, 2 * 8 * 512], BF16) as qwin, \
                 nc.sbuf_tensor("pbuf", [128, 16 * 512], BF16) as pbuf, \
                 nc.sbuf_tensor("bias", [128, MAXBIAS], F32) as bias, \
                 nc.sbuf_tensor("cms", [128, max(CM_COLS, 1)], BF16) as cms, \
                 nc.sbuf_tensor("atok", [128, 2 * 128], BF16) as atok, \
                 nc.sbuf_tensor("arec", [128, 2], F32) as arec, \
                 nc.psum_tensor("ps_s", [128, 4 * 512], F32) as ps_s, \
                 nc.psum_tensor("ps_pv", [128, 4 * 512], F32) as ps_pv:
                tcm = st_c.issue("sp", lambda: nc.sync.dma_start(out=cms[:], in_=cmt))
                pv_war = [[] for _ in range(4)]
                atreg_war = [[] for _ in range(4)]
                s_war = [[], [], [], []]
                at_war = [[], []]
                rec_war = [[], []]
                pb_war = [[] for _ in range(16)]
                uctr = 0
                for t in range(4):
                    QEND = (2 * t + 2) * 512
                    QT = QEND // 128
                    tkw = st_kw.issue("sp", lambda QEND=QEND: nc.sync.dma_start(
                        out=kwin[:], in_=pk[:, QEND:QEND + KPAD]))
                    tvw = st_vw.issue("sp", lambda QT=QT: nc.sync.dma_start(
                        out=vwin[:], in_=pvx[:, QT * 129:(QT + NKT) * 129]))
                    lo, hi = SLOT_COL[t]
                    tbi = st_bi.issue("sp", lambda lo=lo, hi=hi: nc.sync.dma_start(
                        out=bias[:, 0:hi - lo], in_=btbl[:, lo:hi]))
                    ents = [e for e in ATT if e["t"] == t]
                    tqw = None
                    for h in range(H):
                        if h % 8 == 0:
                            hh = h // 8
                            qs_ = hh % 2
                            tqw = st_qw.issue("sp", lambda t=t, hh=hh, qs_=qs_:
                                              nc.sync.dma_start(
                                out=qwin[:, qs_ * 8 * 512:(qs_ + 1) * 8 * 512]
                                    .rearrange("p (h t) -> p h t", h=8),
                                in_=qd[hh * 8:(hh + 1) * 8, :,
                                       t * 512:(t + 1) * 512]
                                    .rearrange("h d t -> d h t")))
                        qg = QG[h]
                        qoff = ((h // 8) % 2) * 4096 + (h % 8) * 512
                        hents = [e for e in ents if e["h"] == h]
                        units = []
                        for sub in sorted(set(e["sub"] for e in hents)):
                            sents = sorted([e for e in hents if e["sub"] == sub],
                                           key=lambda d: d["kt"])
                            for i, e in enumerate(sents):
                                units.append((e, i == 0, i == len(sents) - 1))
                        pv_list = []
                        touched = [False] * 4

                        def emit_pv(e, first, last, pb, ptok, touched=touched,
                                    pv_list=pv_list, qg=qg):
                            sub, kt = e["sub"], e["kt"]
                            njt = max(1, qg // 128)
                            for j in range(njt):
                                if qg >= 128:
                                    qi = sub * njt + j
                                    prow = 0
                                    rows_ = 128
                                else:
                                    qi = sub // 2
                                    prow = 64 * (sub % 2)
                                    rows_ = 64
                                w_ = [ptok, tvw]
                                if first and not touched[qi]:
                                    w_ = w_ + pv_war[qi] + atreg_war[qi]
                                    touched[qi] = True
                                    pv_war[qi] = []
                                    atreg_war[qi] = []
                                tp = (0, prow) if prow else None
                                pm = b.op("pe", lambda pb=pb, j=j, kt=kt, qi=qi,
                                          prow=prow, rows_=rows_, first=first,
                                          last=last, tp=tp, qg=qg:
                                          nc.tensor.matmul(
                                    ps_pv[prow:prow + rows_,
                                          qi * 512:qi * 512 + 129],
                                    pbuf[:, pb * 512 + j * 128:
                                         pb * 512 + j * 128 + min(128, qg)],
                                    vwin[:, kt * 129:(kt + 1) * 129],
                                    start=first, stop=last, tile_position=tp),
                                    waits=w_, sem="pe")
                                pb_war[pb].append(pm)
                                if last:
                                    pv_list.append((qi, prow, rows_, sub, j, pm))

                        BDEPTH = 12
                        pend = []
                        for b0_ in range(0, len(units), BDEPTH):
                            batch = units[b0_:b0_ + BDEPTH]
                            newpend = []
                            for i, (e, first, last) in enumerate(batch):
                                kt = e["kt"]
                                ss = uctr % 4
                                pb = uctr % 16
                                uctr += 1
                                sm = b.op("pe", lambda kt=kt, qg=qg, ss=ss,
                                          qo=qoff + e["sub"] * qg:
                                          nc.tensor.matmul(
                                    ps_s[:, ss * 512:ss * 512 + qg],
                                    kwin[:, kt * 128:(kt + 1) * 128],
                                    qwin[:, qo:qo + qg],
                                    start=True, stop=True),
                                    waits=[tkw, tqw] + s_war[ss], sem="pe")
                                s_war[ss] = []
                                ex = b.op("act", lambda ss=ss, pb=pb, qg=qg,
                                          cix=e["col"] - lo: nc.scalar.activation(
                                    pbuf[:, pb * 512:pb * 512 + qg],
                                    ps_s[:, ss * 512:ss * 512 + qg], AF.Exp,
                                    bias=bias[:, cix:cix + 1], scale=SCALE),
                                    waits=[sm, tbi] + pb_war[pb], sem="act")
                                pb_war[pb] = []
                                s_war[ss].append(ex)
                                ptok = ex
                                if e["cm"] is not None:
                                    ptok = b.op("dve", lambda pb=pb, qg=qg,
                                                cmo=e["cm"]: nc.vector.tensor_mul(
                                        pbuf[:, pb * 512:pb * 512 + qg],
                                        pbuf[:, pb * 512:pb * 512 + qg],
                                        cms[:, cmo:cmo + qg]),
                                        waits=[ptok, tcm], sem="dve")
                                newpend.append((e, first, last, pb, ptok))
                                if i < len(pend):
                                    emit_pv(*pend[i])
                            for j in range(len(batch), len(pend)):
                                emit_pv(*pend[j])
                            pend = newpend
                        for p_ in pend:
                            emit_pv(*p_)
                        for (qi, prow, rows_, sub, j, pm) in pv_list:
                            lcol = t * 512 + sub * qg + j * 128
                            ab_ = (sub + j) % 2
                            rt = b.op("dve", lambda qi=qi, prow=prow, rows_=rows_,
                                      ab_=ab_: nc.vector.reciprocal(
                                arec[prow:prow + rows_, ab_:ab_ + 1],
                                ps_pv[prow:prow + rows_, qi * 512 + 128:qi * 512 + 129]),
                                waits=[pm] + rec_war[ab_], sem="dve")
                            dt = b.op("dve", lambda qi=qi, prow=prow, rows_=rows_,
                                      ab_=ab_: nc.vector.tensor_scalar(
                                atok[prow:prow + rows_, ab_ * 128:ab_ * 128 + 128],
                                ps_pv[prow:prow + rows_, qi * 512:qi * 512 + 128],
                                arec[prow:prow + rows_, ab_:ab_ + 1], None, ALU.mult),
                                waits=[rt] + at_war[ab_], sem="dve")
                            at_war[ab_] = []
                            rec_war[ab_] = [dt]
                            pv_war[qi].append(dt)
                            if rows_ == 128:
                                idap = ident[0:128, 0:128]
                            elif prow == 0:
                                idap = ident[0:64, 0:64]
                            else:
                                idap = ident[64:128, 128:192]
                            at_ap = ps_pv[:, qi * 512 + 256:qi * 512 + 320].bitcast(BF16)
                            tt = b.op("pe", lambda prow=prow, rows_=rows_, ab_=ab_,
                                      idap=idap, at_ap=at_ap: nc.tensor.transpose(
                                at_ap[0:128, 0:rows_],
                                atok[prow:prow + rows_, ab_ * 128:ab_ * 128 + 128],
                                idap),
                                waits=[dt] + atreg_war[qi], sem="pe")
                            atreg_war[qi] = []
                            ev = b.op("dve", lambda h=h, lcol=lcol, rows_=rows_,
                                      at_ap=at_ap: nc.vector.tensor_copy(
                                attn[:, h * 2048 + lcol:h * 2048 + lcol + rows_],
                                at_ap[0:128, 0:rows_]),
                                waits=[tt], sem="dve")
                            atreg_war[qi].append(ev)
                            pv_war[qi].append(ev)
                            at_war[ab_].append(ev)
                        if h % 8 == 7:
                            st_qw.consumed((h // 8) % 2, b.fence(("pe",))[0])
                    fpe = b.fence(("pe",))[0]
                    st_kw.consumed(0, fpe)
                    st_vw.consumed(0, fpe)
                    st_bi.consumed(0, b.fence(("act",))[0])
                    st_qw.consumed(0, fpe)
                    st_qw.consumed(1, fpe)
                b.phase_barrier()

            with nc.sbuf_tensor("wcb", [128, 2 * E], BF16) as wcb, \
                 nc.sbuf_tensor("xmg", [128, 4 * 512], F32) as xmg, \
                 nc.sbuf_tensor("ixms", [128, NCH * 4], I32) as ixms, \
                 nc.sbuf_tensor("x2s", [128, 2 * 512], F32) as x2s, \
                 nc.psum_tensor("ps_wc", [128, 4 * 512], F32) as ps_wc:
                tix = st_c.issue("sp", lambda: nc.sync.dma_start(out=ixms[:], in_=ixm))
                wc_war = [[], [], [], []]
                x2_war = [[], []]
                for oc in range(NCH):
                    ws = st_w.slot()
                    wt = st_w.issue("pool", lambda oc=oc, ws=ws: nc.gpsimd.dma_start(
                        out=wcb[:, ws * E:(ws + 1) * E], in_=wcp[oc]))
                    mms = [None] * 4
                    for dc in range(NCH):
                        for tg in range(4):
                            mms[tg] = b.op("pe", lambda dc=dc, tg=tg, ws=ws:
                                           nc.tensor.matmul(
                                ps_wc[:, tg * 512:(tg + 1) * 512],
                                wcb[:, ws * E + dc * 128:ws * E + (dc + 1) * 128],
                                attn[:, dc * 2048 + tg * 512:dc * 2048 + (tg + 1) * 512],
                                start=(dc == 0), stop=(dc == NCH - 1)),
                                waits=([wt] + wc_war[tg]) if dc == 0 else (), sem="pe")
                            if dc == 0:
                                wc_war[tg] = []
                    for tg in range(4):
                        gs = st_g.slot()
                        gt = st_g.issue("pool", lambda oc=oc, tg=tg, gs=gs:
                                        nc.gpsimd.indirect_dma_start(
                            xmg[:, gs * 512:(gs + 1) * 512], None, xmid,
                            bass.IndirectOffsetOnAxis(
                                ixms[:, oc * 4 + tg:oc * 4 + tg + 1], 1)),
                            waits=[tix])
                        xb_ = tg % 2
                        ad = b.op("dve", lambda tg=tg, gs=gs, xb_=xb_:
                                  nc.vector.tensor_add(
                            x2s[:, xb_ * 512:(xb_ + 1) * 512],
                            ps_wc[:, tg * 512:(tg + 1) * 512],
                            xmg[:, gs * 512:(gs + 1) * 512]),
                            waits=[mms[tg], gt] + x2_war[xb_], sem="dve")
                        x2_war[xb_] = []
                        wc_war[tg].append(ad)
                        st_g.consumed(gs, ad)
                        wt2 = st_wr.issue("act", lambda oc=oc, tg=tg, xb_=xb_:
                                          nc.scalar.dma_start(
                            out=x2[oc * 128:(oc + 1) * 128, tg * 512:(tg + 1) * 512],
                            in_=x2s[:, xb_ * 512:(xb_ + 1) * 512]), waits=[ad])
                        x2_war[xb_].append(wt2)
                    st_w.consumed(ws, mms[3])
                b.phase_barrier()

        with contextlib.ExitStack() as sc:
            xln3 = sc.enter_context(nc.sbuf_tensor("xln3", [128, NCH * 2048], BF16))
            ln_pass("l3", x2, 0, 4, xln3)
            ffn_pass("f2", xln3, 4, mk2, 0, x2, 0, out, 0)

        final_waits = [(rec[0], rec[1]) for rec in b.sems.values()]
        b.op("sp", None, waits=final_waits)

    b.emit()
    return nc


def _host_tables(par):
    bias = np.zeros((128, NBIAS), np.float32)
    ki = np.arange(128)
    for e in ATT:
        t, h, sub, kt, qg, col = e["t"], e["h"], e["sub"], e["kt"], e["qg"], e["col"]
        qend = (2 * t + 2) * 512
        own = (2 * t + par) * 512
        qbase = own + sub * qg
        kpos = qend - KPAD + kt * 128 + ki
        v = SLOPES[h] * (kpos - qbase) - BETA
        killed = (kpos < 0) | (kpos > qbase + qg - 1)
        bias[:, col] = np.where(killed, KILL, v).astype(np.float32)
        if e["cm"] is None:
            assert np.all(killed | (kpos <= qbase)), ("missing cmask", t, h, sub, kt)
    cmask = np.zeros((128, max(CM_COLS, 1)), np.float32)
    for (qg, sub, r), coff in CM_MAP.items():
        kt = NKT - 1 - r
        qend_m_qbase = 1024 - 512 * par - sub * qg
        kpos_rel = kt * 128 + ki - KPAD + qend_m_qbase
        qj = np.arange(qg)
        cmask[:, coff:coff + qg] = (qj[None, :] >= kpos_rel[:, None])
    return bias, cmask.astype(ml_dtypes.bfloat16)


def _threefry2x32(kx, ky, c0, c1):
    rot = [(13, 15, 26, 6), (17, 29, 16, 24)]
    ks0 = np.uint32(kx); ks1 = np.uint32(ky)
    ks2 = np.uint32(np.uint32(0x1BD11BDA) ^ ks0 ^ ks1)
    x0 = (c0 + ks0).astype(np.uint32); x1 = (c1 + ks1).astype(np.uint32)
    ks = [ks0, ks1, ks2]
    for i in range(5):
        for d in rot[i % 2]:
            x0 = (x0 + x1).astype(np.uint32)
            x1 = ((x1 << np.uint32(d)) | (x1 >> np.uint32(32 - d))).astype(np.uint32)
            x1 = (x0 ^ x1).astype(np.uint32)
        x0 = (x0 + ks[(i + 1) % 3]).astype(np.uint32)
        x1 = (x1 + ks[(i + 2) % 3] + np.uint32(i + 1)).astype(np.uint32)
    return x0, x1


def _dropout_mask_indices():
    n = NB * S * E
    out = np.empty(n, np.int8)
    chunk = 1 << 24
    for c0 in range(0, n, chunk):
        c = np.arange(c0, min(n, c0 + chunk), dtype=np.uint32)
        x0, x1 = _threefry2x32(0, 42, np.zeros_like(c), c)
        bits = x0 ^ x1
        fl = ((bits >> np.uint32(9)) | np.uint32(0x3F800000)).view(np.float32) - np.float32(1.0)
        out[c0:c0 + len(c)] = np.floor(fl * 4.0).astype(np.int8)
    return out.reshape(NB, S, E)


def _ffn_mask_tables(idx_ot):
    O, T = idx_ot.shape
    m = np.empty((NCH, 128, NL, T), dtype=np.int8)
    for l in range(NL):
        m[:, :, l, :] = (idx_ot == l).reshape(NCH, 128, T)
    return m


_PROG_CACHE = {}


def kernel(L0_attn_hidden_states, L0_attn_w_qkv, L0_attn_w_c,
           L0_ffn_W0, L0_ffn_b0, L0_ffn_A, L0_ffn_B):
    x = np.ascontiguousarray(np.asarray(L0_attn_hidden_states, dtype=np.float32))
    wqkv = np.ascontiguousarray(np.asarray(L0_attn_w_qkv, dtype=np.float32))
    wc = np.ascontiguousarray(np.asarray(L0_attn_w_c, dtype=np.float32))
    W0 = np.ascontiguousarray(np.asarray(L0_ffn_W0, dtype=np.float32))
    b0 = np.ascontiguousarray(np.asarray(L0_ffn_b0, dtype=np.float32))
    A = np.ascontiguousarray(np.asarray(L0_ffn_A, dtype=np.float32))
    Bm = np.ascontiguousarray(np.asarray(L0_ffn_B, dtype=np.float32))

    w0pp = np.ascontiguousarray(
        W0.reshape(NCH, 128, NCH, 128).transpose(0, 3, 2, 1)).reshape(NCH, 128, E)
    wqpp = np.ascontiguousarray(
        wqkv.reshape(NCH, 128, 34, 128).transpose(2, 1, 0, 3)).reshape(34, 128, E)
    wcpp = np.ascontiguousarray(
        wc.reshape(NCH, 128, NCH, 128).transpose(2, 1, 0, 3)).reshape(NCH, 128, E)
    aTp = np.zeros((128, E), np.float32)
    for dc in range(NCH):
        for l in range(NL):
            aTp[:, dc * 128 + 32 * l:dc * 128 + 32 * l + 16] = \
                A[l, :, dc * 128:(dc + 1) * 128].T
    bTp = np.zeros((128, E), np.float32)
    for l in range(NL):
        bTp[32 * l:32 * l + 16, :] = Bm[l].T
    b0v = np.ascontiguousarray(b0.reshape(NCH, 128).T)

    idx_all = _dropout_mask_indices()
    ident = np.zeros((128, 192), dtype=ml_dtypes.bfloat16)
    ident[:, 0:128] = np.eye(128, dtype=ml_dtypes.bfloat16)
    for i in range(64):
        ident[64 + i, 128 + i] = 1

    in_maps = []
    masks_cache = {}
    for c in range(8):
        bi = c // 2
        par = c % 2
        stripes = [2 * t + par for t in range(4)]
        xc = np.ascontiguousarray(x[bi].T)
        if bi not in masks_cache:
            masks_cache[bi] = _ffn_mask_tables(np.ascontiguousarray(idx_all[bi].T))
        mk1c = masks_cache[bi]
        own_cols = np.concatenate(
            [np.arange(s * 512, (s + 1) * 512) for s in stripes])
        mk2c = np.ascontiguousarray(mk1c[:, :, :, own_cols])
        biasT, cmaskT = _host_tables(par)
        p = np.arange(128)
        ixq_ = np.empty((128, 4 * NCH), np.int32)
        ixm_ = np.empty((128, NCH * 4), np.int32)
        for g in range(4):
            for dc in range(NCH):
                ixq_[:, g * NCH + dc] = (dc * 128 + p) * S + stripes[g] * 512
        for oc in range(NCH):
            for tg in range(4):
                ixm_[:, oc * 4 + tg] = (oc * 128 + p) * S + stripes[tg] * 512
        in_maps.append(dict(
            x_fm=xc, w0pp=w0pp, wqpp=wqpp, wcpp=wcpp, aTp=aTp, bTp=bTp,
            b0v=b0v, mk1=mk1c, mk2=mk2c, btbl=biasT, cmt=cmaskT,
            ixq=ixq_, ixm=ixm_, ident=ident))

    if "prog" not in _PROG_CACHE:
        _PROG_CACHE["prog"] = build_program()
    nc = _PROG_CACHE["prog"]
    res = run_bass_kernel_spmd(nc, in_maps, core_ids=list(range(8)))

    outp = np.empty((NB, S, E), np.float32)
    for c in range(8):
        bi = c // 2
        par = c % 2
        o = res.results[c]["out_fm"]
        for g in range(4):
            s = 2 * g + par
            outp[bi, s * 512:(s + 1) * 512, :] = o[:, g * 512:(g + 1) * 512].T
    return outp


# version 08
# speedup vs baseline: 1.2372x; 1.0721x over previous
import contextlib
import math
import numpy as np
import ml_dtypes
import concourse.bass as bass
import concourse.mybir as mybir
from concourse.bass_utils import run_bass_kernel_spmd

F32 = mybir.dt.float32
BF16 = mybir.dt.bfloat16
I32 = mybir.dt.int32
I8 = mybir.dt.int8
AF = mybir.ActivationFunctionType
ALU = mybir.AluOpType

E = 4096
S = 4096
NB = 4
H = 32
DH = 128
NL = 4
NCH = 32
EPS = 1e-5
SCALE = DH ** -0.5
WIN_T = 50.0
BETA = 10.0
KILL = -1.0e5
NKT = 36
KPAD = NKT * 128


def _slopes():
    base = 2.0 ** (-(2.0 ** (-(math.log2(H) - 3))))
    return [base ** (i + 1) for i in range(H)]


SLOPES = _slopes()


def _qg_of(h):
    for qg in (512, 256, 128):
        if SLOPES[h] * (qg - 1) <= 70.0:
            return qg
    return 64


QG = [_qg_of(h) for h in range(H)]


def _nkt(h, t):
    w = WIN_T / SLOPES[h]
    qend = (2 * t + 2) * 512
    return min(int(math.ceil((1024.0 + w) / 128.0)), qend // 128, NKT)


def _needs_cm(qg, sub, kt):
    for par in (0, 1):
        qmb = 1024 - 512 * par - sub * qg
        lo = kt * 128 - KPAD + qmb
        hi = lo + 127
        if hi >= 1 and lo <= qg - 1:
            return True
    return False


def _att_schedule():
    entries = []
    col = 0
    cm_map = {}
    cm_cols = 0
    for t in range(4):
        for h in range(H):
            qg = QG[h]
            nk = _nkt(h, t)
            for sub in range(512 // qg):
                for kt in range(NKT - nk, NKT):
                    r = NKT - 1 - kt
                    cm = None
                    if _needs_cm(qg, sub, kt):
                        key = (qg, sub, r)
                        if key not in cm_map:
                            cm_map[key] = cm_cols
                            cm_cols += qg
                        cm = cm_map[key]
                    entries.append(dict(t=t, h=h, sub=sub, kt=kt, qg=qg,
                                        col=col, cm=cm))
                    col += 1
    return entries, col, cm_map, cm_cols


ATT, NBIAS, CM_MAP, CM_COLS = _att_schedule()
SLOT_COL = []
for _t in range(4):
    _cs = [e["col"] for e in ATT if e["t"] == _t]
    SLOT_COL.append((min(_cs), max(_cs) + 1))
MAXBIAS = max(b_ - a_ for (a_, b_) in SLOT_COL)

ENGINES = ("pe", "act", "dve", "pool", "sp")


class Bld:
    def __init__(self, nc):
        self.nc = nc
        self.ops = {e: [] for e in ENGINES}
        self.sems = {}
        self._waited = {e: {} for e in ENGINES}

    def sem(self, name):
        if name not in self.sems:
            self.sems[name] = [self.nc.alloc_semaphore(name), 0]
        return self.sems[name]

    def op(self, eng, fn, waits=(), sem=None, dma=False):
        wl = []
        for w in waits:
            if w is None:
                continue
            h, v = w
            if v <= 0:
                continue
            k = id(h)
            if self._waited[eng].get(k, -1) >= v:
                continue
            self._waited[eng][k] = v
            wl.append((h, v))
        tok = None
        inc = None
        if sem is not None:
            rec = self.sem(sem)
            rec[1] += 16 if dma else 1
            tok = (rec[0], rec[1])
            inc = (rec[0], 16 if dma else 1)
        self.ops[eng].append((wl, fn, inc))
        return tok

    def fence(self, engs=ENGINES):
        return [self.op(e, None, sem="f_" + e) for e in engs]

    def phase_barrier(self):
        self.fence()
        waits = [(rec[0], rec[1]) for rec in self.sems.values()]
        for e in ENGINES:
            self.op(e, None, waits=waits)

    def emit(self):
        nc = self.nc
        with nc.Block() as blk:
            def mk(ename):
                def body(eng):
                    for (wl, fn, inc) in self.ops[ename]:
                        for (h, v) in wl:
                            eng.wait_ge(h, v)
                        if fn is None:
                            if inc is not None:
                                eng.nop().then_inc(inc[0], inc[1])
                        else:
                            ins = fn()
                            if inc is not None:
                                ins.then_inc(inc[0], inc[1])
                return body
            blk.tensor(mk("pe"))
            blk.scalar(mk("act"))
            blk.vector(mk("dve"))
            blk.gpsimd(mk("pool"))
            blk.sync(mk("sp"))


class Stream:

    def __init__(self, b, name, nslots):
        self.b = b
        self.name = name
        self.n = nslots
        self.war = [[] for _ in range(nslots)]
        self.i = 0

    def slot(self):
        return self.i % self.n

    def issue(self, eng, fn, waits=()):
        s = self.i % self.n
        self.i += 1
        tok = self.b.op(eng, fn, waits=list(waits) + self.war[s],
                        sem=f"{self.name}{s}", dma=True)
        self.war[s] = []
        return tok

    def consumed(self, s, tok):
        if tok is not None:
            self.war[s].append(tok)


def build_program(debug=False):
    nc = bass.Bass("TRN2", target_bir_lowering=False)
    b = Bld(nc)
    dbg = "ExternalOutput" if debug else "Internal"

    x_in = nc.dram_tensor("x_fm", [E, S], F32, kind="ExternalInput").ap()
    w0 = nc.dram_tensor("w0pp", [NCH, 128, E], F32, kind="ExternalInput").ap()
    wqp = nc.dram_tensor("wqpp", [34, 128, E], F32, kind="ExternalInput").ap()
    wcp = nc.dram_tensor("wcpp", [NCH, 128, E], F32, kind="ExternalInput").ap()
    aTp = nc.dram_tensor("aTp", [128, E], F32, kind="ExternalInput").ap()
    bTp = nc.dram_tensor("bTp", [128, E], F32, kind="ExternalInput").ap()
    b0v = nc.dram_tensor("b0v", [128, NCH], F32, kind="ExternalInput").ap()
    mk1 = nc.dram_tensor("mk1", [NCH, 128, NL, S], I8, kind="ExternalInput").ap()
    mk2 = nc.dram_tensor("mk2", [NCH, 128, NL, 2048], I8, kind="ExternalInput").ap()
    btbl = nc.dram_tensor("btbl", [128, NBIAS], F32, kind="ExternalInput").ap()
    cmt = nc.dram_tensor("cmt", [128, max(CM_COLS, 1)], BF16, kind="ExternalInput").ap()
    ixq = nc.dram_tensor("ixq", [128, 4 * NCH], I32, kind="ExternalInput").ap()
    ixm = nc.dram_tensor("ixm", [128, NCH * 4], I32, kind="ExternalInput").ap()
    ident_in = nc.dram_tensor("ident", [128, 192], BF16, kind="ExternalInput").ap()

    xmid = nc.dram_tensor("xmid", [E, S], F32, kind=dbg).ap()
    xln2d = nc.dram_tensor("xln2d", [E, S], BF16, kind=dbg).ap()
    pk = nc.dram_tensor("pk", [128, KPAD + S], BF16, kind=dbg).ap()
    pvx = nc.dram_tensor("pvx", [128, (NKT + 32) * 129], BF16, kind=dbg).ap()
    qd = nc.dram_tensor("qd", [H, 128, 2048], BF16, kind=dbg).ap()
    x2 = nc.dram_tensor("x2", [E, 2048], F32, kind=dbg).ap()
    out = nc.dram_tensor("out_fm", [E, 2048], F32, kind="ExternalOutput").ap()

    st_x = Stream(b, "sx", 4)
    st_w = Stream(b, "sw", 3)
    st_m = Stream(b, "sm", 4)
    st_xe = Stream(b, "sxe", 4)
    st_wr = Stream(b, "swr", 3)
    st_g = Stream(b, "sg", 4)
    st_c = Stream(b, "sc", 2)

    with contextlib.ExitStack() as top:
        ones_b = top.enter_context(nc.sbuf_tensor("ones_b", [128, 1], BF16))
        ones_r = top.enter_context(nc.sbuf_tensor("ones_r", [1, 128], F32))
        ident = top.enter_context(nc.sbuf_tensor("ident_sb", [128, 192], BF16))
        b0h = top.enter_context(nc.sbuf_tensor("b0h", [128, NCH], F32))

        b.op("dve", lambda: nc.vector.memset(ones_b[:], 1.0), sem="dve")
        b.op("dve", lambda: nc.vector.memset(ones_r[:], 1.0), sem="dve")
        tid = st_c.issue("sp", lambda: nc.sync.dma_start(out=ident[:], in_=ident_in))
        tb0 = st_c.issue("sp", lambda: nc.sync.dma_start(out=b0h[:], in_=b0v))
        tb0h = b.op("dve", lambda: nc.vector.tensor_scalar_mul(b0h[:], b0h[:], 0.5),
                    waits=[tb0], sem="dve")

        def ln_pass(nm, src_ap, src_col0, ntg, xln_sb, dram_dst=None):
            assert ntg == 4
            Tw = ntg * 512
            with nc.psum_tensor(nm + "_s1", [128, 512], F32) as ps_s1, \
                 nc.psum_tensor(nm + "_s2", [128, 512], F32) as ps_s2, \
                 nc.psum_tensor(nm + "_b1", [128, 4 * 512], F32) as ps_b1, \
                 nc.psum_tensor(nm + "_b2", [128, 512], F32) as ps_b2, \
                 nc.sbuf_tensor(nm + "_x", [128, 3 * 2048], F32) as lnx, \
                 nc.sbuf_tensor(nm + "_bf", [128, 3 * 2048], BF16) as lnb, \
                 nc.sbuf_tensor(nm + "_sq", [128, 3 * 2048], BF16) as lnq, \
                 nc.sbuf_tensor(nm + "_t", [128, 4 * 512], F32) as lnt, \
                 nc.sbuf_tensor(nm + "_r", [128, 4 * 512], F32) as lnr, \
                 nc.sbuf_tensor(nm + "_o", [128, 4 * 512], BF16) as lno, \
                 nc.sbuf_tensor(nm + "_rw", [1, 5 * 512], F32) as rows:
                st_ln = Stream(b, nm + "sx", 3)
                m_last = [None, None]
                for c in range(NCH):
                    s = st_ln.slot()
                    xt = st_ln.issue("sp", lambda c=c, s=s: nc.sync.dma_start(
                        out=lnx[:, s * 2048:(s + 1) * 2048],
                        in_=src_ap[c * 128:(c + 1) * 128, src_col0:src_col0 + 2048]))
                    ct = b.op("dve", lambda s=s: nc.vector.tensor_copy(
                        lnb[:, s * 2048:(s + 1) * 2048],
                        lnx[:, s * 2048:(s + 1) * 2048]), waits=[xt], sem="dve")
                    qt = b.op("act", lambda s=s: nc.scalar.activation(
                        lnq[:, s * 2048:(s + 1) * 2048],
                        lnx[:, s * 2048:(s + 1) * 2048], AF.Square),
                        waits=[xt], sem="act")
                    for tg in range(4):
                        m1 = b.op("pe", lambda s=s, c=c, tg=tg: nc.tensor.matmul(
                            ps_s1[32 * tg:32 * tg + 1, :], ones_b[:],
                            lnb[:, s * 2048 + tg * 512:s * 2048 + (tg + 1) * 512],
                            start=(c == 0), stop=(c == NCH - 1),
                            tile_position=(0, 32 * tg)),
                            waits=[ct], sem="pe")
                        m2 = b.op("pe", lambda s=s, c=c, tg=tg: nc.tensor.matmul(
                            ps_s2[32 * tg:32 * tg + 1, :], ones_b[:],
                            lnq[:, s * 2048 + tg * 512:s * 2048 + (tg + 1) * 512],
                            start=(c == 0), stop=(c == NCH - 1),
                            tile_position=(0, 32 * tg)),
                            waits=[qt], sem="pe")
                    st_ln.consumed(s, m2)
                    m_last = [m1, m2]
                bc1s = []
                for tg in range(4):
                    rm = rows[0:1, 0:512]
                    rr = rows[0:1, 512:1024]
                    ra = rows[0:1, 1024:1536]
                    rb = rows[0:1, 1536:2048]
                    rc = rows[0:1, 2048:2560]
                    b.op("dve", lambda rm=rm, tg=tg: nc.vector.tensor_scalar_mul(
                        rm, ps_s1[32 * tg:32 * tg + 1, :], 1.0 / E),
                        waits=m_last, sem="dve")
                    b.op("dve", lambda rb=rb, rm=rm: nc.vector.tensor_mul(rb, rm, rm),
                         sem="dve")
                    b.op("dve", lambda ra=ra, tg=tg: nc.vector.tensor_scalar_mul(
                        ra, ps_s2[32 * tg:32 * tg + 1, :], 1.0 / E), sem="dve")
                    b.op("dve", lambda ra=ra, rb=rb: nc.vector.tensor_sub(ra, ra, rb),
                         sem="dve")
                    tv = b.op("dve", lambda ra=ra: nc.vector.tensor_scalar_add(
                        ra, ra, EPS), sem="dve")
                    ts = b.op("act", lambda rb=rb, ra=ra: nc.scalar.activation(
                        rb, ra, AF.Sqrt), waits=[tv], sem="act")
                    b.op("dve", lambda rc=rc, rb=rb: nc.vector.reciprocal(rc, rb),
                         waits=[ts], sem="dve")
                    b.op("dve", lambda rb=rb, rc=rc: nc.vector.tensor_mul(rb, rc, rc),
                         sem="dve")
                    b.op("dve", lambda rb=rb, ra=ra: nc.vector.tensor_mul(rb, rb, ra),
                         sem="dve")
                    b.op("dve", lambda rb=rb: nc.vector.tensor_scalar(
                        rb, rb, -0.5, 1.5, ALU.mult, ALU.add), sem="dve")
                    trs = b.op("dve", lambda rr=rr, rc=rc, rb=rb: nc.vector.tensor_mul(
                        rr, rc, rb), sem="dve")
                    bc1 = b.op("pe", lambda rm=rm, tg=tg: nc.tensor.matmul(
                        ps_b1[:, tg * 512:(tg + 1) * 512], ones_r[:], rm,
                        start=True, stop=True), waits=[trs], sem="pe")
                    bc1s.append(bc1)
                    bc2 = b.op("pe", lambda rr=rr: nc.tensor.matmul(
                        ps_b2[:], ones_r[:], rr, start=True, stop=True), sem="pe")
                    trv = b.op("dve", lambda tg=tg: nc.vector.tensor_copy(
                        lnr[:, tg * 512:(tg + 1) * 512], ps_b2[:]),
                        waits=[bc2], sem="dve")
                lnt_war = [[] for _ in range(4)]
                lno_war = [[] for _ in range(4)]
                for c in range(NCH):
                    s = st_ln.slot()
                    xt = st_ln.issue("sp", lambda c=c, s=s: nc.sync.dma_start(
                        out=lnx[:, s * 2048:(s + 1) * 2048],
                        in_=src_ap[c * 128:(c + 1) * 128, src_col0:src_col0 + 2048]))
                    last = None
                    for tg in range(4):
                        tsb = b.op("dve", lambda s=s, tg=tg: nc.vector.tensor_sub(
                            lnt[:, tg * 512:(tg + 1) * 512],
                            lnx[:, s * 2048 + tg * 512:s * 2048 + (tg + 1) * 512],
                            ps_b1[:, tg * 512:(tg + 1) * 512]),
                            waits=[xt, trv, bc1s[tg]] + lnt_war[tg], sem="dve")
                        lnt_war[tg] = []
                        last = tsb
                        if dram_dst is None:
                            dcol = c * Tw + tg * 512
                            tp = b.op("pool", lambda tg=tg, dcol=dcol: nc.gpsimd.tensor_mul(
                                xln_sb[:, dcol:dcol + 512],
                                lnt[:, tg * 512:(tg + 1) * 512],
                                lnr[:, tg * 512:(tg + 1) * 512]),
                                waits=[tsb], sem="pool")
                            lnt_war[tg].append(tp)
                        else:
                            tp = b.op("pool", lambda tg=tg: nc.gpsimd.tensor_mul(
                                lno[:, tg * 512:(tg + 1) * 512],
                                lnt[:, tg * 512:(tg + 1) * 512],
                                lnr[:, tg * 512:(tg + 1) * 512]),
                                waits=[tsb] + lno_war[tg], sem="pool")
                            lnt_war[tg].append(tp)
                            lno_war[tg] = []
                            wt = st_wr.issue("act", lambda tg=tg, c=c:
                                             nc.scalar.dma_start(
                                out=dram_dst[c * 128:(c + 1) * 128,
                                             src_col0 + tg * 512:src_col0 + tg * 512 + 512],
                                in_=lno[:, tg * 512:(tg + 1) * 512]), waits=[tp])
                            lno_war[tg].append(wt)
                    st_ln.consumed(s, last)
                b.phase_barrier()

        def ffn_pass(nm, xln_sb, ntg, mask_ap, mask_col0,
                     xsrc_ap, xsrc_col0, xdst_ap, xdst_col0):
            Tw = ntg * 512
            with nc.psum_tensor(nm + "_ba", [128, 4 * 512], F32) as ps_ba, \
                 nc.psum_tensor(nm + "_y", [128, 2 * 512], F32) as ps_y, \
                 nc.psum_tensor(nm + "_h", [128, 512], F32) as ps_h, \
                 nc.sbuf_tensor(nm + "_aT", [128, E], BF16) as aTs, \
                 nc.sbuf_tensor(nm + "_bT", [128, E], BF16) as bTs, \
                 nc.sbuf_tensor(nm + "_hs", [128, Tw], BF16) as h_sb, \
                 nc.sbuf_tensor(nm + "_w", [128, 3 * E], BF16) as wbuf, \
                 nc.sbuf_tensor(nm + "_m", [128, 4 * NL * 512], I8) as mbuf, \
                 nc.sbuf_tensor(nm + "_xe", [128, 4 * 512], F32) as xeb, \
                 nc.sbuf_tensor(nm + "_t0", [128, 4 * 512], F32) as t0b, \
                 nc.sbuf_tensor(nm + "_ac", [128, 2 * 512], F32) as acb, \
                 nc.sbuf_tensor(nm + "_xn", [128, 2 * 512], F32) as xnb:

                ta = st_c.issue("pool", lambda: nc.gpsimd.dma_start(
                    out=aTs[:], in_=aTp))
                tb = st_c.issue("pool", lambda: nc.gpsimd.dma_start(
                    out=bTs[:], in_=bTp))
                htoks = [None] * ntg
                hprev = None
                for tg in range(ntg):
                    mm = None
                    for dc in range(NCH):
                        mm = b.op("pe", lambda dc=dc, tg=tg: nc.tensor.matmul(
                            ps_h[:], aTs[:, dc * 128:(dc + 1) * 128],
                            xln_sb[:, dc * Tw + tg * 512:dc * Tw + tg * 512 + 512],
                            start=(dc == 0), stop=(dc == NCH - 1)),
                            waits=([ta, hprev] if dc == 0 else ()), sem="pe")
                    htoks[tg] = b.op("dve", lambda tg=tg: nc.vector.tensor_copy(
                        h_sb[:, tg * 512:(tg + 1) * 512], ps_h[:]),
                        waits=[mm], sem="dve")
                    hprev = htoks[tg]

                ba_war = [[], [], [], []]
                y_war = [[], []]
                t0_war = [[], [], [], []]
                ac_war = [[], []]
                xn_war = [[], []]
                wtoks = {}
                def w0_load(oc):
                    ws = st_w.slot()
                    wtoks[oc] = (ws, st_w.issue("pool", lambda oc=oc, ws=ws:
                                                nc.gpsimd.dma_start(
                        out=wbuf[:, ws * E:(ws + 1) * E], in_=w0[oc])))
                w0_load(0)
                w0_load(1)
                for oc in range(NCH):
                    if oc + 2 < NCH:
                        w0_load(oc + 2)
                    ws, wt = wtoks.pop(oc)
                    lastmm = None
                    xss, mss, xets, mts = [], [], [], []
                    for tg in range(ntg):
                        xs = st_xe.slot()
                        xets.append(st_xe.issue("sp", lambda oc=oc, tg=tg, xs=xs:
                                                nc.sync.dma_start(
                            out=xeb[:, xs * 512:(xs + 1) * 512],
                            in_=xsrc_ap[oc * 128:(oc + 1) * 128,
                                        xsrc_col0 + tg * 512:xsrc_col0 + tg * 512 + 512])))
                        xss.append(xs)
                        ms = st_m.slot()
                        mts.append(st_m.issue("act", lambda oc=oc, tg=tg, ms=ms:
                                              nc.scalar.dma_start(
                            out=mbuf[:, ms * NL * 512:(ms + 1) * NL * 512]
                                .rearrange("p (l t) -> p l t", l=NL),
                            in_=mask_ap[oc, :, :,
                                        mask_col0 + tg * 512:mask_col0 + tg * 512 + 512])))
                        mss.append(ms)
                    mmtg = [None] * ntg
                    for dc in range(NCH):
                        for tg in range(ntg):
                            mmtg[tg] = b.op("pe", lambda dc=dc, tg=tg, ws=ws:
                                            nc.tensor.matmul(
                                ps_ba[:, tg * 512:(tg + 1) * 512],
                                wbuf[:, ws * E + dc * 128:ws * E + (dc + 1) * 128],
                                xln_sb[:, dc * Tw + tg * 512:dc * Tw + tg * 512 + 512],
                                start=(dc == 0), stop=(dc == NCH - 1)),
                                waits=([wt] + ba_war[tg]) if dc == 0 else (), sem="pe")
                            if dc == 0:
                                ba_war[tg] = []
                    tacts = []
                    for tg in range(ntg):
                        u = oc * ntg + tg
                        ab = u % 2
                        tacts.append(b.op("act", lambda oc=oc, tg=tg:
                                          nc.scalar.activation(
                            t0b[:, tg * 512:(tg + 1) * 512],
                            ps_ba[:, tg * 512:(tg + 1) * 512], AF.Identity,
                            bias=b0h[:, oc:oc + 1], scale=0.5),
                            waits=[mmtg[tg], tb0h] + t0_war[tg], sem="act"))
                        t0_war[tg] = []
                    for tg in range(ntg):
                        u = oc * ntg + tg
                        ab = u % 2
                        bb = tg
                        mm = mmtg[tg]
                        xs = xss[tg]
                        ms = mss[tg]
                        xet = xets[tg]
                        mt = mts[tg]
                        tact = tacts[tg]
                        lastmm = mm

                        def ymm(l, yb):
                            return b.op("pe", lambda l=l, oc=oc, tg=tg, yb=yb:
                                        nc.tensor.matmul(
                                ps_y[:, yb * 512:(yb + 1) * 512],
                                bTs[32 * l:32 * l + 16, oc * 128:(oc + 1) * 128],
                                h_sb[32 * l:32 * l + 16, tg * 512:(tg + 1) * 512],
                                start=True, stop=True, tile_position=(32 * l, 0)),
                                waits=[tb, htoks[tg]] + y_war[yb], sem="pe")
                        y0 = ymm(0, 0)
                        y_war[0] = []
                        y1 = ymm(1, 1)
                        y_war[1] = []
                        d0 = b.op("dve", lambda ab=ab: nc.vector.tensor_copy(
                            acb[:, ab * 512:(ab + 1) * 512], ps_y[:, 0:512]),
                            waits=[y0] + ac_war[ab], sem="dve")
                        ac_war[ab] = []
                        y_war[0].append(d0)
                        y2 = ymm(2, 0)
                        y_war[0] = []
                        d1 = b.op("dve", lambda ab=ab, ms=ms: nc.vector.copy_predicated(
                            acb[:, ab * 512:(ab + 1) * 512],
                            mbuf[:, ms * NL * 512 + 512:ms * NL * 512 + 1024],
                            ps_y[:, 512:1024]), waits=[y1, mt], sem="dve")
                        y_war[1].append(d1)
                        y3 = ymm(3, 1)
                        y_war[1] = []
                        d2 = b.op("dve", lambda ab=ab, ms=ms: nc.vector.copy_predicated(
                            acb[:, ab * 512:(ab + 1) * 512],
                            mbuf[:, ms * NL * 512 + 1024:ms * NL * 512 + 1536],
                            ps_y[:, 0:512]), waits=[y2], sem="dve")
                        y_war[0].append(d2)
                        d3 = b.op("dve", lambda ab=ab, ms=ms: nc.vector.copy_predicated(
                            acb[:, ab * 512:(ab + 1) * 512],
                            mbuf[:, ms * NL * 512 + 1536:ms * NL * 512 + 2048],
                            ps_y[:, 512:1024]), waits=[y3], sem="dve")
                        y_war[1].append(d3)
                        st_m.consumed(ms, d3)
                        ba_war[bb].append(tact)
                        tst = b.op("dve", lambda ab=ab, tg=tg: nc.vector.scalar_tensor_tensor(
                            acb[:, ab * 512:(ab + 1) * 512],
                            acb[:, ab * 512:(ab + 1) * 512], 4.0,
                            t0b[:, tg * 512:(tg + 1) * 512], ALU.mult, ALU.add),
                            waits=[tact], sem="dve")
                        t0_war[tg].append(tst)
                        tpn = b.op("pool", lambda ab=ab, xs=xs: nc.gpsimd.tensor_add(
                            xnb[:, ab * 512:(ab + 1) * 512],
                            acb[:, ab * 512:(ab + 1) * 512],
                            xeb[:, xs * 512:(xs + 1) * 512]),
                            waits=[tst, xet] + xn_war[ab], sem="pool")
                        xn_war[ab] = []
                        ac_war[ab].append(tpn)
                        st_xe.consumed(xs, tpn)
                        wtok = st_wr.issue("act", lambda oc=oc, tg=tg, ab=ab:
                                           nc.scalar.dma_start(
                            out=xdst_ap[oc * 128:(oc + 1) * 128,
                                        xdst_col0 + tg * 512:xdst_col0 + tg * 512 + 512],
                            in_=xnb[:, ab * 512:(ab + 1) * 512]), waits=[tpn])
                        xn_war[ab].append(wtok)
                    st_w.consumed(ws, lastmm)
                b.phase_barrier()

        with contextlib.ExitStack() as sa:
            xln = sa.enter_context(nc.sbuf_tensor("xln", [128, NCH * 2048], BF16))
            for hf in range(2):
                ln_pass(f"l1{hf}", x_in, hf * 2048, 4, xln)
                ffn_pass(f"f1{hf}", xln, 4, mk1, hf * 2048,
                         x_in, hf * 2048, xmid, hf * 2048)
            ln_pass("l2a", xmid, 0, 4, None, dram_dst=xln2d)
            ln_pass("l2b", xmid, 2048, 4, None, dram_dst=xln2d)

        with contextlib.ExitStack() as sq:
            kvs = sq.enter_context(nc.sbuf_tensor("kvs", [128, 2 * 512], BF16))
            v_sb = sq.enter_context(nc.sbuf_tensor("v_sb", [128, S], BF16))
            xst = sq.enter_context(nc.sbuf_tensor("xst", [128, 3 * 2048], BF16))
            wqb = sq.enter_context(nc.sbuf_tensor("wqb", [128, 3 * E], BF16))
            zbuf = sq.enter_context(nc.sbuf_tensor("zbuf", [128, 516], BF16))
            tz = b.op("dve", lambda: nc.vector.memset(zbuf[:], 0.0), sem="dve")
            for i in range(9):
                st_wr.issue("act", lambda i=i: nc.scalar.dma_start(
                    out=pk[:, i * 512:(i + 1) * 512], in_=zbuf[:, 0:512]), waits=[tz])
            for i in range(9):
                st_wr.issue("act", lambda i=i: nc.scalar.dma_start(
                    out=pvx[:, i * 516:(i + 1) * 516], in_=zbuf[:, 0:516]), waits=[tz])
            with nc.psum_tensor("ps_kv", [128, 8 * 512], F32) as ps_kv:
                kv_war = [[] for _ in range(8)]
                kvslot_war = [[], []]
                vdone = []
                st_kv = Stream(b, "skv", 3)
                wts = []
                wss = []
                for kvoc in range(2):
                    ws = st_w.slot()
                    wss.append(ws)
                    wts.append(st_w.issue("pool", lambda kvoc=kvoc, ws=ws:
                                          nc.gpsimd.dma_start(
                        out=wqb[:, ws * E:(ws + 1) * E], in_=wqp[32 + kvoc])))
                mmk = [None] * 8
                for hf in range(2):
                    for dc in range(NCH):
                        s = st_kv.slot()
                        xt = st_kv.issue("sp", lambda dc=dc, hf=hf, s=s:
                                         nc.sync.dma_start(
                            out=xst[:, s * 2048:(s + 1) * 2048],
                            in_=xln2d[dc * 128:(dc + 1) * 128,
                                      hf * 2048:(hf + 1) * 2048]))
                        last = None
                        for kvoc in range(2):
                            for tg in range(4):
                                bk = kvoc * 4 + tg
                                last = b.op("pe", lambda dc=dc, kvoc=kvoc, tg=tg,
                                            bk=bk, s=s: nc.tensor.matmul(
                                    ps_kv[:, bk * 512:(bk + 1) * 512],
                                    wqb[:, wss[kvoc] * E + dc * 128:
                                        wss[kvoc] * E + (dc + 1) * 128],
                                    xst[:, s * 2048 + tg * 512:s * 2048 + (tg + 1) * 512],
                                    start=(dc == 0), stop=(dc == NCH - 1)),
                                    waits=([wts[kvoc], xt] + kv_war[bk])
                                          if dc == 0 else [xt], sem="pe")
                                if dc == 0:
                                    kv_war[bk] = []
                                mmk[bk] = last
                        st_kv.consumed(s, last)
                    for kvoc in range(2):
                        for tg in range(4):
                            bk = kvoc * 4 + tg
                            gtg = hf * 4 + tg
                            kb = bk % 2
                            ev = b.op("dve", lambda bk=bk, kb=kb: nc.vector.tensor_copy(
                                kvs[:, kb * 512:(kb + 1) * 512],
                                ps_kv[:, bk * 512:(bk + 1) * 512]),
                                waits=[mmk[bk]] + kv_war[bk] + kvslot_war[kb],
                                sem="dve")
                            kvslot_war[kb] = []
                            kv_war[bk].append(ev)
                            if kvoc == 0:
                                wk = st_wr.issue("act", lambda gtg=gtg, kb=kb:
                                                 nc.scalar.dma_start(
                                    out=pk[:, KPAD + gtg * 512:KPAD + (gtg + 1) * 512],
                                    in_=kvs[:, kb * 512:(kb + 1) * 512]), waits=[ev])
                                kv_war[bk].append(wk)
                                kvslot_war[kb].append(wk)
                            else:
                                vt = b.op("dve", lambda gtg=gtg, kb=kb:
                                          nc.vector.tensor_copy(
                                    v_sb[:, gtg * 512:(gtg + 1) * 512],
                                    kvs[:, kb * 512:(kb + 1) * 512]),
                                    waits=[ev], sem="dve")
                                vdone.append(vt)
                                kv_war[bk].append(vt)
                                kvslot_war[kb].append(vt)
                st_w.consumed(st_w.i % st_w.n, mmk[0])
                st_w.consumed((st_w.i + 1) % st_w.n, mmk[7])
            with nc.psum_tensor("ps_vt", [128, 128], BF16) as ps_vt, \
                 nc.sbuf_tensor("vstg", [128, 2 * 129], BF16) as vstg:
                to1 = b.op("dve", lambda: nc.vector.memset(vstg[:], 1.0),
                           waits=vdone, sem="dve")
                vw = [[], []]
                tprev = None
                for kt in range(32):
                    vs = kt % 2
                    tt = b.op("pe", lambda kt=kt: nc.tensor.transpose(
                        ps_vt[:], v_sb[:, kt * 128:(kt + 1) * 128], ident[:, 0:128]),
                        waits=[to1, tid] + ([tprev] if tprev else []), sem="pe")
                    ev = b.op("dve", lambda vs=vs: nc.vector.tensor_copy(
                        vstg[:, vs * 129:vs * 129 + 128], ps_vt[:]),
                        waits=[tt] + vw[vs], sem="dve")
                    vw[vs] = []
                    tprev = ev
                    wt = st_wr.issue("act", lambda kt=kt, vs=vs: nc.scalar.dma_start(
                        out=pvx[:, (NKT + kt) * 129:(NKT + kt + 1) * 129],
                        in_=vstg[:, vs * 129:(vs + 1) * 129]), waits=[ev])
                    vw[vs].append(wt)
            b.phase_barrier()
            with nc.sbuf_tensor("xlq", [128, 4 * NCH * 512], BF16) as xlq, \
                 nc.sbuf_tensor("ixqs", [128, 4 * NCH], I32) as ixqs, \
                 nc.sbuf_tensor("qstg", [128, 2 * 512], BF16) as qstg, \
                 nc.psum_tensor("ps_q", [128, 4 * 512], F32) as ps_q:
                tix = st_c.issue("sp", lambda: nc.sync.dma_start(out=ixqs[:], in_=ixq))
                gtoks = []
                for g in range(4):
                    for dc in range(NCH):
                        gt = st_g.issue("pool", lambda g=g, dc=dc:
                                        nc.gpsimd.indirect_dma_start(
                            xlq[:, (g * NCH + dc) * 512:(g * NCH + dc + 1) * 512],
                            None, xln2d,
                            bass.IndirectOffsetOnAxis(
                                ixqs[:, g * NCH + dc:g * NCH + dc + 1], 1)),
                            waits=[tix])
                        gtoks.append(gt)
                q_war = [[], [], [], []]
                qs_war = [[], []]
                qwtoks = {}
                def wq_load(oc):
                    ws = st_w.slot()
                    qwtoks[oc] = (ws, st_w.issue("pool", lambda oc=oc, ws=ws:
                                                 nc.gpsimd.dma_start(
                        out=wqb[:, ws * E:(ws + 1) * E], in_=wqp[oc])))
                wq_load(0)
                wq_load(1)
                for oc in range(NCH):
                    if oc + 2 < NCH:
                        wq_load(oc + 2)
                    ws, wt = qwtoks.pop(oc)
                    mmg = [None] * 4
                    for dc in range(NCH):
                        for g in range(4):
                            mmg[g] = b.op("pe", lambda dc=dc, g=g, ws=ws:
                                          nc.tensor.matmul(
                                ps_q[:, g * 512:(g + 1) * 512],
                                wqb[:, ws * E + dc * 128:ws * E + (dc + 1) * 128],
                                xlq[:, (g * NCH + dc) * 512:(g * NCH + dc + 1) * 512],
                                start=(dc == 0), stop=(dc == NCH - 1)),
                                waits=([wt] + q_war[g] + [gtoks[g * NCH + dc]])
                                      if dc == 0 else [gtoks[g * NCH + dc]],
                                sem="pe")
                            if dc == 0:
                                q_war[g] = []
                    for g in range(4):
                        qb = g % 2
                        ev = b.op("dve", lambda qb=qb, g=g: nc.vector.tensor_copy(
                            qstg[:, qb * 512:(qb + 1) * 512],
                            ps_q[:, g * 512:(g + 1) * 512]),
                            waits=[mmg[g]] + qs_war[qb], sem="dve")
                        qs_war[qb] = []
                        q_war[g].append(ev)
                        wqt = st_wr.issue("act", lambda oc=oc, g=g, qb=qb:
                                          nc.scalar.dma_start(
                            out=qd[oc, :, g * 512:(g + 1) * 512],
                            in_=qstg[:, qb * 512:(qb + 1) * 512]), waits=[ev])
                        qs_war[qb].append(wqt)
                    st_w.consumed(ws, mmg[3])
            b.phase_barrier()

        with contextlib.ExitStack() as sbA:
            attn = sbA.enter_context(nc.sbuf_tensor("attn", [128, H * 2048], BF16))
            st_kw = Stream(b, "skw", 1)
            st_vw = Stream(b, "svw", 1)
            st_qw = Stream(b, "sqw", 2)
            st_bi = Stream(b, "sbi", 1)
            with nc.sbuf_tensor("kwin", [128, NKT * 128], BF16) as kwin, \
                 nc.sbuf_tensor("vwin", [128, NKT * 129], BF16) as vwin, \
                 nc.sbuf_tensor("qwin", [128, 2 * 8 * 512], BF16) as qwin, \
                 nc.sbuf_tensor("pbuf", [128, 16 * 512], BF16) as pbuf, \
                 nc.sbuf_tensor("bias", [128, MAXBIAS], F32) as bias, \
                 nc.sbuf_tensor("cms", [128, max(CM_COLS, 1)], BF16) as cms, \
                 nc.sbuf_tensor("atok", [128, 2 * 128], BF16) as atok, \
                 nc.sbuf_tensor("arec", [128, 2], F32) as arec, \
                 nc.psum_tensor("ps_s", [128, 4 * 512], F32) as ps_s, \
                 nc.psum_tensor("ps_pv", [128, 4 * 512], F32) as ps_pv:
                tcm = st_c.issue("sp", lambda: nc.sync.dma_start(out=cms[:], in_=cmt))
                pv_war = [[] for _ in range(4)]
                atreg_war = [[] for _ in range(4)]
                s_war = [[], [], [], []]
                at_war = [[], []]
                rec_war = [[], []]
                pb_war = [[] for _ in range(16)]
                uctr = 0
                for t in range(4):
                    QEND = (2 * t + 2) * 512
                    QT = QEND // 128
                    tkw = st_kw.issue("sp", lambda QEND=QEND: nc.sync.dma_start(
                        out=kwin[:], in_=pk[:, QEND:QEND + KPAD]))
                    tvw = st_vw.issue("sp", lambda QT=QT: nc.sync.dma_start(
                        out=vwin[:], in_=pvx[:, QT * 129:(QT + NKT) * 129]))
                    lo, hi = SLOT_COL[t]
                    tbi = st_bi.issue("sp", lambda lo=lo, hi=hi: nc.sync.dma_start(
                        out=bias[:, 0:hi - lo], in_=btbl[:, lo:hi]))
                    ents = [e for e in ATT if e["t"] == t]
                    tqw = None
                    for h in range(H):
                        if h % 8 == 0:
                            hh = h // 8
                            qs_ = hh % 2
                            tqw = st_qw.issue("sp", lambda t=t, hh=hh, qs_=qs_:
                                              nc.sync.dma_start(
                                out=qwin[:, qs_ * 8 * 512:(qs_ + 1) * 8 * 512]
                                    .rearrange("p (h t) -> p h t", h=8),
                                in_=qd[hh * 8:(hh + 1) * 8, :,
                                       t * 512:(t + 1) * 512]
                                    .rearrange("h d t -> d h t")))
                        qg = QG[h]
                        qoff = ((h // 8) % 2) * 4096 + (h % 8) * 512
                        hents = [e for e in ents if e["h"] == h]
                        units = []
                        for sub in sorted(set(e["sub"] for e in hents)):
                            sents = sorted([e for e in hents if e["sub"] == sub],
                                           key=lambda d: d["kt"])
                            for i, e in enumerate(sents):
                                units.append((e, i == 0, i == len(sents) - 1))
                        pv_list = []
                        touched = [False] * 4

                        def emit_pv(e, first, last, pb, ptok, touched=touched,
                                    pv_list=pv_list, qg=qg):
                            sub, kt = e["sub"], e["kt"]
                            njt = max(1, qg // 128)
                            for j in range(njt):
                                if qg >= 128:
                                    qi = sub * njt + j
                                    prow = 0
                                    rows_ = 128
                                else:
                                    qi = sub // 2
                                    prow = 64 * (sub % 2)
                                    rows_ = 64
                                w_ = [ptok, tvw]
                                if first and not touched[qi]:
                                    w_ = w_ + pv_war[qi] + atreg_war[qi]
                                    touched[qi] = True
                                    pv_war[qi] = []
                                    atreg_war[qi] = []
                                tp = (0, prow) if prow else None
                                pm = b.op("pe", lambda pb=pb, j=j, kt=kt, qi=qi,
                                          prow=prow, rows_=rows_, first=first,
                                          last=last, tp=tp, qg=qg:
                                          nc.tensor.matmul(
                                    ps_pv[prow:prow + rows_,
                                          qi * 512:qi * 512 + 129],
                                    pbuf[:, pb * 512 + j * 128:
                                         pb * 512 + j * 128 + min(128, qg)],
                                    vwin[:, kt * 129:(kt + 1) * 129],
                                    start=first, stop=last, tile_position=tp),
                                    waits=w_, sem="pe")
                                pb_war[pb].append(pm)
                                if last:
                                    pv_list.append((qi, prow, rows_, sub, j, pm))

                        BDEPTH = 12
                        pend = []
                        for b0_ in range(0, len(units), BDEPTH):
                            batch = units[b0_:b0_ + BDEPTH]
                            newpend = []
                            for i, (e, first, last) in enumerate(batch):
                                kt = e["kt"]
                                ss = uctr % 4
                                pb = uctr % 16
                                uctr += 1
                                sm = b.op("pe", lambda kt=kt, qg=qg, ss=ss,
                                          qo=qoff + e["sub"] * qg:
                                          nc.tensor.matmul(
                                    ps_s[:, ss * 512:ss * 512 + qg],
                                    kwin[:, kt * 128:(kt + 1) * 128],
                                    qwin[:, qo:qo + qg],
                                    start=True, stop=True),
                                    waits=[tkw, tqw] + s_war[ss], sem="pe")
                                s_war[ss] = []
                                ex = b.op("act", lambda ss=ss, pb=pb, qg=qg,
                                          cix=e["col"] - lo: nc.scalar.activation(
                                    pbuf[:, pb * 512:pb * 512 + qg],
                                    ps_s[:, ss * 512:ss * 512 + qg], AF.Exp,
                                    bias=bias[:, cix:cix + 1], scale=SCALE),
                                    waits=[sm, tbi] + pb_war[pb], sem="act")
                                pb_war[pb] = []
                                s_war[ss].append(ex)
                                ptok = ex
                                if e["cm"] is not None:
                                    ptok = b.op("dve", lambda pb=pb, qg=qg,
                                                cmo=e["cm"]: nc.vector.tensor_mul(
                                        pbuf[:, pb * 512:pb * 512 + qg],
                                        pbuf[:, pb * 512:pb * 512 + qg],
                                        cms[:, cmo:cmo + qg]),
                                        waits=[ptok, tcm], sem="dve")
                                newpend.append((e, first, last, pb, ptok))
                                if i < len(pend):
                                    emit_pv(*pend[i])
                            for j in range(len(batch), len(pend)):
                                emit_pv(*pend[j])
                            pend = newpend
                        for p_ in pend:
                            emit_pv(*p_)
                        for (qi, prow, rows_, sub, j, pm) in pv_list:
                            lcol = t * 512 + sub * qg + j * 128
                            ab_ = (sub + j) % 2
                            rt = b.op("dve", lambda qi=qi, prow=prow, rows_=rows_,
                                      ab_=ab_: nc.vector.reciprocal(
                                arec[prow:prow + rows_, ab_:ab_ + 1],
                                ps_pv[prow:prow + rows_, qi * 512 + 128:qi * 512 + 129]),
                                waits=[pm] + rec_war[ab_], sem="dve")
                            dt = b.op("dve", lambda qi=qi, prow=prow, rows_=rows_,
                                      ab_=ab_: nc.vector.tensor_scalar(
                                atok[prow:prow + rows_, ab_ * 128:ab_ * 128 + 128],
                                ps_pv[prow:prow + rows_, qi * 512:qi * 512 + 128],
                                arec[prow:prow + rows_, ab_:ab_ + 1], None, ALU.mult),
                                waits=[rt] + at_war[ab_], sem="dve")
                            at_war[ab_] = []
                            rec_war[ab_] = [dt]
                            pv_war[qi].append(dt)
                            if rows_ == 128:
                                idap = ident[0:128, 0:128]
                            elif prow == 0:
                                idap = ident[0:64, 0:64]
                            else:
                                idap = ident[64:128, 128:192]
                            at_ap = ps_pv[:, qi * 512 + 256:qi * 512 + 320].bitcast(BF16)
                            tt = b.op("pe", lambda prow=prow, rows_=rows_, ab_=ab_,
                                      idap=idap, at_ap=at_ap: nc.tensor.transpose(
                                at_ap[0:128, 0:rows_],
                                atok[prow:prow + rows_, ab_ * 128:ab_ * 128 + 128],
                                idap),
                                waits=[dt] + atreg_war[qi], sem="pe")
                            atreg_war[qi] = []
                            ev = b.op("dve", lambda h=h, lcol=lcol, rows_=rows_,
                                      at_ap=at_ap: nc.vector.tensor_copy(
                                attn[:, h * 2048 + lcol:h * 2048 + lcol + rows_],
                                at_ap[0:128, 0:rows_]),
                                waits=[tt], sem="dve")
                            atreg_war[qi].append(ev)
                            pv_war[qi].append(ev)
                            at_war[ab_].append(ev)
                        if h % 8 == 7:
                            st_qw.consumed((h // 8) % 2, b.fence(("pe",))[0])
                    fpe = b.fence(("pe",))[0]
                    st_kw.consumed(0, fpe)
                    st_vw.consumed(0, fpe)
                    st_bi.consumed(0, b.fence(("act",))[0])
                    st_qw.consumed(0, fpe)
                    st_qw.consumed(1, fpe)
                b.phase_barrier()

            with nc.sbuf_tensor("wcb", [128, 3 * E], BF16) as wcb, \
                 nc.sbuf_tensor("xmg", [128, 4 * 512], F32) as xmg, \
                 nc.sbuf_tensor("ixms", [128, NCH * 4], I32) as ixms, \
                 nc.sbuf_tensor("x2s", [128, 2 * 512], F32) as x2s, \
                 nc.psum_tensor("ps_wc", [128, 4 * 512], F32) as ps_wc:
                tix = st_c.issue("sp", lambda: nc.sync.dma_start(out=ixms[:], in_=ixm))
                wc_war = [[], [], [], []]
                x2_war = [[], []]
                cwtoks = {}
                def wc_load(oc):
                    ws = st_w.slot()
                    cwtoks[oc] = (ws, st_w.issue("pool", lambda oc=oc, ws=ws:
                                                 nc.gpsimd.dma_start(
                        out=wcb[:, ws * E:(ws + 1) * E], in_=wcp[oc])))
                wc_load(0)
                wc_load(1)
                for oc in range(NCH):
                    if oc + 2 < NCH:
                        wc_load(oc + 2)
                    ws, wt = cwtoks.pop(oc)
                    mms = [None] * 4
                    for dc in range(NCH):
                        for tg in range(4):
                            mms[tg] = b.op("pe", lambda dc=dc, tg=tg, ws=ws:
                                           nc.tensor.matmul(
                                ps_wc[:, tg * 512:(tg + 1) * 512],
                                wcb[:, ws * E + dc * 128:ws * E + (dc + 1) * 128],
                                attn[:, dc * 2048 + tg * 512:dc * 2048 + (tg + 1) * 512],
                                start=(dc == 0), stop=(dc == NCH - 1)),
                                waits=([wt] + wc_war[tg]) if dc == 0 else (), sem="pe")
                            if dc == 0:
                                wc_war[tg] = []
                    for tg in range(4):
                        gs = st_g.slot()
                        gt = st_g.issue("pool", lambda oc=oc, tg=tg, gs=gs:
                                        nc.gpsimd.indirect_dma_start(
                            xmg[:, gs * 512:(gs + 1) * 512], None, xmid,
                            bass.IndirectOffsetOnAxis(
                                ixms[:, oc * 4 + tg:oc * 4 + tg + 1], 1)),
                            waits=[tix])
                        xb_ = tg % 2
                        ad = b.op("dve", lambda tg=tg, gs=gs, xb_=xb_:
                                  nc.vector.tensor_add(
                            x2s[:, xb_ * 512:(xb_ + 1) * 512],
                            ps_wc[:, tg * 512:(tg + 1) * 512],
                            xmg[:, gs * 512:(gs + 1) * 512]),
                            waits=[mms[tg], gt] + x2_war[xb_], sem="dve")
                        x2_war[xb_] = []
                        wc_war[tg].append(ad)
                        st_g.consumed(gs, ad)
                        wt2 = st_wr.issue("act", lambda oc=oc, tg=tg, xb_=xb_:
                                          nc.scalar.dma_start(
                            out=x2[oc * 128:(oc + 1) * 128, tg * 512:(tg + 1) * 512],
                            in_=x2s[:, xb_ * 512:(xb_ + 1) * 512]), waits=[ad])
                        x2_war[xb_].append(wt2)
                    st_w.consumed(ws, mms[3])
                b.phase_barrier()

        with contextlib.ExitStack() as sc:
            xln3 = sc.enter_context(nc.sbuf_tensor("xln3", [128, NCH * 2048], BF16))
            ln_pass("l3", x2, 0, 4, xln3)
            ffn_pass("f2", xln3, 4, mk2, 0, x2, 0, out, 0)

        final_waits = [(rec[0], rec[1]) for rec in b.sems.values()]
        b.op("sp", None, waits=final_waits)

    b.emit()
    return nc


def _host_tables(par):
    bias = np.zeros((128, NBIAS), np.float32)
    ki = np.arange(128)
    for e in ATT:
        t, h, sub, kt, qg, col = e["t"], e["h"], e["sub"], e["kt"], e["qg"], e["col"]
        qend = (2 * t + 2) * 512
        own = (2 * t + par) * 512
        qbase = own + sub * qg
        kpos = qend - KPAD + kt * 128 + ki
        v = SLOPES[h] * (kpos - qbase) - BETA
        killed = (kpos < 0) | (kpos > qbase + qg - 1)
        bias[:, col] = np.where(killed, KILL, v).astype(np.float32)
        if e["cm"] is None:
            assert np.all(killed | (kpos <= qbase)), ("missing cmask", t, h, sub, kt)
    cmask = np.zeros((128, max(CM_COLS, 1)), np.float32)
    for (qg, sub, r), coff in CM_MAP.items():
        kt = NKT - 1 - r
        qend_m_qbase = 1024 - 512 * par - sub * qg
        kpos_rel = kt * 128 + ki - KPAD + qend_m_qbase
        qj = np.arange(qg)
        cmask[:, coff:coff + qg] = (qj[None, :] >= kpos_rel[:, None])
    return bias, cmask.astype(ml_dtypes.bfloat16)


def _threefry2x32(kx, ky, c0, c1):
    rot = [(13, 15, 26, 6), (17, 29, 16, 24)]
    ks0 = np.uint32(kx); ks1 = np.uint32(ky)
    ks2 = np.uint32(np.uint32(0x1BD11BDA) ^ ks0 ^ ks1)
    x0 = (c0 + ks0).astype(np.uint32); x1 = (c1 + ks1).astype(np.uint32)
    ks = [ks0, ks1, ks2]
    for i in range(5):
        for d in rot[i % 2]:
            x0 = (x0 + x1).astype(np.uint32)
            x1 = ((x1 << np.uint32(d)) | (x1 >> np.uint32(32 - d))).astype(np.uint32)
            x1 = (x0 ^ x1).astype(np.uint32)
        x0 = (x0 + ks[(i + 1) % 3]).astype(np.uint32)
        x1 = (x1 + ks[(i + 2) % 3] + np.uint32(i + 1)).astype(np.uint32)
    return x0, x1


def _dropout_mask_indices():
    n = NB * S * E
    out = np.empty(n, np.int8)
    chunk = 1 << 24
    for c0 in range(0, n, chunk):
        c = np.arange(c0, min(n, c0 + chunk), dtype=np.uint32)
        x0, x1 = _threefry2x32(0, 42, np.zeros_like(c), c)
        bits = x0 ^ x1
        fl = ((bits >> np.uint32(9)) | np.uint32(0x3F800000)).view(np.float32) - np.float32(1.0)
        out[c0:c0 + len(c)] = np.floor(fl * 4.0).astype(np.int8)
    return out.reshape(NB, S, E)


def _ffn_mask_tables(idx_ot):
    O, T = idx_ot.shape
    m = np.empty((NCH, 128, NL, T), dtype=np.int8)
    for l in range(NL):
        m[:, :, l, :] = (idx_ot == l).reshape(NCH, 128, T)
    return m


_PROG_CACHE = {}


def kernel(L0_attn_hidden_states, L0_attn_w_qkv, L0_attn_w_c,
           L0_ffn_W0, L0_ffn_b0, L0_ffn_A, L0_ffn_B):
    x = np.ascontiguousarray(np.asarray(L0_attn_hidden_states, dtype=np.float32))
    wqkv = np.ascontiguousarray(np.asarray(L0_attn_w_qkv, dtype=np.float32))
    wc = np.ascontiguousarray(np.asarray(L0_attn_w_c, dtype=np.float32))
    W0 = np.ascontiguousarray(np.asarray(L0_ffn_W0, dtype=np.float32))
    b0 = np.ascontiguousarray(np.asarray(L0_ffn_b0, dtype=np.float32))
    A = np.ascontiguousarray(np.asarray(L0_ffn_A, dtype=np.float32))
    Bm = np.ascontiguousarray(np.asarray(L0_ffn_B, dtype=np.float32))

    w0pp = np.ascontiguousarray(
        W0.reshape(NCH, 128, NCH, 128).transpose(0, 3, 2, 1)).reshape(NCH, 128, E)
    wqpp = np.ascontiguousarray(
        wqkv.reshape(NCH, 128, 34, 128).transpose(2, 1, 0, 3)).reshape(34, 128, E)
    wcpp = np.ascontiguousarray(
        wc.reshape(NCH, 128, NCH, 128).transpose(2, 1, 0, 3)).reshape(NCH, 128, E)
    aTp = np.zeros((128, E), np.float32)
    for dc in range(NCH):
        for l in range(NL):
            aTp[:, dc * 128 + 32 * l:dc * 128 + 32 * l + 16] = \
                A[l, :, dc * 128:(dc + 1) * 128].T
    bTp = np.zeros((128, E), np.float32)
    for l in range(NL):
        bTp[32 * l:32 * l + 16, :] = Bm[l].T
    b0v = np.ascontiguousarray(b0.reshape(NCH, 128).T)

    idx_all = _dropout_mask_indices()
    ident = np.zeros((128, 192), dtype=ml_dtypes.bfloat16)
    ident[:, 0:128] = np.eye(128, dtype=ml_dtypes.bfloat16)
    for i in range(64):
        ident[64 + i, 128 + i] = 1

    in_maps = []
    masks_cache = {}
    for c in range(8):
        bi = c // 2
        par = c % 2
        stripes = [2 * t + par for t in range(4)]
        xc = np.ascontiguousarray(x[bi].T)
        if bi not in masks_cache:
            masks_cache[bi] = _ffn_mask_tables(np.ascontiguousarray(idx_all[bi].T))
        mk1c = masks_cache[bi]
        own_cols = np.concatenate(
            [np.arange(s * 512, (s + 1) * 512) for s in stripes])
        mk2c = np.ascontiguousarray(mk1c[:, :, :, own_cols])
        biasT, cmaskT = _host_tables(par)
        p = np.arange(128)
        ixq_ = np.empty((128, 4 * NCH), np.int32)
        ixm_ = np.empty((128, NCH * 4), np.int32)
        for g in range(4):
            for dc in range(NCH):
                ixq_[:, g * NCH + dc] = (dc * 128 + p) * S + stripes[g] * 512
        for oc in range(NCH):
            for tg in range(4):
                ixm_[:, oc * 4 + tg] = (oc * 128 + p) * S + stripes[tg] * 512
        in_maps.append(dict(
            x_fm=xc, w0pp=w0pp, wqpp=wqpp, wcpp=wcpp, aTp=aTp, bTp=bTp,
            b0v=b0v, mk1=mk1c, mk2=mk2c, btbl=biasT, cmt=cmaskT,
            ixq=ixq_, ixm=ixm_, ident=ident))

    if "prog" not in _PROG_CACHE:
        _PROG_CACHE["prog"] = build_program()
    nc = _PROG_CACHE["prog"]
    res = run_bass_kernel_spmd(nc, in_maps, core_ids=list(range(8)))

    outp = np.empty((NB, S, E), np.float32)
    for c in range(8):
        bi = c // 2
        par = c % 2
        o = res.results[c]["out_fm"]
        for g in range(4):
            s = 2 * g + par
            outp[bi, s * 512:(s + 1) * 512, :] = o[:, g * 512:(g + 1) * 512].T
    return outp


# version 09
# speedup vs baseline: 1.2850x; 1.0386x over previous
import contextlib
import math
import numpy as np
import ml_dtypes
import concourse.bass as bass
import concourse.mybir as mybir
from concourse.bass_utils import run_bass_kernel_spmd

F32 = mybir.dt.float32
BF16 = mybir.dt.bfloat16
I32 = mybir.dt.int32
I8 = mybir.dt.int8
AF = mybir.ActivationFunctionType
ALU = mybir.AluOpType

E = 4096
S = 4096
NB = 4
H = 32
DH = 128
NL = 4
NCH = 32
EPS = 1e-5
SCALE = DH ** -0.5
WIN_T = 50.0
BETA = 10.0
KILL = -1.0e5
NKT = 36
KPAD = NKT * 128


def _slopes():
    base = 2.0 ** (-(2.0 ** (-(math.log2(H) - 3))))
    return [base ** (i + 1) for i in range(H)]


SLOPES = _slopes()


def _qg_of(h):
    for qg in (512, 256, 128):
        if SLOPES[h] * (qg - 1) <= 70.0:
            return qg
    return 64


QG = [_qg_of(h) for h in range(H)]


def _nkt(h, t):
    w = WIN_T / SLOPES[h]
    qend = (2 * t + 2) * 512
    return min(int(math.ceil((1024.0 + w) / 128.0)), qend // 128, NKT)


def _needs_cm(qg, sub, kt):
    for par in (0, 1):
        qmb = 1024 - 512 * par - sub * qg
        lo = kt * 128 - KPAD + qmb
        hi = lo + 127
        if hi >= 1 and lo <= qg - 1:
            return True
    return False


def _att_schedule():
    entries = []
    col = 0
    cm_map = {}
    cm_cols = 0
    for t in range(4):
        for h in range(H):
            qg = QG[h]
            nk = _nkt(h, t)
            for sub in range(512 // qg):
                for kt in range(NKT - nk, NKT):
                    r = NKT - 1 - kt
                    cm = None
                    if _needs_cm(qg, sub, kt):
                        key = (qg, sub, r)
                        if key not in cm_map:
                            cm_map[key] = cm_cols
                            cm_cols += qg
                        cm = cm_map[key]
                    entries.append(dict(t=t, h=h, sub=sub, kt=kt, qg=qg,
                                        col=col, cm=cm))
                    col += 1
    return entries, col, cm_map, cm_cols


ATT, NBIAS, CM_MAP, CM_COLS = _att_schedule()
SLOT_COL = []
for _t in range(4):
    _cs = [e["col"] for e in ATT if e["t"] == _t]
    SLOT_COL.append((min(_cs), max(_cs) + 1))
MAXBIAS = max(b_ - a_ for (a_, b_) in SLOT_COL)

ENGINES = ("pe", "act", "dve", "pool", "sp")


class Bld:
    def __init__(self, nc):
        self.nc = nc
        self.ops = {e: [] for e in ENGINES}
        self.sems = {}
        self._waited = {e: {} for e in ENGINES}

    def sem(self, name):
        if name not in self.sems:
            self.sems[name] = [self.nc.alloc_semaphore(name), 0]
        return self.sems[name]

    def op(self, eng, fn, waits=(), sem=None, dma=False):
        wl = []
        for w in waits:
            if w is None:
                continue
            h, v = w
            if v <= 0:
                continue
            k = id(h)
            if self._waited[eng].get(k, -1) >= v:
                continue
            self._waited[eng][k] = v
            wl.append((h, v))
        tok = None
        inc = None
        if sem is not None:
            rec = self.sem(sem)
            rec[1] += 16 if dma else 1
            tok = (rec[0], rec[1])
            inc = (rec[0], 16 if dma else 1)
        self.ops[eng].append((wl, fn, inc))
        return tok

    def fence(self, engs=ENGINES):
        return [self.op(e, None, sem="f_" + e) for e in engs]

    def phase_barrier(self):
        self.fence()
        waits = [(rec[0], rec[1]) for rec in self.sems.values()]
        for e in ENGINES:
            self.op(e, None, waits=waits)

    def emit(self):
        nc = self.nc
        with nc.Block() as blk:
            def mk(ename):
                def body(eng):
                    for (wl, fn, inc) in self.ops[ename]:
                        for (h, v) in wl:
                            eng.wait_ge(h, v)
                        if fn is None:
                            if inc is not None:
                                eng.nop().then_inc(inc[0], inc[1])
                        else:
                            ins = fn()
                            if inc is not None:
                                ins.then_inc(inc[0], inc[1])
                return body
            blk.tensor(mk("pe"))
            blk.scalar(mk("act"))
            blk.vector(mk("dve"))
            blk.gpsimd(mk("pool"))
            blk.sync(mk("sp"))


class Stream:

    def __init__(self, b, name, nslots):
        self.b = b
        self.name = name
        self.n = nslots
        self.war = [[] for _ in range(nslots)]
        self.i = 0

    def slot(self):
        return self.i % self.n

    def issue(self, eng, fn, waits=()):
        s = self.i % self.n
        self.i += 1
        tok = self.b.op(eng, fn, waits=list(waits) + self.war[s],
                        sem=f"{self.name}{s}", dma=True)
        self.war[s] = []
        return tok

    def consumed(self, s, tok):
        if tok is not None:
            self.war[s].append(tok)


def build_program(debug=False):
    nc = bass.Bass("TRN2", target_bir_lowering=False)
    b = Bld(nc)
    dbg = "ExternalOutput" if debug else "Internal"

    x_in = nc.dram_tensor("x_fm", [E, S], F32, kind="ExternalInput").ap()
    w0 = nc.dram_tensor("w0pp", [NCH, 128, E], F32, kind="ExternalInput").ap()
    wqp = nc.dram_tensor("wqpp", [34, 128, E], F32, kind="ExternalInput").ap()
    wcp = nc.dram_tensor("wcpp", [NCH, 128, E], F32, kind="ExternalInput").ap()
    aTp = nc.dram_tensor("aTp", [128, E], F32, kind="ExternalInput").ap()
    bTp = nc.dram_tensor("bTp", [128, E], F32, kind="ExternalInput").ap()
    b0v = nc.dram_tensor("b0v", [128, NCH], F32, kind="ExternalInput").ap()
    mk1 = nc.dram_tensor("mk1", [NCH, 128, NL, S], I8, kind="ExternalInput").ap()
    mk2 = nc.dram_tensor("mk2", [NCH, 128, NL, 2048], I8, kind="ExternalInput").ap()
    btbl = nc.dram_tensor("btbl", [128, NBIAS], F32, kind="ExternalInput").ap()
    cmt = nc.dram_tensor("cmt", [128, max(CM_COLS, 1)], BF16, kind="ExternalInput").ap()
    ixq = nc.dram_tensor("ixq", [128, 4 * NCH], I32, kind="ExternalInput").ap()
    ixm = nc.dram_tensor("ixm", [128, NCH * 4], I32, kind="ExternalInput").ap()
    ident_in = nc.dram_tensor("ident", [128, 192], BF16, kind="ExternalInput").ap()

    xmid = nc.dram_tensor("xmid", [E, S], F32, kind=dbg).ap()
    xln2d = nc.dram_tensor("xln2d", [E, S], BF16, kind=dbg).ap()
    pk = nc.dram_tensor("pk", [128, KPAD + S], BF16, kind=dbg).ap()
    pvx = nc.dram_tensor("pvx", [128, (NKT + 32) * 129], BF16, kind=dbg).ap()
    qd = nc.dram_tensor("qd", [H, 128, 2048], BF16, kind=dbg).ap()
    x2 = nc.dram_tensor("x2", [E, 2048], F32, kind=dbg).ap()
    out = nc.dram_tensor("out_fm", [E, 2048], F32, kind="ExternalOutput").ap()

    st_x = Stream(b, "sx", 4)
    st_w = Stream(b, "sw", 3)
    st_m = Stream(b, "sm", 4)
    st_xe = Stream(b, "sxe", 4)
    st_wr = Stream(b, "swr", 3)
    st_g = Stream(b, "sg", 4)
    st_c = Stream(b, "sc", 2)

    with contextlib.ExitStack() as top:
        ones_b = top.enter_context(nc.sbuf_tensor("ones_b", [128, 1], BF16))
        ones_r = top.enter_context(nc.sbuf_tensor("ones_r", [1, 128], F32))
        ident = top.enter_context(nc.sbuf_tensor("ident_sb", [128, 192], BF16))
        b0h = top.enter_context(nc.sbuf_tensor("b0h", [128, NCH], F32))

        b.op("dve", lambda: nc.vector.memset(ones_b[:], 1.0), sem="dve")
        b.op("dve", lambda: nc.vector.memset(ones_r[:], 1.0), sem="dve")
        tid = st_c.issue("sp", lambda: nc.sync.dma_start(out=ident[:], in_=ident_in))
        tb0 = st_c.issue("sp", lambda: nc.sync.dma_start(out=b0h[:], in_=b0v))
        tb0h = b.op("dve", lambda: nc.vector.tensor_scalar_mul(b0h[:], b0h[:], 0.5),
                    waits=[tb0], sem="dve")

        def ln_pass(nm, src_ap, src_col0, ntg, xln_sb, dram_dst=None):
            assert ntg == 4
            Tw = ntg * 512
            with nc.psum_tensor(nm + "_s1", [128, 512], F32) as ps_s1, \
                 nc.psum_tensor(nm + "_s2", [128, 512], F32) as ps_s2, \
                 nc.psum_tensor(nm + "_b1", [128, 4 * 512], F32) as ps_b1, \
                 nc.psum_tensor(nm + "_b2", [128, 512], F32) as ps_b2, \
                 nc.sbuf_tensor(nm + "_x", [128, 3 * 2048], F32) as lnx, \
                 nc.sbuf_tensor(nm + "_bf", [128, 3 * 2048], BF16) as lnb, \
                 nc.sbuf_tensor(nm + "_sq", [128, 3 * 2048], BF16) as lnq, \
                 nc.sbuf_tensor(nm + "_t", [128, 4 * 512], F32) as lnt, \
                 nc.sbuf_tensor(nm + "_r", [128, 4 * 512], F32) as lnr, \
                 nc.sbuf_tensor(nm + "_o", [128, 4 * 512], BF16) as lno, \
                 nc.sbuf_tensor(nm + "_rw", [1, 5 * 512], F32) as rows:
                st_ln = Stream(b, nm + "sx", 3)
                m_last = [None, None]
                for c in range(NCH):
                    s = st_ln.slot()
                    xt = st_ln.issue("sp", lambda c=c, s=s: nc.sync.dma_start(
                        out=lnx[:, s * 2048:(s + 1) * 2048],
                        in_=src_ap[c * 128:(c + 1) * 128, src_col0:src_col0 + 2048]))
                    ct = b.op("dve", lambda s=s: nc.vector.tensor_copy(
                        lnb[:, s * 2048:(s + 1) * 2048],
                        lnx[:, s * 2048:(s + 1) * 2048]), waits=[xt], sem="dve")
                    qt = b.op("act", lambda s=s: nc.scalar.activation(
                        lnq[:, s * 2048:(s + 1) * 2048],
                        lnx[:, s * 2048:(s + 1) * 2048], AF.Square),
                        waits=[xt], sem="act")
                    for tg in range(4):
                        m1 = b.op("pe", lambda s=s, c=c, tg=tg: nc.tensor.matmul(
                            ps_s1[32 * tg:32 * tg + 1, :], ones_b[:],
                            lnb[:, s * 2048 + tg * 512:s * 2048 + (tg + 1) * 512],
                            start=(c == 0), stop=(c == NCH - 1),
                            tile_position=(0, 32 * tg)),
                            waits=[ct], sem="pe")
                        m2 = b.op("pe", lambda s=s, c=c, tg=tg: nc.tensor.matmul(
                            ps_s2[32 * tg:32 * tg + 1, :], ones_b[:],
                            lnq[:, s * 2048 + tg * 512:s * 2048 + (tg + 1) * 512],
                            start=(c == 0), stop=(c == NCH - 1),
                            tile_position=(0, 32 * tg)),
                            waits=[qt], sem="pe")
                    st_ln.consumed(s, m2)
                    m_last = [m1, m2]
                bc1s = []
                for tg in range(4):
                    rm = rows[0:1, 0:512]
                    rr = rows[0:1, 512:1024]
                    ra = rows[0:1, 1024:1536]
                    rb = rows[0:1, 1536:2048]
                    rc = rows[0:1, 2048:2560]
                    b.op("dve", lambda rm=rm, tg=tg: nc.vector.tensor_scalar_mul(
                        rm, ps_s1[32 * tg:32 * tg + 1, :], 1.0 / E),
                        waits=m_last, sem="dve")
                    b.op("dve", lambda rb=rb, rm=rm: nc.vector.tensor_mul(rb, rm, rm),
                         sem="dve")
                    b.op("dve", lambda ra=ra, tg=tg: nc.vector.tensor_scalar_mul(
                        ra, ps_s2[32 * tg:32 * tg + 1, :], 1.0 / E), sem="dve")
                    b.op("dve", lambda ra=ra, rb=rb: nc.vector.tensor_sub(ra, ra, rb),
                         sem="dve")
                    tv = b.op("dve", lambda ra=ra: nc.vector.tensor_scalar_add(
                        ra, ra, EPS), sem="dve")
                    ts = b.op("act", lambda rb=rb, ra=ra: nc.scalar.activation(
                        rb, ra, AF.Sqrt), waits=[tv], sem="act")
                    b.op("dve", lambda rc=rc, rb=rb: nc.vector.reciprocal(rc, rb),
                         waits=[ts], sem="dve")
                    b.op("dve", lambda rb=rb, rc=rc: nc.vector.tensor_mul(rb, rc, rc),
                         sem="dve")
                    b.op("dve", lambda rb=rb, ra=ra: nc.vector.tensor_mul(rb, rb, ra),
                         sem="dve")
                    b.op("dve", lambda rb=rb: nc.vector.tensor_scalar(
                        rb, rb, -0.5, 1.5, ALU.mult, ALU.add), sem="dve")
                    trs = b.op("dve", lambda rr=rr, rc=rc, rb=rb: nc.vector.tensor_mul(
                        rr, rc, rb), sem="dve")
                    bc1 = b.op("pe", lambda rm=rm, tg=tg: nc.tensor.matmul(
                        ps_b1[:, tg * 512:(tg + 1) * 512], ones_r[:], rm,
                        start=True, stop=True), waits=[trs], sem="pe")
                    bc1s.append(bc1)
                    bc2 = b.op("pe", lambda rr=rr: nc.tensor.matmul(
                        ps_b2[:], ones_r[:], rr, start=True, stop=True), sem="pe")
                    trv = b.op("dve", lambda tg=tg: nc.vector.tensor_copy(
                        lnr[:, tg * 512:(tg + 1) * 512], ps_b2[:]),
                        waits=[bc2], sem="dve")
                lnt_war = [[] for _ in range(4)]
                lno_war = [[] for _ in range(4)]
                for c in range(NCH):
                    s = st_ln.slot()
                    xt = st_ln.issue("sp", lambda c=c, s=s: nc.sync.dma_start(
                        out=lnx[:, s * 2048:(s + 1) * 2048],
                        in_=src_ap[c * 128:(c + 1) * 128, src_col0:src_col0 + 2048]))
                    last = None
                    for tg in range(4):
                        tsb = b.op("dve", lambda s=s, tg=tg: nc.vector.tensor_sub(
                            lnt[:, tg * 512:(tg + 1) * 512],
                            lnx[:, s * 2048 + tg * 512:s * 2048 + (tg + 1) * 512],
                            ps_b1[:, tg * 512:(tg + 1) * 512]),
                            waits=[xt, trv, bc1s[tg]] + lnt_war[tg], sem="dve")
                        lnt_war[tg] = []
                        last = tsb
                        if dram_dst is None:
                            dcol = c * Tw + tg * 512
                            tp = b.op("pool", lambda tg=tg, dcol=dcol: nc.gpsimd.tensor_mul(
                                xln_sb[:, dcol:dcol + 512],
                                lnt[:, tg * 512:(tg + 1) * 512],
                                lnr[:, tg * 512:(tg + 1) * 512]),
                                waits=[tsb], sem="pool")
                            lnt_war[tg].append(tp)
                        else:
                            tp = b.op("pool", lambda tg=tg: nc.gpsimd.tensor_mul(
                                lno[:, tg * 512:(tg + 1) * 512],
                                lnt[:, tg * 512:(tg + 1) * 512],
                                lnr[:, tg * 512:(tg + 1) * 512]),
                                waits=[tsb] + lno_war[tg], sem="pool")
                            lnt_war[tg].append(tp)
                            lno_war[tg] = []
                            wt = st_wr.issue("act", lambda tg=tg, c=c:
                                             nc.scalar.dma_start(
                                out=dram_dst[c * 128:(c + 1) * 128,
                                             src_col0 + tg * 512:src_col0 + tg * 512 + 512],
                                in_=lno[:, tg * 512:(tg + 1) * 512]), waits=[tp])
                            lno_war[tg].append(wt)
                    st_ln.consumed(s, last)
                b.phase_barrier()

        def ffn_pass(nm, xln_sb, ntg, mask_ap, mask_col0,
                     xsrc_ap, xsrc_col0, xdst_ap, xdst_col0):
            Tw = ntg * 512
            with nc.psum_tensor(nm + "_ba", [128, 4 * 512], F32) as ps_ba, \
                 nc.psum_tensor(nm + "_y", [128, 3 * 512], F32) as ps_y, \
                 nc.psum_tensor(nm + "_h", [128, 512], F32) as ps_h, \
                 nc.sbuf_tensor(nm + "_aT", [128, E], BF16) as aTs, \
                 nc.sbuf_tensor(nm + "_bT", [128, E], BF16) as bTs, \
                 nc.sbuf_tensor(nm + "_hs", [128, Tw], BF16) as h_sb, \
                 nc.sbuf_tensor(nm + "_w", [128, 3 * E], BF16) as wbuf, \
                 nc.sbuf_tensor(nm + "_m", [128, 4 * NL * 512], I8) as mbuf, \
                 nc.sbuf_tensor(nm + "_xe", [128, 4 * 512], F32) as xeb, \
                 nc.sbuf_tensor(nm + "_t0", [128, 4 * 512], F32) as t0b, \
                 nc.sbuf_tensor(nm + "_ac", [128, 2 * 512], F32) as acb, \
                 nc.sbuf_tensor(nm + "_xn", [128, 2 * 512], F32) as xnb:

                ta = st_c.issue("pool", lambda: nc.gpsimd.dma_start(
                    out=aTs[:], in_=aTp))
                tb = st_c.issue("pool", lambda: nc.gpsimd.dma_start(
                    out=bTs[:], in_=bTp))
                htoks = [None] * ntg
                hprev = None
                for tg in range(ntg):
                    mm = None
                    for dc in range(NCH):
                        mm = b.op("pe", lambda dc=dc, tg=tg: nc.tensor.matmul(
                            ps_h[:], aTs[:, dc * 128:(dc + 1) * 128],
                            xln_sb[:, dc * Tw + tg * 512:dc * Tw + tg * 512 + 512],
                            start=(dc == 0), stop=(dc == NCH - 1)),
                            waits=([ta, hprev] if dc == 0 else ()), sem="pe")
                    htoks[tg] = b.op("dve", lambda tg=tg: nc.vector.tensor_copy(
                        h_sb[:, tg * 512:(tg + 1) * 512], ps_h[:]),
                        waits=[mm], sem="dve")
                    hprev = htoks[tg]

                ba_war = [[], [], [], []]
                y_war = [[], [], []]
                t0_war = [[], [], [], []]
                ac_war = [[], []]
                xn_war = [[], []]
                wtoks = {}
                def w0_load(oc):
                    ws = st_w.slot()
                    wtoks[oc] = (ws, st_w.issue("pool", lambda oc=oc, ws=ws:
                                                nc.gpsimd.dma_start(
                        out=wbuf[:, ws * E:(ws + 1) * E], in_=w0[oc])))
                w0_load(0)
                w0_load(1)
                for oc in range(NCH):
                    if oc + 2 < NCH:
                        w0_load(oc + 2)
                    ws, wt = wtoks.pop(oc)
                    lastmm = None
                    xss, mss, xets, mts = [], [], [], []
                    for tg in range(ntg):
                        xs = st_xe.slot()
                        xets.append(st_xe.issue("sp", lambda oc=oc, tg=tg, xs=xs:
                                                nc.sync.dma_start(
                            out=xeb[:, xs * 512:(xs + 1) * 512],
                            in_=xsrc_ap[oc * 128:(oc + 1) * 128,
                                        xsrc_col0 + tg * 512:xsrc_col0 + tg * 512 + 512])))
                        xss.append(xs)
                        ms = st_m.slot()
                        mts.append(st_m.issue("act", lambda oc=oc, tg=tg, ms=ms:
                                              nc.scalar.dma_start(
                            out=mbuf[:, ms * NL * 512:(ms + 1) * NL * 512]
                                .rearrange("p (l t) -> p l t", l=NL),
                            in_=mask_ap[oc, :, :,
                                        mask_col0 + tg * 512:mask_col0 + tg * 512 + 512])))
                        mss.append(ms)
                    mmtg = [None] * ntg
                    for dc in range(NCH):
                        for tg in range(ntg):
                            mmtg[tg] = b.op("pe", lambda dc=dc, tg=tg, ws=ws:
                                            nc.tensor.matmul(
                                ps_ba[:, tg * 512:(tg + 1) * 512],
                                wbuf[:, ws * E + dc * 128:ws * E + (dc + 1) * 128],
                                xln_sb[:, dc * Tw + tg * 512:dc * Tw + tg * 512 + 512],
                                start=(dc == 0), stop=(dc == NCH - 1)),
                                waits=([wt] + ba_war[tg]) if dc == 0 else (), sem="pe")
                            if dc == 0:
                                ba_war[tg] = []
                    tacts = []
                    for tg in range(ntg):
                        u = oc * ntg + tg
                        ab = u % 2
                        tacts.append(b.op("act", lambda oc=oc, tg=tg:
                                          nc.scalar.activation(
                            t0b[:, tg * 512:(tg + 1) * 512],
                            ps_ba[:, tg * 512:(tg + 1) * 512], AF.Identity,
                            bias=b0h[:, oc:oc + 1], scale=0.5),
                            waits=[mmtg[tg], tb0h] + t0_war[tg], sem="act"))
                        t0_war[tg] = []
                    for tg in range(ntg):
                        u = oc * ntg + tg
                        ab = u % 2
                        bb = tg
                        mm = mmtg[tg]
                        xs = xss[tg]
                        ms = mss[tg]
                        xet = xets[tg]
                        mt = mts[tg]
                        tact = tacts[tg]
                        lastmm = mm

                        def ymm(l, yb, extra=()):
                            return b.op("pe", lambda l=l, oc=oc, tg=tg, yb=yb:
                                        nc.tensor.matmul(
                                ps_y[:, yb * 512:(yb + 1) * 512],
                                bTs[32 * l:32 * l + 16, oc * 128:(oc + 1) * 128],
                                h_sb[32 * l:32 * l + 16, tg * 512:(tg + 1) * 512],
                                start=True, stop=True, tile_position=(32 * l, 0)),
                                waits=[tb, htoks[tg]] + y_war[yb] + list(extra),
                                sem="pe")
                        y0 = ymm(0, 0)
                        y1 = ymm(1, 1)
                        y2 = ymm(2, 2)
                        d0 = b.op("dve", lambda ab=ab: nc.vector.tensor_copy(
                            acb[:, ab * 512:(ab + 1) * 512], ps_y[:, 0:512]),
                            waits=[y0] + ac_war[ab], sem="dve")
                        ac_war[ab] = []
                        d1 = b.op("dve", lambda ab=ab, ms=ms: nc.vector.copy_predicated(
                            acb[:, ab * 512:(ab + 1) * 512],
                            mbuf[:, ms * NL * 512 + 512:ms * NL * 512 + 1024],
                            ps_y[:, 512:1024]), waits=[y1, mt], sem="dve")
                        d2 = b.op("dve", lambda ab=ab, ms=ms: nc.vector.copy_predicated(
                            acb[:, ab * 512:(ab + 1) * 512],
                            mbuf[:, ms * NL * 512 + 1024:ms * NL * 512 + 1536],
                            ps_y[:, 1024:1536]), waits=[y2], sem="dve")
                        y3 = ymm(3, 0, extra=[d0])
                        d3 = b.op("dve", lambda ab=ab, ms=ms: nc.vector.copy_predicated(
                            acb[:, ab * 512:(ab + 1) * 512],
                            mbuf[:, ms * NL * 512 + 1536:ms * NL * 512 + 2048],
                            ps_y[:, 0:512]), waits=[y3], sem="dve")
                        y_war[0] = [d3]
                        y_war[1] = [d1]
                        y_war[2] = [d2]
                        st_m.consumed(ms, d3)
                        ba_war[bb].append(tact)
                        tst = b.op("dve", lambda ab=ab, tg=tg: nc.vector.scalar_tensor_tensor(
                            acb[:, ab * 512:(ab + 1) * 512],
                            acb[:, ab * 512:(ab + 1) * 512], 4.0,
                            t0b[:, tg * 512:(tg + 1) * 512], ALU.mult, ALU.add),
                            waits=[tact], sem="dve")
                        t0_war[tg].append(tst)
                        tpn = b.op("pool", lambda ab=ab, xs=xs: nc.gpsimd.tensor_add(
                            xnb[:, ab * 512:(ab + 1) * 512],
                            acb[:, ab * 512:(ab + 1) * 512],
                            xeb[:, xs * 512:(xs + 1) * 512]),
                            waits=[tst, xet] + xn_war[ab], sem="pool")
                        xn_war[ab] = []
                        ac_war[ab].append(tpn)
                        st_xe.consumed(xs, tpn)
                        wtok = st_wr.issue("act", lambda oc=oc, tg=tg, ab=ab:
                                           nc.scalar.dma_start(
                            out=xdst_ap[oc * 128:(oc + 1) * 128,
                                        xdst_col0 + tg * 512:xdst_col0 + tg * 512 + 512],
                            in_=xnb[:, ab * 512:(ab + 1) * 512]), waits=[tpn])
                        xn_war[ab].append(wtok)
                    st_w.consumed(ws, lastmm)
                b.phase_barrier()

        with contextlib.ExitStack() as sa:
            xln = sa.enter_context(nc.sbuf_tensor("xln", [128, NCH * 2048], BF16))
            for hf in range(2):
                ln_pass(f"l1{hf}", x_in, hf * 2048, 4, xln)
                ffn_pass(f"f1{hf}", xln, 4, mk1, hf * 2048,
                         x_in, hf * 2048, xmid, hf * 2048)
            ln_pass("l2a", xmid, 0, 4, None, dram_dst=xln2d)
            ln_pass("l2b", xmid, 2048, 4, None, dram_dst=xln2d)

        with contextlib.ExitStack() as sq:
            kvs = sq.enter_context(nc.sbuf_tensor("kvs", [128, 2 * 512], BF16))
            v_sb = sq.enter_context(nc.sbuf_tensor("v_sb", [128, S], BF16))
            xst = sq.enter_context(nc.sbuf_tensor("xst", [128, 3 * 2048], BF16))
            wqb = sq.enter_context(nc.sbuf_tensor("wqb", [128, 3 * E], BF16))
            zbuf = sq.enter_context(nc.sbuf_tensor("zbuf", [128, 516], BF16))
            tz = b.op("dve", lambda: nc.vector.memset(zbuf[:], 0.0), sem="dve")
            for i in range(9):
                st_wr.issue("act", lambda i=i: nc.scalar.dma_start(
                    out=pk[:, i * 512:(i + 1) * 512], in_=zbuf[:, 0:512]), waits=[tz])
            for i in range(9):
                st_wr.issue("act", lambda i=i: nc.scalar.dma_start(
                    out=pvx[:, i * 516:(i + 1) * 516], in_=zbuf[:, 0:516]), waits=[tz])
            with nc.psum_tensor("ps_kv", [128, 8 * 512], F32) as ps_kv:
                kv_war = [[] for _ in range(8)]
                kvslot_war = [[], []]
                vdone = []
                st_kv = Stream(b, "skv", 3)
                wts = []
                wss = []
                for kvoc in range(2):
                    ws = st_w.slot()
                    wss.append(ws)
                    wts.append(st_w.issue("pool", lambda kvoc=kvoc, ws=ws:
                                          nc.gpsimd.dma_start(
                        out=wqb[:, ws * E:(ws + 1) * E], in_=wqp[32 + kvoc])))
                mmk = [None] * 8
                for hf in range(2):
                    for dc in range(NCH):
                        s = st_kv.slot()
                        xt = st_kv.issue("sp", lambda dc=dc, hf=hf, s=s:
                                         nc.sync.dma_start(
                            out=xst[:, s * 2048:(s + 1) * 2048],
                            in_=xln2d[dc * 128:(dc + 1) * 128,
                                      hf * 2048:(hf + 1) * 2048]))
                        last = None
                        for kvoc in range(2):
                            for tg in range(4):
                                bk = kvoc * 4 + tg
                                last = b.op("pe", lambda dc=dc, kvoc=kvoc, tg=tg,
                                            bk=bk, s=s: nc.tensor.matmul(
                                    ps_kv[:, bk * 512:(bk + 1) * 512],
                                    wqb[:, wss[kvoc] * E + dc * 128:
                                        wss[kvoc] * E + (dc + 1) * 128],
                                    xst[:, s * 2048 + tg * 512:s * 2048 + (tg + 1) * 512],
                                    start=(dc == 0), stop=(dc == NCH - 1)),
                                    waits=([wts[kvoc], xt] + kv_war[bk])
                                          if dc == 0 else [xt], sem="pe")
                                if dc == 0:
                                    kv_war[bk] = []
                                mmk[bk] = last
                        st_kv.consumed(s, last)
                    for kvoc in range(2):
                        for tg in range(4):
                            bk = kvoc * 4 + tg
                            gtg = hf * 4 + tg
                            kb = bk % 2
                            ev = b.op("dve", lambda bk=bk, kb=kb: nc.vector.tensor_copy(
                                kvs[:, kb * 512:(kb + 1) * 512],
                                ps_kv[:, bk * 512:(bk + 1) * 512]),
                                waits=[mmk[bk]] + kv_war[bk] + kvslot_war[kb],
                                sem="dve")
                            kvslot_war[kb] = []
                            kv_war[bk].append(ev)
                            if kvoc == 0:
                                wk = st_wr.issue("act", lambda gtg=gtg, kb=kb:
                                                 nc.scalar.dma_start(
                                    out=pk[:, KPAD + gtg * 512:KPAD + (gtg + 1) * 512],
                                    in_=kvs[:, kb * 512:(kb + 1) * 512]), waits=[ev])
                                kv_war[bk].append(wk)
                                kvslot_war[kb].append(wk)
                            else:
                                vt = b.op("dve", lambda gtg=gtg, kb=kb:
                                          nc.vector.tensor_copy(
                                    v_sb[:, gtg * 512:(gtg + 1) * 512],
                                    kvs[:, kb * 512:(kb + 1) * 512]),
                                    waits=[ev], sem="dve")
                                vdone.append(vt)
                                kv_war[bk].append(vt)
                                kvslot_war[kb].append(vt)
                st_w.consumed(st_w.i % st_w.n, mmk[0])
                st_w.consumed((st_w.i + 1) % st_w.n, mmk[7])
            with nc.psum_tensor("ps_vt", [128, 128], BF16) as ps_vt, \
                 nc.sbuf_tensor("vstg", [128, 2 * 129], BF16) as vstg:
                to1 = b.op("dve", lambda: nc.vector.memset(vstg[:], 1.0),
                           waits=vdone, sem="dve")
                vw = [[], []]
                tprev = None
                for kt in range(32):
                    vs = kt % 2
                    tt = b.op("pe", lambda kt=kt: nc.tensor.transpose(
                        ps_vt[:], v_sb[:, kt * 128:(kt + 1) * 128], ident[:, 0:128]),
                        waits=[to1, tid] + ([tprev] if tprev else []), sem="pe")
                    ev = b.op("dve", lambda vs=vs: nc.vector.tensor_copy(
                        vstg[:, vs * 129:vs * 129 + 128], ps_vt[:]),
                        waits=[tt] + vw[vs], sem="dve")
                    vw[vs] = []
                    tprev = ev
                    wt = st_wr.issue("act", lambda kt=kt, vs=vs: nc.scalar.dma_start(
                        out=pvx[:, (NKT + kt) * 129:(NKT + kt + 1) * 129],
                        in_=vstg[:, vs * 129:(vs + 1) * 129]), waits=[ev])
                    vw[vs].append(wt)
            b.phase_barrier()
            with nc.sbuf_tensor("xlq", [128, 4 * NCH * 512], BF16) as xlq, \
                 nc.sbuf_tensor("ixqs", [128, 4 * NCH], I32) as ixqs, \
                 nc.sbuf_tensor("qstg", [128, 2 * 512], BF16) as qstg, \
                 nc.psum_tensor("ps_q", [128, 4 * 512], F32) as ps_q:
                tix = st_c.issue("sp", lambda: nc.sync.dma_start(out=ixqs[:], in_=ixq))
                gtoks = []
                for g in range(4):
                    for dc in range(NCH):
                        gt = st_g.issue("pool", lambda g=g, dc=dc:
                                        nc.gpsimd.indirect_dma_start(
                            xlq[:, (g * NCH + dc) * 512:(g * NCH + dc + 1) * 512],
                            None, xln2d,
                            bass.IndirectOffsetOnAxis(
                                ixqs[:, g * NCH + dc:g * NCH + dc + 1], 1)),
                            waits=[tix])
                        gtoks.append(gt)
                q_war = [[], [], [], []]
                qs_war = [[], []]
                qwtoks = {}
                def wq_load(oc):
                    ws = st_w.slot()
                    qwtoks[oc] = (ws, st_w.issue("pool", lambda oc=oc, ws=ws:
                                                 nc.gpsimd.dma_start(
                        out=wqb[:, ws * E:(ws + 1) * E], in_=wqp[oc])))
                wq_load(0)
                wq_load(1)
                for oc in range(NCH):
                    if oc + 2 < NCH:
                        wq_load(oc + 2)
                    ws, wt = qwtoks.pop(oc)
                    mmg = [None] * 4
                    for dc in range(NCH):
                        for g in range(4):
                            mmg[g] = b.op("pe", lambda dc=dc, g=g, ws=ws:
                                          nc.tensor.matmul(
                                ps_q[:, g * 512:(g + 1) * 512],
                                wqb[:, ws * E + dc * 128:ws * E + (dc + 1) * 128],
                                xlq[:, (g * NCH + dc) * 512:(g * NCH + dc + 1) * 512],
                                start=(dc == 0), stop=(dc == NCH - 1)),
                                waits=([wt] + q_war[g] + [gtoks[g * NCH + dc]])
                                      if dc == 0 else [gtoks[g * NCH + dc]],
                                sem="pe")
                            if dc == 0:
                                q_war[g] = []
                    for g in range(4):
                        qb = g % 2
                        ev = b.op("dve", lambda qb=qb, g=g: nc.vector.tensor_copy(
                            qstg[:, qb * 512:(qb + 1) * 512],
                            ps_q[:, g * 512:(g + 1) * 512]),
                            waits=[mmg[g]] + qs_war[qb], sem="dve")
                        qs_war[qb] = []
                        q_war[g].append(ev)
                        wqt = st_wr.issue("act", lambda oc=oc, g=g, qb=qb:
                                          nc.scalar.dma_start(
                            out=qd[oc, :, g * 512:(g + 1) * 512],
                            in_=qstg[:, qb * 512:(qb + 1) * 512]), waits=[ev])
                        qs_war[qb].append(wqt)
                    st_w.consumed(ws, mmg[3])
            b.phase_barrier()

        with contextlib.ExitStack() as sbA:
            attn = sbA.enter_context(nc.sbuf_tensor("attn", [128, H * 2048], BF16))
            st_kw = Stream(b, "skw", 1)
            st_vw = Stream(b, "svw", 1)
            st_qw = Stream(b, "sqw", 2)
            st_bi = Stream(b, "sbi", 1)
            with nc.sbuf_tensor("kwin", [128, NKT * 128], BF16) as kwin, \
                 nc.sbuf_tensor("vwin", [128, NKT * 129], BF16) as vwin, \
                 nc.sbuf_tensor("qwin", [128, 2 * 8 * 512], BF16) as qwin, \
                 nc.sbuf_tensor("pbuf", [128, 16 * 512], BF16) as pbuf, \
                 nc.sbuf_tensor("bias", [128, MAXBIAS], F32) as bias, \
                 nc.sbuf_tensor("cms", [128, max(CM_COLS, 1)], BF16) as cms, \
                 nc.sbuf_tensor("atok", [128, 2 * 128], BF16) as atok, \
                 nc.sbuf_tensor("arec", [128, 2], F32) as arec, \
                 nc.psum_tensor("ps_s", [128, 4 * 512], F32) as ps_s, \
                 nc.psum_tensor("ps_pv", [128, 4 * 512], F32) as ps_pv:
                tcm = st_c.issue("sp", lambda: nc.sync.dma_start(out=cms[:], in_=cmt))
                pv_war = [[] for _ in range(4)]
                atreg_war = [[] for _ in range(4)]
                s_war = [[], [], [], []]
                at_war = [[], []]
                rec_war = [[], []]
                pb_war = [[] for _ in range(16)]
                uctr = 0
                for t in range(4):
                    QEND = (2 * t + 2) * 512
                    QT = QEND // 128
                    tkw = st_kw.issue("sp", lambda QEND=QEND: nc.sync.dma_start(
                        out=kwin[:], in_=pk[:, QEND:QEND + KPAD]))
                    tvw = st_vw.issue("sp", lambda QT=QT: nc.sync.dma_start(
                        out=vwin[:], in_=pvx[:, QT * 129:(QT + NKT) * 129]))
                    lo, hi = SLOT_COL[t]
                    tbi = st_bi.issue("sp", lambda lo=lo, hi=hi: nc.sync.dma_start(
                        out=bias[:, 0:hi - lo], in_=btbl[:, lo:hi]))
                    ents = [e for e in ATT if e["t"] == t]
                    tqw = None
                    for h in range(H):
                        if h % 8 == 0:
                            hh = h // 8
                            qs_ = hh % 2
                            tqw = st_qw.issue("sp", lambda t=t, hh=hh, qs_=qs_:
                                              nc.sync.dma_start(
                                out=qwin[:, qs_ * 8 * 512:(qs_ + 1) * 8 * 512]
                                    .rearrange("p (h t) -> p h t", h=8),
                                in_=qd[hh * 8:(hh + 1) * 8, :,
                                       t * 512:(t + 1) * 512]
                                    .rearrange("h d t -> d h t")))
                        qg = QG[h]
                        qoff = ((h // 8) % 2) * 4096 + (h % 8) * 512
                        hents = [e for e in ents if e["h"] == h]
                        units = []
                        for sub in sorted(set(e["sub"] for e in hents)):
                            sents = sorted([e for e in hents if e["sub"] == sub],
                                           key=lambda d: d["kt"])
                            for i, e in enumerate(sents):
                                units.append((e, i == 0, i == len(sents) - 1))
                        pv_list = []
                        touched = [False] * 4

                        def emit_pv(e, first, last, pb, ptok, touched=touched,
                                    pv_list=pv_list, qg=qg):
                            sub, kt = e["sub"], e["kt"]
                            njt = max(1, qg // 128)
                            for j in range(njt):
                                if qg >= 128:
                                    qi = sub * njt + j
                                    prow = 0
                                    rows_ = 128
                                else:
                                    qi = sub // 2
                                    prow = 64 * (sub % 2)
                                    rows_ = 64
                                w_ = [ptok, tvw]
                                if first and not touched[qi]:
                                    w_ = w_ + pv_war[qi] + atreg_war[qi]
                                    touched[qi] = True
                                    pv_war[qi] = []
                                    atreg_war[qi] = []
                                tp = (0, prow) if prow else None
                                pm = b.op("pe", lambda pb=pb, j=j, kt=kt, qi=qi,
                                          prow=prow, rows_=rows_, first=first,
                                          last=last, tp=tp, qg=qg:
                                          nc.tensor.matmul(
                                    ps_pv[prow:prow + rows_,
                                          qi * 512:qi * 512 + 129],
                                    pbuf[:, pb * 512 + j * 128:
                                         pb * 512 + j * 128 + min(128, qg)],
                                    vwin[:, kt * 129:(kt + 1) * 129],
                                    start=first, stop=last, tile_position=tp),
                                    waits=w_, sem="pe")
                                pb_war[pb].append(pm)
                                if last:
                                    pv_list.append((qi, prow, rows_, sub, j, pm))

                        BDEPTH = 12
                        pend = []
                        for b0_ in range(0, len(units), BDEPTH):
                            batch = units[b0_:b0_ + BDEPTH]
                            newpend = []
                            for i, (e, first, last) in enumerate(batch):
                                kt = e["kt"]
                                ss = uctr % 4
                                pb = uctr % 16
                                uctr += 1
                                sm = b.op("pe", lambda kt=kt, qg=qg, ss=ss,
                                          qo=qoff + e["sub"] * qg:
                                          nc.tensor.matmul(
                                    ps_s[:, ss * 512:ss * 512 + qg],
                                    kwin[:, kt * 128:(kt + 1) * 128],
                                    qwin[:, qo:qo + qg],
                                    start=True, stop=True),
                                    waits=[tkw, tqw] + s_war[ss], sem="pe")
                                s_war[ss] = []
                                ex = b.op("act", lambda ss=ss, pb=pb, qg=qg,
                                          cix=e["col"] - lo: nc.scalar.activation(
                                    pbuf[:, pb * 512:pb * 512 + qg],
                                    ps_s[:, ss * 512:ss * 512 + qg], AF.Exp,
                                    bias=bias[:, cix:cix + 1], scale=SCALE),
                                    waits=[sm, tbi] + pb_war[pb], sem="act")
                                pb_war[pb] = []
                                s_war[ss].append(ex)
                                ptok = ex
                                if e["cm"] is not None:
                                    ptok = b.op("dve", lambda pb=pb, qg=qg,
                                                cmo=e["cm"]: nc.vector.tensor_mul(
                                        pbuf[:, pb * 512:pb * 512 + qg],
                                        pbuf[:, pb * 512:pb * 512 + qg],
                                        cms[:, cmo:cmo + qg]),
                                        waits=[ptok, tcm], sem="dve")
                                newpend.append((e, first, last, pb, ptok))
                                if i < len(pend):
                                    emit_pv(*pend[i])
                            for j in range(len(batch), len(pend)):
                                emit_pv(*pend[j])
                            pend = newpend
                        for p_ in pend:
                            emit_pv(*p_)
                        for (qi, prow, rows_, sub, j, pm) in pv_list:
                            lcol = t * 512 + sub * qg + j * 128
                            ab_ = (sub + j) % 2
                            rt = b.op("dve", lambda qi=qi, prow=prow, rows_=rows_,
                                      ab_=ab_: nc.vector.reciprocal(
                                arec[prow:prow + rows_, ab_:ab_ + 1],
                                ps_pv[prow:prow + rows_, qi * 512 + 128:qi * 512 + 129]),
                                waits=[pm] + rec_war[ab_], sem="dve")
                            dt = b.op("dve", lambda qi=qi, prow=prow, rows_=rows_,
                                      ab_=ab_: nc.vector.tensor_scalar(
                                atok[prow:prow + rows_, ab_ * 128:ab_ * 128 + 128],
                                ps_pv[prow:prow + rows_, qi * 512:qi * 512 + 128],
                                arec[prow:prow + rows_, ab_:ab_ + 1], None, ALU.mult),
                                waits=[rt] + at_war[ab_], sem="dve")
                            at_war[ab_] = []
                            rec_war[ab_] = [dt]
                            pv_war[qi].append(dt)
                            if rows_ == 128:
                                idap = ident[0:128, 0:128]
                            elif prow == 0:
                                idap = ident[0:64, 0:64]
                            else:
                                idap = ident[64:128, 128:192]
                            at_ap = ps_pv[:, qi * 512 + 256:qi * 512 + 320].bitcast(BF16)
                            tt = b.op("pe", lambda prow=prow, rows_=rows_, ab_=ab_,
                                      idap=idap, at_ap=at_ap: nc.tensor.transpose(
                                at_ap[0:128, 0:rows_],
                                atok[prow:prow + rows_, ab_ * 128:ab_ * 128 + 128],
                                idap),
                                waits=[dt] + atreg_war[qi], sem="pe")
                            atreg_war[qi] = []
                            ev = b.op("dve", lambda h=h, lcol=lcol, rows_=rows_,
                                      at_ap=at_ap: nc.vector.tensor_copy(
                                attn[:, h * 2048 + lcol:h * 2048 + lcol + rows_],
                                at_ap[0:128, 0:rows_]),
                                waits=[tt], sem="dve")
                            atreg_war[qi].append(ev)
                            pv_war[qi].append(ev)
                            at_war[ab_].append(ev)
                        if h % 8 == 7:
                            st_qw.consumed((h // 8) % 2, b.fence(("pe",))[0])
                    fpe = b.fence(("pe",))[0]
                    st_kw.consumed(0, fpe)
                    st_vw.consumed(0, fpe)
                    st_bi.consumed(0, b.fence(("act",))[0])
                    st_qw.consumed(0, fpe)
                    st_qw.consumed(1, fpe)
                b.phase_barrier()

            with nc.sbuf_tensor("wcb", [128, 3 * E], BF16) as wcb, \
                 nc.sbuf_tensor("xmg", [128, 4 * 512], F32) as xmg, \
                 nc.sbuf_tensor("ixms", [128, NCH * 4], I32) as ixms, \
                 nc.sbuf_tensor("x2s", [128, 2 * 512], F32) as x2s, \
                 nc.psum_tensor("ps_wc", [128, 4 * 512], F32) as ps_wc:
                tix = st_c.issue("sp", lambda: nc.sync.dma_start(out=ixms[:], in_=ixm))
                wc_war = [[], [], [], []]
                x2_war = [[], []]
                cwtoks = {}
                def wc_load(oc):
                    ws = st_w.slot()
                    cwtoks[oc] = (ws, st_w.issue("pool", lambda oc=oc, ws=ws:
                                                 nc.gpsimd.dma_start(
                        out=wcb[:, ws * E:(ws + 1) * E], in_=wcp[oc])))
                wc_load(0)
                wc_load(1)
                for oc in range(NCH):
                    if oc + 2 < NCH:
                        wc_load(oc + 2)
                    ws, wt = cwtoks.pop(oc)
                    mms = [None] * 4
                    for dc in range(NCH):
                        for tg in range(4):
                            mms[tg] = b.op("pe", lambda dc=dc, tg=tg, ws=ws:
                                           nc.tensor.matmul(
                                ps_wc[:, tg * 512:(tg + 1) * 512],
                                wcb[:, ws * E + dc * 128:ws * E + (dc + 1) * 128],
                                attn[:, dc * 2048 + tg * 512:dc * 2048 + (tg + 1) * 512],
                                start=(dc == 0), stop=(dc == NCH - 1)),
                                waits=([wt] + wc_war[tg]) if dc == 0 else (), sem="pe")
                            if dc == 0:
                                wc_war[tg] = []
                    for tg in range(4):
                        gs = st_g.slot()
                        gt = st_g.issue("pool", lambda oc=oc, tg=tg, gs=gs:
                                        nc.gpsimd.indirect_dma_start(
                            xmg[:, gs * 512:(gs + 1) * 512], None, xmid,
                            bass.IndirectOffsetOnAxis(
                                ixms[:, oc * 4 + tg:oc * 4 + tg + 1], 1)),
                            waits=[tix])
                        xb_ = tg % 2
                        ad = b.op("dve", lambda tg=tg, gs=gs, xb_=xb_:
                                  nc.vector.tensor_add(
                            x2s[:, xb_ * 512:(xb_ + 1) * 512],
                            ps_wc[:, tg * 512:(tg + 1) * 512],
                            xmg[:, gs * 512:(gs + 1) * 512]),
                            waits=[mms[tg], gt] + x2_war[xb_], sem="dve")
                        x2_war[xb_] = []
                        wc_war[tg].append(ad)
                        st_g.consumed(gs, ad)
                        wt2 = st_wr.issue("act", lambda oc=oc, tg=tg, xb_=xb_:
                                          nc.scalar.dma_start(
                            out=x2[oc * 128:(oc + 1) * 128, tg * 512:(tg + 1) * 512],
                            in_=x2s[:, xb_ * 512:(xb_ + 1) * 512]), waits=[ad])
                        x2_war[xb_].append(wt2)
                    st_w.consumed(ws, mms[3])
                b.phase_barrier()

        with contextlib.ExitStack() as sc:
            xln3 = sc.enter_context(nc.sbuf_tensor("xln3", [128, NCH * 2048], BF16))
            ln_pass("l3", x2, 0, 4, xln3)
            ffn_pass("f2", xln3, 4, mk2, 0, x2, 0, out, 0)

        final_waits = [(rec[0], rec[1]) for rec in b.sems.values()]
        b.op("sp", None, waits=final_waits)

    b.emit()
    return nc


def _host_tables(par):
    bias = np.zeros((128, NBIAS), np.float32)
    ki = np.arange(128)
    for e in ATT:
        t, h, sub, kt, qg, col = e["t"], e["h"], e["sub"], e["kt"], e["qg"], e["col"]
        qend = (2 * t + 2) * 512
        own = (2 * t + par) * 512
        qbase = own + sub * qg
        kpos = qend - KPAD + kt * 128 + ki
        v = SLOPES[h] * (kpos - qbase) - BETA
        killed = (kpos < 0) | (kpos > qbase + qg - 1)
        bias[:, col] = np.where(killed, KILL, v).astype(np.float32)
        if e["cm"] is None:
            assert np.all(killed | (kpos <= qbase)), ("missing cmask", t, h, sub, kt)
    cmask = np.zeros((128, max(CM_COLS, 1)), np.float32)
    for (qg, sub, r), coff in CM_MAP.items():
        kt = NKT - 1 - r
        qend_m_qbase = 1024 - 512 * par - sub * qg
        kpos_rel = kt * 128 + ki - KPAD + qend_m_qbase
        qj = np.arange(qg)
        cmask[:, coff:coff + qg] = (qj[None, :] >= kpos_rel[:, None])
    return bias, cmask.astype(ml_dtypes.bfloat16)


def _threefry2x32(kx, ky, c0, c1):
    rot = [(13, 15, 26, 6), (17, 29, 16, 24)]
    ks0 = np.uint32(kx); ks1 = np.uint32(ky)
    ks2 = np.uint32(np.uint32(0x1BD11BDA) ^ ks0 ^ ks1)
    x0 = (c0 + ks0).astype(np.uint32); x1 = (c1 + ks1).astype(np.uint32)
    ks = [ks0, ks1, ks2]
    for i in range(5):
        for d in rot[i % 2]:
            x0 = (x0 + x1).astype(np.uint32)
            x1 = ((x1 << np.uint32(d)) | (x1 >> np.uint32(32 - d))).astype(np.uint32)
            x1 = (x0 ^ x1).astype(np.uint32)
        x0 = (x0 + ks[(i + 1) % 3]).astype(np.uint32)
        x1 = (x1 + ks[(i + 2) % 3] + np.uint32(i + 1)).astype(np.uint32)
    return x0, x1


def _dropout_mask_indices():
    n = NB * S * E
    out = np.empty(n, np.int8)
    chunk = 1 << 24
    for c0 in range(0, n, chunk):
        c = np.arange(c0, min(n, c0 + chunk), dtype=np.uint32)
        x0, x1 = _threefry2x32(0, 42, np.zeros_like(c), c)
        bits = x0 ^ x1
        fl = ((bits >> np.uint32(9)) | np.uint32(0x3F800000)).view(np.float32) - np.float32(1.0)
        out[c0:c0 + len(c)] = np.floor(fl * 4.0).astype(np.int8)
    return out.reshape(NB, S, E)


def _ffn_mask_tables(idx_ot):
    O, T = idx_ot.shape
    m = np.empty((NCH, 128, NL, T), dtype=np.int8)
    for l in range(NL):
        m[:, :, l, :] = (idx_ot == l).reshape(NCH, 128, T)
    return m


_PROG_CACHE = {}


def kernel(L0_attn_hidden_states, L0_attn_w_qkv, L0_attn_w_c,
           L0_ffn_W0, L0_ffn_b0, L0_ffn_A, L0_ffn_B):
    x = np.ascontiguousarray(np.asarray(L0_attn_hidden_states, dtype=np.float32))
    wqkv = np.ascontiguousarray(np.asarray(L0_attn_w_qkv, dtype=np.float32))
    wc = np.ascontiguousarray(np.asarray(L0_attn_w_c, dtype=np.float32))
    W0 = np.ascontiguousarray(np.asarray(L0_ffn_W0, dtype=np.float32))
    b0 = np.ascontiguousarray(np.asarray(L0_ffn_b0, dtype=np.float32))
    A = np.ascontiguousarray(np.asarray(L0_ffn_A, dtype=np.float32))
    Bm = np.ascontiguousarray(np.asarray(L0_ffn_B, dtype=np.float32))

    w0pp = np.ascontiguousarray(
        W0.reshape(NCH, 128, NCH, 128).transpose(0, 3, 2, 1)).reshape(NCH, 128, E)
    wqpp = np.ascontiguousarray(
        wqkv.reshape(NCH, 128, 34, 128).transpose(2, 1, 0, 3)).reshape(34, 128, E)
    wcpp = np.ascontiguousarray(
        wc.reshape(NCH, 128, NCH, 128).transpose(2, 1, 0, 3)).reshape(NCH, 128, E)
    aTp = np.zeros((128, E), np.float32)
    for dc in range(NCH):
        for l in range(NL):
            aTp[:, dc * 128 + 32 * l:dc * 128 + 32 * l + 16] = \
                A[l, :, dc * 128:(dc + 1) * 128].T
    bTp = np.zeros((128, E), np.float32)
    for l in range(NL):
        bTp[32 * l:32 * l + 16, :] = Bm[l].T
    b0v = np.ascontiguousarray(b0.reshape(NCH, 128).T)

    idx_all = _dropout_mask_indices()
    ident = np.zeros((128, 192), dtype=ml_dtypes.bfloat16)
    ident[:, 0:128] = np.eye(128, dtype=ml_dtypes.bfloat16)
    for i in range(64):
        ident[64 + i, 128 + i] = 1

    in_maps = []
    masks_cache = {}
    for c in range(8):
        bi = c // 2
        par = c % 2
        stripes = [2 * t + par for t in range(4)]
        xc = np.ascontiguousarray(x[bi].T)
        if bi not in masks_cache:
            masks_cache[bi] = _ffn_mask_tables(np.ascontiguousarray(idx_all[bi].T))
        mk1c = masks_cache[bi]
        own_cols = np.concatenate(
            [np.arange(s * 512, (s + 1) * 512) for s in stripes])
        mk2c = np.ascontiguousarray(mk1c[:, :, :, own_cols])
        biasT, cmaskT = _host_tables(par)
        p = np.arange(128)
        ixq_ = np.empty((128, 4 * NCH), np.int32)
        ixm_ = np.empty((128, NCH * 4), np.int32)
        for g in range(4):
            for dc in range(NCH):
                ixq_[:, g * NCH + dc] = (dc * 128 + p) * S + stripes[g] * 512
        for oc in range(NCH):
            for tg in range(4):
                ixm_[:, oc * 4 + tg] = (oc * 128 + p) * S + stripes[tg] * 512
        in_maps.append(dict(
            x_fm=xc, w0pp=w0pp, wqpp=wqpp, wcpp=wcpp, aTp=aTp, bTp=bTp,
            b0v=b0v, mk1=mk1c, mk2=mk2c, btbl=biasT, cmt=cmaskT,
            ixq=ixq_, ixm=ixm_, ident=ident))

    if "prog" not in _PROG_CACHE:
        _PROG_CACHE["prog"] = build_program()
    nc = _PROG_CACHE["prog"]
    res = run_bass_kernel_spmd(nc, in_maps, core_ids=list(range(8)))

    outp = np.empty((NB, S, E), np.float32)
    for c in range(8):
        bi = c // 2
        par = c % 2
        o = res.results[c]["out_fm"]
        for g in range(4):
            s = 2 * g + par
            outp[bi, s * 512:(s + 1) * 512, :] = o[:, g * 512:(g + 1) * 512].T
    return outp
